# version 30
# baseline (speedup 1.0000x reference)
import numpy as np
import concourse.bass as bass
import concourse.mybir as mybir
from concourse.bass_utils import run_bass_kernel_spmd
from contextlib import ExitStack

F32 = mybir.dt.float32
BF16 = mybir.dt.bfloat16
I32 = mybir.dt.int32
ALU = mybir.AluOpType
AF = mybir.ActivationFunctionType
AXX = mybir.AxisListType.X

NCORES = 8
TC = 2048
NT = 16
D = 1024
H = 16
HF = 32
NP = 8
SW = 65 * NP

ENGS = ("sync", "gpsimd", "vector", "scalar", "tensor")


def build_nc():
    nc = bass.Bass("TRN2", target_bir_lowering=False, debug=False)

    x_d = nc.dram_tensor("x", [TC, D], F32, kind="ExternalInput").ap()
    xT_d = nc.dram_tensor("xT", [D, TC], F32, kind="ExternalInput").ap()
    wq_d = nc.dram_tensor("wqT", [D, D], F32, kind="ExternalInput").ap()
    wk_d = nc.dram_tensor("wkT", [D, D], F32, kind="ExternalInput").ap()
    wv_d = nc.dram_tensor("wvT", [D, D], F32, kind="ExternalInput").ap()
    wo_d = nc.dram_tensor("woT", [D, D], F32, kind="ExternalInput").ap()
    rhsT_d = nc.dram_tensor("rhsT", [D, D], F32, kind="ExternalInput").ap()
    cs_d = nc.dram_tensor("cs", [TC, 3 * HF], F32, kind="ExternalInput").ap()
    tri_d = nc.dram_tensor("tri", [128, 128], F32, kind="ExternalInput").ap()
    pw_d = nc.dram_tensor("pw", [128, NCORES], F32, kind="ExternalInput").ap()
    cst_d = nc.dram_tensor("cstd", [128, 8], F32, kind="ExternalInput").ap()
    ones_d = nc.dram_tensor("ones1", [1, 128], F32, kind="ExternalInput").ap()
    out_d = nc.dram_tensor("out", [TC, D], F32, kind="ExternalOutput").ap()
    cc_in = nc.dram_tensor("cc_in", [128, SW], F32)
    cc_out = nc.dram_tensor("cc_out", [NCORES * 128, SW], F32)
    rs_sc = nc.dram_tensor("rs_sc", [128, 8], F32)
    v65_d = nc.dram_tensor("v65sp", [128, NT * 65 * H], BF16)

    streams = {e: [] for e in ENGS}
    cnt = {}
    es = ExitStack()
    sem = {}
    ENG_SEM = {"tensor": "pe", "vector": "vec", "scalar": "scl",
               "gpsimd": "pl", "sync": "syq"}

    def S(name):
        if name not in sem:
            sem[name] = es.enter_context(nc.semaphore("s_" + name))
            cnt[name] = 0
        return sem[name]

    def op(eng, fn, sm=None, inc=0):
        if sm is not None:
            S(sm)
            cnt[sm] += inc
            streams[eng].append(("op", fn, sm, inc))
            return (sm, cnt[sm])
        streams[eng].append(("op", fn, None, 0))
        return None

    def eop(eng, fn):
        own = ENG_SEM[eng]
        if eng != "tensor" and cnt.get(own, 0) > 0:
            streams[eng].append(("wait", own, cnt[own]))
        return op(eng, fn, own, 1)

    def wt(eng, spec):
        if spec is None:
            import sys
            print("WT-NONE-SKIP on", eng, file=sys.stderr)
            return
        sm, val = spec
        if val > 0:
            streams[eng].append(("wait", sm, val))

    def marks_now():
        return dict(cnt)

    def barrier(marks):
        for e in ENGS:
            own = ENG_SEM[e]
            for sm, val in marks.items():
                if sm != own and sm not in ("cc", "ccin") and val > 0:
                    streams[e].append(("wait", sm, val))

    Sg = es.enter_context(nc.sbuf_tensor("Sg", [128, SW], F32))
    tri_b = es.enter_context(nc.sbuf_tensor("tri_b", [128, 128], BF16))
    pw_t = es.enter_context(nc.sbuf_tensor("pw_t", [128, NCORES], F32))
    cst = es.enter_context(nc.sbuf_tensor("cst", [128, 8], F32))
    ones_f = es.enter_context(nc.sbuf_tensor("ones_f", [1, 128], F32))
    zb1 = es.enter_context(nc.sbuf_tensor("zb1", [1, 128], BF16))
    esAB = ExitStack()
    phk_fm = esAB.enter_context(nc.sbuf_tensor("phk_fm", [128, 8, TC], BF16))
    v65 = esAB.enter_context(nc.sbuf_tensor("v65", [128, 3, 65 * H], BF16))
    D_bf = esAB.enter_context(nc.sbuf_tensor("D_bf", [128, NT, SW], BF16))
    cs_t = esAB.enter_context(nc.sbuf_tensor("cs_t", [128, NT, 3 * HF], F32))
    wqb = esAB.enter_context(nc.sbuf_tensor("wqb", [128, 8, D], BF16))
    xtB = esAB.enter_context(nc.sbuf_tensor("xtB", [128, 2, 8, 128], BF16))

    C_INV1024, C_LNEPS, C_NHALF, C_NEG1, C_NINV1024 = 0, 1, 2, 3, 4

    def cstc(i):
        return cst[:, i:i + 1]

    TRI = op("gpsimd", lambda h: h.dma_start(out=tri_b[:], in_=tri_d),
             "tri", 16)
    op("sync", lambda h: h.dma_start(
        out=cs_t[:], in_=cs_d.rearrange("(n p) i -> p n i", p=128)),
        "const", 16)
    op("sync", lambda h: h.dma_start(out=pw_t[:], in_=pw_d), "const", 16)
    op("sync", lambda h: h.dma_start(out=cst[:], in_=cst_d), "const", 16)
    CONST = op("sync", lambda h: h.dma_start(out=ones_f[:], in_=ones_d),
               "const", 16)

    eop("vector", lambda h: h.memset(zb1[:], 0.0))
    eop("vector", lambda h: h.memset(zb5[:], 0.0))
    eop("vector", lambda h: h.memset(Sg[:], 0.0))
    V_INIT = eop("vector", lambda h: h.memset(
        v65[:].rearrange("p n (g w) -> p n g w", w=65)[:, :, :, 64:65], 1.0))

    esA = ExitStack()
    wkv = esA.enter_context(nc.sbuf_tensor("wkv", [128, 16, D], BF16))
    xtA = esA.enter_context(nc.sbuf_tensor("xtA", [128, 2, 8, 128], BF16))
    ktm = esA.enter_context(nc.sbuf_tensor("ktm", [128, 2, D], BF16))
    mA = esA.enter_context(nc.sbuf_tensor("mA", [128, 4, 512], F32))
    rk = esA.enter_context(nc.sbuf_tensor("rk", [128, 2, D], F32))
    mnk = esA.enter_context(nc.sbuf_tensor("mnk", [128, 1, D], F32))
    ek = esA.enter_context(nc.sbuf_tensor("ek", [128, 2, D], BF16))
    Stot = esA.enter_context(nc.sbuf_tensor("Stot", [128, SW], F32))
    zb5 = esA.enter_context(nc.sbuf_tensor("zb5", [1, 512], BF16))
    pjk = esA.enter_context(nc.psum_tensor("pjk", [128, D], F32))
    pjv = esA.enter_context(nc.psum_tensor("pjv", [128, D], F32))
    dps = [esA.enter_context(nc.psum_tensor(f"dps{i}", [128, 512], F32))
           for i in range(2)]

    WK = op("gpsimd", lambda h: h.dma_start(
        out=wkv[:, 0:8, :], in_=wk_d.rearrange("(c p) o -> p c o", p=128)),
        "wkv", 16)
    WKV = op("gpsimd", lambda h: h.dma_start(
        out=wkv[:, 8:16, :], in_=wv_d.rearrange("(c p) o -> p c o", p=128)),
        "wkv", 16)

    wt("tensor", V_INIT)
    for i in range(2):
        eop("tensor", lambda h, i=i: h.matmul(
            dps[i][:, 0:260], zb1[:], zb5[:, 0:260],
            start=True, stop=False, skip_group_check=True))

    PJ, VE, RMK, PHIK, DMM, DEV, TQK, XTA, EXK = \
        [None] * NT, [None] * NT, [None] * NT, [None] * NT, [None] * NT, \
        [None] * NT, [None] * NT, [None] * NT, [None] * NT
    MINKL = [None] * NT
    RKPL = [None] * NT
    PJKL = [None] * NT

    def a_xload(n):
        if not (0 <= n < NT):
            return
        if n >= 2:
            wt("gpsimd", PJ[n - 2])
        XTA[n] = op("gpsimd", lambda h, n=n, sl=n % 2: h.dma_start(
            out=xtA[:, sl, :, :],
            in_=xT_d[:, n * 128:(n + 1) * 128].rearrange(
                "(c p) t -> p c t", p=128)), "xta%d" % (n % 2), 16)

    def a_kvproj(n):
        if not (0 <= n < NT):
            return
        wt("tensor", XTA[n])
        if n == 0:
            wt("tensor", WK)
        if n >= 1:
            wt("tensor", RMK[n - 1])
        last = None
        for cch in range(8):
            for hf in range(2):
                last = eop("tensor", lambda h, sl=n % 2, hf=hf, c=cch:
                           h.matmul(
                    pjk[:, hf * 512:(hf + 1) * 512],
                    xtA[:, sl, c, :], wkv[:, c, hf * 512:(hf + 1) * 512],
                    start=(c == 0), stop=(c == 7),
                    skip_group_check=True))
        PJKL[n] = last
        if n == 0:
            wt("tensor", WKV)
        if n >= 1:
            wt("tensor", VE[n - 1])
        for cch in range(8):
            for hf in range(2):
                last = eop("tensor", lambda h, sl=n % 2, hf=hf, c=cch:
                           h.matmul(
                    pjv[:, hf * 512:(hf + 1) * 512],
                    xtA[:, sl, c, :], wkv[:, 8 + c, hf * 512:(hf + 1) * 512],
                    start=(c == 0), stop=(c == 7),
                    skip_group_check=True))
        PJ[n] = last

    def a_D(n):
        if not (0 <= n < NT):
            return
        wt("tensor", PHIK[n])
        wt("tensor", VE[n])
        if n >= 1:
            wt("tensor", DEV[n - 1])
        last = None
        for p in range(NP):
            bank, cb = p // 4, (p % 4) * 65
            for par in range(2):
                hh = 2 * p + par
                last = eop("tensor", lambda h, sl=n % 2, par=par, hh=hh,
                           bank=bank, cb=cb, n=n: h.matmul(
                    dps[bank][par * 64:(par + 1) * 64, cb:cb + 65],
                    ktm[:, sl, hh * 64:(hh + 1) * 64],
                    v65[:, n % 3, hh * 65:(hh + 1) * 65],
                    start=False,
                    stop=(n == NT - 1 and hh == 2 * NP - 1),
                    tile_position=(0, par * 64), skip_group_check=True))
        DMM[n] = last

    def a_m(n):
        if not (0 <= n < NT):
            return
        wt("vector", PJKL[n])
        if n == 0:
            wt("vector", CONST)
        if n >= 1:
            wt("vector", RKPL[n - 1])
        xx = pjk[:].rearrange("p (g t w) -> p t g w", t=2, w=32)
        last = None
        for half in range(2):
            tab = cs_t[:, n, half * 32:half * 32 + 64].rearrange(
                "p (t o i) -> p t o i", t=2, o=1).broadcast_to(
                [128, 2, 16, 32])
            last = eop("vector", lambda h, half=half, tab=tab:
                       h.tensor_tensor(
                mA[:, 2 * half:2 * half + 2, :].rearrange(
                    "p t (g i) -> p t g i", i=32),
                xx, tab, ALU.mult))
        RMK[n] = last

    def a_r(n):
        if not (0 <= n < NT):
            return
        wt("gpsimd", RMK[n])
        if n >= 2:
            wt("gpsimd", PHIK[n - 2])
        eop("gpsimd", lambda h, sl=n % 2: h.tensor_tensor(
            rk[:, sl, :].rearrange("p (g w) -> p g w", w=64)[:, :, 0:32],
            mA[:, 0, :].rearrange("p (g i) -> p g i", i=32),
            mA[:, 1, :].rearrange("p (g i) -> p g i", i=32), ALU.subtract))
        RKPL[n] = eop("gpsimd", lambda h, sl=n % 2: h.tensor_tensor(
            rk[:, sl, :].rearrange("p (g w) -> p g w", w=64)[:, :, 32:64],
            mA[:, 2, :].rearrange("p (g i) -> p g i", i=32),
            mA[:, 3, :].rearrange("p (g i) -> p g i", i=32), ALU.add))

    def a_exp(n):
        if not (0 <= n < NT):
            return
        wt("scalar", RKPL[n])
        if n == 0:
            wt("scalar", CONST)
        eop("scalar", lambda h, sl=n % 2: h.activation(
            mnk[:, 0, :], rk[:, sl, :], AF.Relu, scale=cstc(C_NEG1)))
        if n >= 2:
            wt("scalar", PHIK[n - 2])
        EXK[n] = eop("scalar", lambda h, sl=n % 2: h.activation(
            ek[:, sl, :], mnk[:, 0, :], AF.Exp, scale=cstc(C_NEG1)))

    def a_stt(n):
        if not (0 <= n < NT):
            return
        wt("vector", EXK[n])
        if n >= 2:
            wt("vector", TQK[n - 2])
        if n >= 2:
            wt("vector", DMM[n - 2])
        PHIK[n] = eop("vector", lambda h, sl=n % 2: h.scalar_tensor_tensor(
            ktm[:, sl, :], rk[:, sl, :], 0.0, ek[:, sl, :],
            ALU.max, ALU.add))

    SPV = [None] * NT

    def a_vevac(n):
        if not (0 <= n < NT):
            return
        wt("scalar", PJ[n])
        if n == 0:
            wt("scalar", V_INIT)
        if n >= 3:
            wt("scalar", SPV[n - 3])
            wt("scalar", DMM[n - 3])
        VE[n] = eop("scalar", lambda h, n=n, sl=n % 3: h.activation(
            v65[:, sl, :].rearrange("p (g w) -> p g w", w=65)[:, :, 0:64],
            pjv[:].rearrange("p (g w) -> p g w", w=64), AF.Copy))
        wt("scalar", VE[n])
        SPV[n] = op("scalar", lambda h, n=n, sl=n % 3: h.dma_start(
            out=v65_d.ap()[:, n * 1040:(n + 1) * 1040], in_=v65[:, sl, :]),
            "spv%d" % (n % 2), 16)

    def a_tq(n):
        if not (0 <= n < NT):
            return
        wt("sync", PHIK[n])
        TQK[n] = op("sync", lambda h, n=n, sl=n % 2: h.dma_start_transpose(
            phk_fm[:, :, n * 128:(n + 1) * 128], ktm[:, sl, :]),
            "tqk%d" % (n % 2), 16)

    def a_devac(n):
        if not (0 <= n < NT):
            return
        wt("vector", DMM[n])
        eop("vector", lambda h, n=n: h.tensor_copy(
            D_bf[:, n, 0:260], dps[0][:, 0:260]))
        DEV[n] = eop("vector", lambda h, n=n: h.tensor_copy(
            D_bf[:, n, 260:520], dps[1][:, 0:260]))

    XTB = [None] * NT
    WQ = None

    def b_xload(c):
        if not (0 <= c < NT):
            return
        if XTB[c] is not None:
            return
        if c >= 2:
            wt("gpsimd", PJQL[c - 2])
        XTB[c] = op("gpsimd", lambda h, c=c, sl=c % 2: h.dma_start(
            out=xtB[:, sl, :, :],
            in_=xT_d[:, c * 128:(c + 1) * 128].rearrange(
                "(c p) t -> p c t", p=128)), "xtb%d" % (c % 2), 16)

    a_xload(0)
    for j in range(NT + 2):
        a_xload(j + 1)
        a_kvproj(j)
        a_m(j)
        a_stt(j - 2)
        a_tq(j - 2)
        a_D(j - 2)
        a_devac(j - 2)
        a_r(j)
        a_vevac(j)
        a_exp(j - 1)
        if 8 <= j <= 15:
            cch = j - 8
            WQ = op("gpsimd", lambda h, c=cch: h.dma_start(
                out=wqb[:, c, :],
                in_=wq_d[c * 128:(c + 1) * 128, :]), "wq", 16)
        if j == 14:
            b_xload(0)
        if j == 15:
            b_xload(1)

    eop("vector", lambda h: h.tensor_copy(Stot[:, 0:260], dps[0][:, 0:260]))
    V_STOT = eop("vector", lambda h: h.tensor_copy(
        Stot[:, 260:520], dps[1][:, 0:260]))
    wt("gpsimd", V_STOT)
    CCW = op("gpsimd", lambda h: h.dma_start(out=cc_in.ap(), in_=Stot[:]),
             "ccin", 16)
    wt("gpsimd", CCW)
    CC = op("gpsimd", lambda h: h.collective_compute(
        "AllGather", ALU.bypass, replica_groups=[list(range(NCORES))],
        ins=[cc_in.ap().opt()], outs=[cc_out.ap().opt()]), "cc", 1)

    endA = marks_now()
    esA.close()
    esBC = ExitStack()
    YT = esBC.enter_context(nc.sbuf_tensor("YT", [128, 8, TC], BF16, side="right"))
    wo_b = esBC.enter_context(nc.sbuf_tensor("wo_b", [128, 8, D], BF16, side="right"))

    esB = ExitStack()
    Am = esB.enter_context(nc.sbuf_tensor("Am", [128, 5, 2048], BF16))
    qfm = esB.enter_context(nc.sbuf_tensor("qfm", [128, 5, 8, 128], BF16))
    Ytm = esB.enter_context(nc.sbuf_tensor("Ytm", [128, 2, D], BF16))
    S65 = esB.enter_context(nc.sbuf_tensor("S65", [128, 2, SW], BF16))
    denf = esB.enter_context(nc.sbuf_tensor("denf", [128, H], F32))
    rden = esB.enter_context(nc.sbuf_tensor("rden", [128, H], F32))
    denm = esB.enter_context(nc.sbuf_tensor("denm", [128, H], F32))
    gsl4 = esB.enter_context(nc.sbuf_tensor("gsl4", [128, 5, SW], F32))
    mB = esB.enter_context(nc.sbuf_tensor("mB", [128, 4, 512], F32))
    pjs = esB.enter_context(nc.sbuf_tensor("pjs", [128, 2, D], BF16))
    rq = esB.enter_context(nc.sbuf_tensor("rq", [128, 2, D], F32))
    mnq = esB.enter_context(nc.sbuf_tensor("mnq", [128, 1, D], F32))
    eq = esB.enter_context(nc.sbuf_tensor("eq", [128, 1, D], BF16))
    qtm = esB.enter_context(nc.sbuf_tensor("qtm", [128, 1, D], BF16))
    pjq = esB.enter_context(nc.psum_tensor("pjq", [128, D], F32))
    aps = [esB.enter_context(nc.psum_tensor(f"aps{i}", [128, 512], F32))
           for i in range(2)]
    yps = [esB.enter_context(nc.psum_tensor(f"yps{i}", [128, 512], F32))
           for i in range(3)]

    barrier(endA)

    WO = op("gpsimd", lambda h: h.dma_start(
        out=wo_b[:], in_=wo_d.rearrange("(c p) o -> p c o", p=128)), "wo", 16)

    SLW, SGP = [None] * NCORES, [None] * NCORES

    def b_slabs():
        ranks = [0, 1, 2, 4, 5, 6]
        wt("scalar", CC)
        for i, r in enumerate(ranks[:5]):
            SLW[r] = op("scalar", lambda h, i=i, r=r: h.dma_start(
                out=gsl4[:, i, :],
                in_=cc_out.ap()[r * 128:(r + 1) * 128, :]),
                "slab%d" % (i % 4), 16)
        for i, r in enumerate(ranks[:5]):
            wt("vector", SLW[r])
            SGP[r] = eop("vector", lambda h, i=i, r=r: h.scalar_tensor_tensor(
                Sg[:], gsl4[:, i, :], pw_t[:, r:r + 1], Sg[:],
                ALU.mult, ALU.add))
        wt("scalar", SGP[ranks[0]])
        SLW[6] = op("scalar", lambda h: h.dma_start(
            out=gsl4[:, 0, :], in_=cc_out.ap()[6 * 128:7 * 128, :]),
            "slab0", 16)
        wt("vector", SLW[6])
        SGP[6] = eop("vector", lambda h: h.scalar_tensor_tensor(
            Sg[:], gsl4[:, 0, :], pw_t[:, 6:7], Sg[:],
            ALU.mult, ALU.add))
        SGP[NCORES - 1] = SGP[6]

    S65R = [None] * NT

    def am_off(hh):
        p, par = hh // 2, hh % 2
        return (p // 4) * 1024 + par * 512 + (p % 4) * 128

    PJQ, RMQ, PHIQ, TQQ, YMM, YEV, TQY, EXQ = \
        [None] * NT, [None] * NT, [None] * NT, [None] * NT, [None] * NT, \
        [None] * NT, [None] * NT, [None] * NT
    MKh = [[None, None] for _ in range(NT)]
    AMMh = [[None, None] for _ in range(NT)]
    MINQL = [None] * NT
    RQPL = [None] * NT
    PJQL = [None] * NT
    PJS = [None] * NT

    def b_qproj(c):
        if not (0 <= c < NT):
            return
        wt("tensor", XTB[c])
        if c == 0:
            wt("tensor", WQ)
        if c >= 1:
            wt("tensor", PJS[c - 1])
        last = None
        for cch in range(8):
            for hf in range(2):
                last = eop("tensor", lambda h, sl=c % 2, hf=hf, c2=cch:
                           h.matmul(
                    pjq[:, hf * 512:(hf + 1) * 512],
                    xtB[:, sl, c2, :], wqb[:, c2, hf * 512:(hf + 1) * 512],
                    start=(c2 == 0), stop=(c2 == 7),
                    skip_group_check=True))
        PJQL[c] = last
        wt("scalar", PJQL[c])
        if c >= 2:
            wt("scalar", RMQ[c - 2])
        PJS[c] = eop("scalar", lambda h, sl=c % 2: h.activation(
            pjs[:, sl, :], pjq[:], AF.Copy))

    def b_m(c):
        if not (0 <= c < NT):
            return
        wt("vector", PJS[c])
        if c >= 1:
            wt("vector", RQPL[c - 1])
        xx = pjs[:, c % 2, :].rearrange("p (g t w) -> p t g w", t=2, w=32)
        last = None
        for half in range(2):
            tab = cs_t[:, c, half * 32:half * 32 + 64].rearrange(
                "p (t o i) -> p t o i", t=2, o=1).broadcast_to(
                [128, 2, 16, 32])
            last = eop("vector", lambda h, half=half, tab=tab:
                       h.tensor_tensor(
                mB[:, 2 * half:2 * half + 2, :].rearrange(
                    "p t (g i) -> p t g i", i=32),
                xx, tab, ALU.mult))
        RMQ[c] = last

    def b_r(c):
        if not (0 <= c < NT):
            return
        wt("gpsimd", RMQ[c])
        if c >= 2:
            wt("gpsimd", PHIQ[c - 2])
        eop("gpsimd", lambda h, sl=c % 2: h.tensor_tensor(
            rq[:, sl, :].rearrange("p (g w) -> p g w", w=64)[:, :, 0:32],
            mB[:, 0, :].rearrange("p (g i) -> p g i", i=32),
            mB[:, 1, :].rearrange("p (g i) -> p g i", i=32), ALU.subtract))
        RQPL[c] = eop("gpsimd", lambda h, sl=c % 2: h.tensor_tensor(
            rq[:, sl, :].rearrange("p (g w) -> p g w", w=64)[:, :, 32:64],
            mB[:, 2, :].rearrange("p (g i) -> p g i", i=32),
            mB[:, 3, :].rearrange("p (g i) -> p g i", i=32), ALU.add))

    def b_exp(c):
        if not (0 <= c < NT):
            return
        wt("scalar", RQPL[c])
        eop("scalar", lambda h, sl=c % 2: h.activation(
            mnq[:, 0, :], rq[:, sl, :], AF.Relu, scale=cstc(C_NEG1)))
        if c >= 1:
            wt("scalar", PHIQ[c - 1])
        EXQ[c] = eop("scalar", lambda h: h.activation(
            eq[:, 0, :], mnq[:, 0, :], AF.Exp, scale=cstc(C_NEG1)))

    def b_stt(c):
        if not (0 <= c < NT):
            return
        wt("vector", EXQ[c])
        if c >= 1:
            wt("vector", TQQ[c - 1])
        PHIQ[c] = eop("vector", lambda h, sl=c % 2: h.scalar_tensor_tensor(
            qtm[:, 0, :], rq[:, sl, :], 0.0, eq[:, 0, :],
            ALU.max, ALU.add))

    def b_tqq(c):
        if not (0 <= c < NT):
            return
        wt("sync", PHIQ[c])
        if c >= 5:
            wt("sync", YMM[c - 5])
        TQQ[c] = op("sync", lambda h, sl=c % 5: h.dma_start_transpose(
            qfm[:, sl, :, :], qtm[:, 0, :]), "tqq%d" % (c % 2), 16)

    def b_amm(c, half):
        if not (0 <= c < NT):
            return
        if half == 0:
            wt("tensor", TQK[c])
            wt("tensor", TQQ[c])
            if c >= 1:
                wt("tensor", MKh[c - 1][1])
        else:
            wt("tensor", MKh[c][0])
        last = None
        for pp in range(4):
            p = half * 4 + pp
            for par in range(2):
                last = eop("tensor", lambda h, sl=c % 5, p=p, pp=pp, par=par,
                           c=c: h.matmul(
                    aps[par][:, pp * 128:(pp + 1) * 128],
                    phk_fm[par * 64:(par + 1) * 64, p,
                           c * 128:(c + 1) * 128],
                    qfm[par * 64:(par + 1) * 64, sl, p, :],
                    start=(pp == 0), stop=(pp == 3),
                    skip_group_check=True))
        AMMh[c][half] = last

    def b_mask(c, half):
        if not (0 <= c < NT):
            return
        wt("vector", AMMh[c][half])
        if c == 0 and half == 0:
            wt("vector", TRI)
        if c >= 5:
            wt("vector", YMM[c - 5])
        tribc = tri_b[:].rearrange("p (o i) -> p o i", o=1).broadcast_to(
            [128, 4, 128])
        last = None
        for par in range(2):
            last = eop("vector", lambda h, sl=c % 5, half=half, par=par,
                       tribc=tribc: h.tensor_tensor(
                Am[:, sl, half * 1024 + par * 512:
                   half * 1024 + par * 512 + 512].rearrange(
                    "p (g i) -> p g i", i=128),
                aps[par][:].rearrange("p (g i) -> p g i", i=128),
                tribc, ALU.mult))
        MKh[c][half] = last

    def b_s65(c):
        if not (1 <= c < NT):
            return
        if c == 1:
            wt("gpsimd", SGP[NCORES - 1])
        if c >= 2:
            wt("gpsimd", YMM[c - 2])
        S65R[c] = eop("gpsimd", lambda h, c=c, d2=c % 2: h.tensor_tensor(
            S65[:, d2, :], Sg[:], D_bf[:, c - 1, :], ALU.add))

    VB = [None] * NT

    def b_vload(c):
        if not (0 <= c < NT):
            return
        if c >= 3:
            wt("sync", YMM[c - 3])
        VB[c] = op("sync", lambda h, c=c, sl=c % 3: h.dma_start(
            out=v65[:, sl, :], in_=v65_d.ap()[:, c * 1040:(c + 1) * 1040]),
            "vb%d" % (c % 2), 16)

    def b_y(c):
        if not (0 <= c < NT):
            return
        wt("tensor", MKh[c][1])
        wt("tensor", S65R[c])
        wt("tensor", VB[c])
        if c >= 1:
            wt("tensor", YEV[c - 1])
        last = None
        for p in range(NP):
            g, off = p // 3, (p % 3) * 130
            for par in range(2):
                hh = 2 * p + par
                st = (p % 3 == 0 and par == 0)
                eop("tensor", lambda h, sl=c % 5, hh=hh, g=g, off=off,
                    par=par, c=c, st=st: h.matmul(
                    yps[g][:, off + par * 65: off + par * 65 + 65],
                    Am[:, sl, am_off(hh):am_off(hh) + 128],
                    v65[:, c % 3, hh * 65:(hh + 1) * 65],
                    start=st, stop=False, skip_group_check=True))
                sp2 = ((p % 3 == 2 or p == NP - 1) and par == 1)
                last = eop("tensor", lambda h, sl=c % 5, d2=c % 2, p=p,
                           par=par, g=g,
                           off=off, sp2=sp2: h.matmul(
                    yps[g][:, off + par * 65: off + par * 65 + 65],
                    qfm[par * 64:(par + 1) * 64, sl, p, :],
                    S65[par * 64:(par + 1) * 64, d2, p * 65:(p + 1) * 65],
                    start=False, stop=sp2, skip_group_check=True))
        YMM[c] = last

    def b_yev(c):
        if not (0 <= c < NT):
            return
        wt("vector", YMM[c])
        if c >= 2:
            wt("vector", TQY[c - 2])
        for g in range(3):
            ng = 3 if g < 2 else 2
            src2 = yps[g][:, 0:130 * ng].rearrange(
                "p (a b) -> p a b", b=65)[:, :, 64:65]
            eop("vector", lambda h, g=g, ng=ng, src2=src2: h.tensor_copy(
                denf[:, 6 * g: 6 * g + 2 * ng],
                src2.rearrange("p a b -> p (a b)")))
        eop("vector", lambda h: h.tensor_single_scalar(
            denm[:], denf[:], 1e-6, ALU.max))
        eop("vector", lambda h: h.reciprocal(rden[:], denm[:]))
        last = None
        for g in range(3):
            ng = 3 if g < 2 else 2
            ysrc = yps[g][:, 0:130 * ng].rearrange(
                "p (a b) -> p a b", b=65)[:, :, 0:64]
            ydst = Ytm[:, c % 2, 128 * 3 * g: 128 * 3 * g + 128 * ng].rearrange(
                "p (a b) -> p a b", b=64)
            rsrc = rden[:, 6 * g:6 * g + 2 * ng].rearrange(
                "p (a o) -> p a o", o=1).broadcast_to([128, 2 * ng, 64])
            last = eop("vector", lambda h, ysrc=ysrc, ydst=ydst,
                       rsrc=rsrc: h.tensor_tensor(
                ydst, ysrc, rsrc, ALU.mult))
        YEV[c] = last

    def b_tqy(c):
        if not (0 <= c < NT):
            return
        wt("sync", YEV[c])
        TQY[c] = op("sync", lambda h, c=c, sl=c % 2: h.dma_start_transpose(
            YT[:, :, c * 128:(c + 1) * 128], Ytm[:, sl, :]),
            "tqy%d" % (c % 2), 16)

    b_xload(0)
    for j in range(NT + 7):
        b_xload(j + 1)
        b_stt(j - 3)
        b_tqq(j - 3)
        b_amm(j - 4, 0)
        b_qproj(j)
        b_mask(j - 4, 0)
        b_amm(j - 4, 1)
        b_m(j - 1)
        b_mask(j - 4, 1)
        b_vload(j - 5)
        if j == 7:
            b_slabs()
            wt("vector", SGP[NCORES - 1])
            S65R[0] = eop("vector", lambda h: h.tensor_copy(
                S65[:, 0, :], Sg[:]))
        b_y(j - 7)
        b_s65(j - 6)
        b_yev(j - 7)
        b_r(j - 1)
        b_exp(j - 2)
        b_tqy(j - 7)

    endB = marks_now()
    esB.close()
    esAB.close()

    esC = ExitStack()
    rhq = esC.enter_context(nc.sbuf_tensor("rhq", [128, 8, D], BF16))
    qrT = esC.enter_context(nc.sbuf_tensor("qrT", [128, 2, D], BF16))
    rhf = esC.enter_context(nc.sbuf_tensor("rhf", [128, 2, D], F32))
    rsq = esC.enter_context(nc.sbuf_tensor("rsq", [128, 8], F32))
    rrq = esC.enter_context(nc.sbuf_tensor("rrq", [128, 8], F32))
    rs_row = esC.enter_context(nc.sbuf_tensor("rs_row", [1, D], F32))
    rsbc = esC.enter_context(nc.sbuf_tensor("rsbc", [128, D], F32))
    qfr = esC.enter_context(nc.sbuf_tensor("qfr", [128, D], F32))
    qir = esC.enter_context(nc.sbuf_tensor("qir", [128, D], I32))
    xC = esC.enter_context(nc.sbuf_tensor("xC", [128, 3, D], F32))
    z1 = esC.enter_context(nc.sbuf_tensor("z1", [128, 4, D], F32))
    h1 = esC.enter_context(nc.sbuf_tensor("h1", [128, 4, D], F32))
    z2 = esC.enter_context(nc.sbuf_tensor("z2", [128, 3, D], F32))
    sqd = esC.enter_context(nc.sbuf_tensor("sqd", [128, D], F32))
    t1 = esC.enter_context(nc.sbuf_tensor("t1", [128, 2, D], F32))
    t2 = esC.enter_context(nc.sbuf_tensor("t2", [128, 2, D], F32))
    qf = esC.enter_context(nc.sbuf_tensor("qf", [128, D], F32))
    qi = esC.enter_context(nc.sbuf_tensor("qi", [128, D], I32))
    qltm = esC.enter_context(nc.sbuf_tensor("qltm", [128, 2, D], BF16))
    qlT = esC.enter_context(nc.sbuf_tensor("qlT", [128, 4, 8, 128], BF16))
    lsv = esC.enter_context(nc.sbuf_tensor("lsv", [128, 4], F32))
    outb = esC.enter_context(nc.sbuf_tensor("outb", [128, 2, D], F32))
    sv = esC.enter_context(nc.sbuf_tensor("sv", [128, 60], F32))
    ops2 = [esC.enter_context(nc.psum_tensor(f"ops2_{i}", [128, D], F32))
            for i in range(2)]
    ffs2 = [esC.enter_context(nc.psum_tensor(f"ffs{i}", [128, D], F32))
            for i in range(2)]

    (SV_MU, SV_MS, SV_VAR, SV_L, SV_RSTD, SV_AM, SV_LS, SV_RLS,
     SV_S1, SV_S2, SV_MU2, SV_MS2, SV_VAR2, SV_L2, SV_RSTD2,
     SV_S3, SV_S4, SV_LNB, SV_LNB2, SV_MUN) = range(20)

    def svc(i, n):
        j = i + 20 * (n % 3)
        return sv[:, j:j + 1]

    def lsc(n):
        return lsv[:, (n % 4):(n % 4) + 1]

    RHL, QRT = [None] * 8, [None] * 8
    TQRL = [None] * 8

    def c_rhs_load(oc):
        if not (0 <= oc < 8):
            return
        if oc >= 2:
            wt("sync", QRT[oc - 2])
        RHL[oc] = op("sync", lambda h, oc=oc: h.dma_start(
            out=rhf[:, oc % 2, :], in_=rhsT_d[oc * 128:(oc + 1) * 128, :]),
            "rhf%d" % (oc % 2), 16)

    def c_rhs_quant(oc):
        if not (0 <= oc < 8):
            return
        wt("vector", RHL[oc])
        eop("vector", lambda h, oc=oc: h.tensor_reduce(
            rsq[:, oc:oc + 1], rhf[:, oc % 2, :], AXX,
            ALU.max, apply_absolute_value=True))
        eop("vector", lambda h, oc=oc: h.tensor_scalar(
            rsq[:, oc:oc + 1], rsq[:, oc:oc + 1], 1e-12, 1.0 / 127.0,
            ALU.max, ALU.mult))
        eop("vector", lambda h, oc=oc: h.reciprocal(
            rrq[:, oc:oc + 1], rsq[:, oc:oc + 1]))
        eop("vector", lambda h, oc=oc: h.tensor_tensor(
            qfr[:], rhf[:, oc % 2, :],
            rrq[:, oc:oc + 1].broadcast_to([128, D]), ALU.mult))
        eop("vector", lambda h: h.tensor_scalar(
            qir[:], qfr[:], 127.0, -127.0, ALU.min, ALU.max))
        if oc >= 2:
            wt("vector", TQRL[oc - 2])
        QRT[oc] = eop("vector", lambda h, oc=oc: h.tensor_copy(
            qrT[:, oc % 2, :], qir[:]))
        wt("sync", QRT[oc])
        TQRL[oc] = op("sync", lambda h, oc=oc: h.dma_start_transpose(
            rhq[:, :, oc * 128:(oc + 1) * 128], qrT[:, oc % 2, :]), "tqr", 16)

    def c_rs_row():
        RS1 = op("sync", lambda h: h.dma_start(out=rs_sc.ap(), in_=rsq[:]),
                 "rs1", 16)
        wt("sync", RS1)
        return op("sync", lambda h: h.dma_start(
            out=rs_row[:].rearrange("o (c p) -> o c p", p=128),
            in_=rs_sc.ap().rearrange("p (o c) -> o c p", o=1),
            allow_slow_non_contiguous=True), "rs2", 16)

    OPM, Z1V, SQ1, LNV1, LNR1, H1S = ([None] * NT for _ in range(6))
    LNB1V, LNB2V, SC3, T2P = ([None] * NT for _ in range(4))
    AMV, QIV, QBB, TQF, FFM = ([None] * NT for _ in range(5))
    EP0, Z2V, SQ2, LNV2, LNR2, FIN, STO, XCW = ([None] * NT for _ in range(8))
    cst_state = {}

    def c_xload(n):
        if not (0 <= n < NT):
            return
        if n >= 3:
            wt("sync", Z1V[n - 3])
        XCW[n] = op("sync", lambda h, n=n, sl=n % 3: h.dma_start(
            out=xC[:, sl, :], in_=x_d[n * 128:(n + 1) * 128, :]),
            "xc%d" % (n % 3), 16)

    def c_opm(n):
        if not (0 <= n < NT):
            return
        if n == 0:
            wt("tensor", WO)
        if n >= 2:
            wt("tensor", Z1V[n - 2])
        last = None
        for e in range(8):
            for hf in range(2):
                last = eop("tensor", lambda h, n=n, hf=hf, e=e: h.matmul(
                    ops2[n % 2][:, hf * 512:(hf + 1) * 512],
                    YT[:, e, n * 128:(n + 1) * 128],
                    wo_b[:, e, hf * 512:(hf + 1) * 512],
                    start=(e == 0), stop=(e == 7),
                    skip_group_check=True))
        OPM[n] = last

    def c_ln1(n):
        if not (0 <= n < NT):
            return
        wt("vector", OPM[n])
        wt("vector", XCW[n])
        if n >= 4:
            wt("vector", H1S[n - 4])
        Z1V[n] = eop("vector", lambda h, n=n, sl=n % 4, xl=n % 3:
                     h.scalar_tensor_tensor(
            z1[:, sl, :], ops2[n % 2][:], 1.0, xC[:, xl, :],
            ALU.mult, ALU.add, accum_out=svc(SV_S1, n)))
        wt("scalar", Z1V[n])
        if n == 0:
            wt("scalar", CONST)
        SQ1[n] = eop("scalar", lambda h, n=n, sl=n % 4: h.activation(
            sqd[:], z1[:, sl, :], AF.Square, accum_out=svc(SV_S2, n)))

    def c_lnr(n):
        if not (0 <= n < NT):
            return
        eop("scalar", lambda h, n=n: h.activation(
            svc(SV_MU, n), svc(SV_S1, n), AF.Copy, scale=cstc(C_INV1024)))
        eop("scalar", lambda h, n=n: h.activation(
            svc(SV_MS, n), svc(SV_S2, n), AF.Identity,
            scale=cstc(C_INV1024), bias=cstc(C_LNEPS)))
        eop("scalar", lambda h, n=n: h.activation(
            svc(SV_VAR, n), svc(SV_MU, n), AF.Square))
        eop("scalar", lambda h, n=n: h.activation(
            svc(SV_L, n), svc(SV_VAR, n), AF.Ln, scale=cstc(C_NEG1),
            bias=svc(SV_MS, n)))
        LNR1[n] = eop("scalar", lambda h, n=n: h.activation(
            svc(SV_RSTD, n), svc(SV_L, n), AF.Exp, scale=cstc(C_NHALF)))
        eop("scalar", lambda h, n=n: h.activation(
            svc(SV_MUN, n), svc(SV_S1, n), AF.Copy, scale=cstc(C_NINV1024)))
        LNB1V[n] = eop("scalar", lambda h, n=n: h.activation(
            svc(SV_LNB, n), svc(SV_MUN, n), AF.Copy, scale=svc(SV_RSTD, n)))

    def c_quant(n):
        if not (0 <= n < NT):
            return
        if n >= 4:
            wt("vector", Z2V[n - 4])
        H1S[n] = eop("vector", lambda h, n=n, sl=n % 4, zl=n % 4:
                     h.scalar_tensor_tensor(
            h1[:, sl, :], z1[:, zl, :], svc(SV_MU, n),
            rstd_bc(n), ALU.subtract, ALU.mult))
        eop("vector", lambda h, n=n, sl=n % 4: h.tensor_reduce(
            svc(SV_AM, n), h1[:, sl, :], AXX, ALU.max,
            apply_absolute_value=True))
        eop("vector", lambda h, n=n: h.tensor_scalar(
            lsc(n), svc(SV_AM, n), 1e-12, 1.0 / 127.0,
            ALU.max, ALU.mult))
        eop("vector", lambda h, n=n: h.reciprocal(
            svc(SV_RLS, n), lsc(n)))
        eop("vector", lambda h, n=n, sl=n % 4: h.tensor_scalar(
            qf[:], h1[:, sl, :], svc(SV_RLS, n), None, ALU.mult))
        QIV[n] = eop("vector", lambda h: h.tensor_scalar(
            qi[:], qf[:], 127.0, -127.0, ALU.min, ALU.max))
        if n >= 2:
            wt("vector", TQF[n - 2])
        QBB[n] = eop("vector", lambda h, sl=n % 2: h.tensor_copy(
            qltm[:, sl, :], qi[:]))
        wt("sync", QBB[n])
        if n >= 4:
            wt("sync", FFM[n - 4])
        TQF[n] = op("sync", lambda h, sl=n % 4, ql=n % 2: h.dma_start_transpose(
            qlT[:, sl, :, :], qltm[:, ql, :]), "tqf%d" % (n % 2), 16)

    def rstd_bc(n):
        return svc(SV_RSTD, n).broadcast_to([128, D])

    def rstd2_bc(n):
        return svc(SV_RSTD2, n).broadcast_to([128, D])

    def c_bc():
        wt("tensor", cst_state["RS2"])
        wt("tensor", CONST)
        last = None
        for hf in range(2):
            last = eop("tensor", lambda h, hf=hf: h.matmul(
                ffs2[1][:, hf * 512:(hf + 1) * 512], ones_f[:],
                rs_row[:, hf * 512:(hf + 1) * 512],
                start=True, stop=True, skip_group_check=True))
        wt("scalar", last)
        cst_state["RSBC"] = eop("scalar", lambda h: h.activation(
            rsbc[:], ffs2[1][:], AF.Copy))

    def c_ffn(n):
        if not (0 <= n < NT):
            return
        wt("tensor", TQF[n])
        if n == 0:
            wt("tensor", TQRL[7])
        if n == 1:
            wt("tensor", cst_state["RSBC"])
        if n >= 2:
            wt("tensor", EP0[n - 2])
        last = None
        for dch in range(8):
            for hf in range(2):
                last = eop("tensor", lambda h, sl=n % 4, fl=n % 2, hf=hf,
                           d2=dch: h.matmul(
                    ffs2[fl][:, hf * 512:(hf + 1) * 512],
                    qlT[:, sl, d2, :], rhq[:, d2, hf * 512:(hf + 1) * 512],
                    start=(d2 == 0), stop=(d2 == 7),
                    skip_group_check=True))
        FFM[n] = last

    def c_ln2a(n):
        if not (0 <= n < NT):
            return
        wt("scalar", FFM[n])
        if n >= 2:
            wt("scalar", T2P[n - 2])
        EP0[n] = eop("scalar", lambda h, n=n, fl=n % 2: h.activation(
            t1[:, n % 2, :], ffs2[fl][:], AF.Copy, scale=lsc(n)))
    def c_t2(n):
        if not (0 <= n < NT):
            return
        wt("vector", EP0[n])
        if n == 0:
            wt("vector", cst_state["RSBC"])
        T2P[n] = eop("vector", lambda h, n=n, sl=n % 2:
                     h.scalar_tensor_tensor(
            t2[:, sl, :], t1[:, sl, :], 1.0, rsbc[:], ALU.mult, ALU.mult,
            accum_out=svc(SV_S3, n)))

    def c_ln2z(n):
        if not (0 <= n < NT):
            return
        if n >= 3:
            wt("vector", FIN[n - 3])
        Z2V[n] = eop("vector", lambda h, n=n, sl=n % 3, hl=n % 4, tl=n % 2:
                     h.tensor_tensor(
            z2[:, sl, :], t2[:, tl, :], h1[:, hl, :], ALU.add))
        wt("scalar", Z2V[n])
        SQ2[n] = eop("scalar", lambda h, n=n, sl=n % 3: h.activation(
            sqd[:], z2[:, sl, :], AF.Square, accum_out=svc(SV_S4, n)))

    def c_ln2b(n):
        if not (0 <= n < NT):
            return
        eop("scalar", lambda h, n=n: h.activation(
            svc(SV_MU2, n), svc(SV_S3, n), AF.Copy, scale=cstc(C_INV1024)))
        eop("scalar", lambda h, n=n: h.activation(
            svc(SV_MS2, n), svc(SV_S4, n), AF.Identity,
            scale=cstc(C_INV1024), bias=cstc(C_LNEPS)))
        eop("scalar", lambda h, n=n: h.activation(
            svc(SV_VAR2, n), svc(SV_MU2, n), AF.Square))
        eop("scalar", lambda h, n=n: h.activation(
            svc(SV_L2, n), svc(SV_VAR2, n), AF.Ln, scale=cstc(C_NEG1),
            bias=svc(SV_MS2, n)))
        LNR2[n] = eop("scalar", lambda h, n=n: h.activation(
            svc(SV_RSTD2, n), svc(SV_L2, n), AF.Exp, scale=cstc(C_NHALF)))
        eop("scalar", lambda h, n=n: h.activation(
            svc(SV_MUN, n), svc(SV_S3, n), AF.Copy, scale=cstc(C_NINV1024)))
        LNB2V[n] = eop("scalar", lambda h, n=n: h.activation(
            svc(SV_LNB2, n), svc(SV_MUN, n), AF.Copy,
            scale=svc(SV_RSTD2, n)))

    def c_fin(n):
        if not (0 <= n < NT):
            return
        wt("vector", LNR2[n])
        if n >= 2:
            wt("vector", STO[n - 2])
        FIN[n] = eop("vector", lambda h, n=n, sl=n % 2, zl=n % 3:
                     h.scalar_tensor_tensor(
            outb[:, sl, :], z2[:, zl, :], svc(SV_MU2, n),
            rstd2_bc(n), ALU.subtract, ALU.mult))
        wt("sync", FIN[n])
        STO[n] = op("sync", lambda h, n=n, sl=n % 2: h.dma_start(
            out=out_d[n * 128:(n + 1) * 128, :], in_=outb[:, sl, :]),
            "sto%d" % (n % 2), 16)

    wt("sync", PHIQ[NT - 1])
    wt("sync", RQPL[NT - 1])
    wt("sync", EXQ[NT - 1])
    wt("vector", RQPL[NT - 1])
    wt("vector", EXQ[NT - 1])
    c_rhs_load(0)
    c_rhs_load(1)
    for oc in range(8):
        c_rhs_quant(oc)
        c_rhs_load(oc + 2)
    cst_state["RS2"] = c_rs_row()

    barrier(endB)

    c_xload(0)
    for j in range(NT + 8):
        c_xload(j + 1)
        c_ln1(j - 1)
        c_opm(j)
        if j == 1:
            c_bc()
        c_lnr(j - 2)
        c_quant(j - 3)
        c_ffn(j - 4)
        c_ln2a(j - 5)
        c_t2(j - 5)
        c_ln2z(j - 6)
        c_ln2b(j - 7)
        c_fin(j - 8)

    wt("sync", STO[NT - 2])
    wt("sync", STO[NT - 1])

    with nc.Block() as block:
        def runner(eng):
            def run(h):
                for item in streams[eng]:
                    if item[0] == "wait":
                        h.wait_ge(sem[item[1]], item[2])
                    else:
                        inst = item[1](h)
                        if item[2] is not None:
                            inst.then_inc(sem[item[2]], item[3])
            return run
        block.sync(runner("sync"))
        block.gpsimd(runner("gpsimd"))
        block.vector(runner("vector"))
        block.scalar(runner("scalar"))
        block.tensor(runner("tensor"))

    esC.close()
    esBC.close()
    es.close()
    return nc


_NC_CACHE = None


def kernel(L0_attn_x, L0_attn_Wq, L0_attn_Wk, L0_attn_Wv, L0_attn_Wo,
           L0_ffn_rhs):
    global _NC_CACHE
    x = np.ascontiguousarray(np.asarray(L0_attn_x, dtype=np.float32))
    wq = np.ascontiguousarray(np.asarray(L0_attn_Wq, dtype=np.float32).T)
    wk = np.ascontiguousarray(np.asarray(L0_attn_Wk, dtype=np.float32).T)
    wv = np.ascontiguousarray(np.asarray(L0_attn_Wv, dtype=np.float32).T)
    wo = np.ascontiguousarray(np.asarray(L0_attn_Wo, dtype=np.float32).T)
    rhsT = np.ascontiguousarray(np.asarray(L0_ffn_rhs, dtype=np.float32).T)

    inv = (10000.0 ** (-np.arange(HF, dtype=np.float64) / HF)).astype(np.float32)
    tri = np.ascontiguousarray(
        np.tril(np.ones((128, 128), dtype=np.float32)).T)
    cstv = np.zeros((128, 8), dtype=np.float32)
    cstv[:, 0] = 1.0 / 1024.0
    cstv[:, 1] = 1e-5
    cstv[:, 2] = -0.5
    cstv[:, 3] = -1.0
    cstv[:, 4] = -1.0 / 1024.0
    ones1 = np.ones((1, 128), dtype=np.float32)

    if _NC_CACHE is None:
        _NC_CACHE = build_nc()
    nc = _NC_CACHE

    in_maps = []
    for c in range(NCORES):
        b, qi2 = divmod(c, 4)
        t0 = qi2 * TC
        xs = np.ascontiguousarray(x[b, t0:t0 + TC, :])
        xTs = np.ascontiguousarray(xs.T)
        pos = (t0 + np.arange(TC)).astype(np.float32)
        ang = pos[:, None] * inv[None, :]
        cosv = np.cos(ang).astype(np.float32)
        sinv = np.sin(ang).astype(np.float32)
        cs = np.ascontiguousarray(np.concatenate(
            [cosv, sinv, cosv], axis=1))
        pw = np.zeros((128, NCORES), dtype=np.float32)
        for r in range(4 * b, 4 * b + qi2):
            pw[:, r] = 1.0
        in_maps.append({
            "x": xs, "xT": xTs, "wqT": wq, "wkT": wk, "wvT": wv, "woT": wo,
            "rhsT": rhsT, "cs": cs, "tri": tri, "pw": pw, "cstd": cstv,
            "ones1": ones1,
        })

    res = run_bass_kernel_spmd(nc, in_maps, core_ids=list(range(NCORES)))
    out = np.empty((2, 8192, 1024), dtype=np.float32)
    for c in range(NCORES):
        b, qi2 = divmod(c, 4)
        out[b, qi2 * TC:(qi2 + 1) * TC, :] = res.results[c]["out"]
    return out


# version 31
# speedup vs baseline: 1.0140x; 1.0140x over previous
import numpy as np
import concourse.bass as bass
import concourse.mybir as mybir
from concourse.bass_utils import run_bass_kernel_spmd
from contextlib import ExitStack

F32 = mybir.dt.float32
BF16 = mybir.dt.bfloat16
I32 = mybir.dt.int32
ALU = mybir.AluOpType
AF = mybir.ActivationFunctionType
AXX = mybir.AxisListType.X

NCORES = 8
TC = 2048
NT = 16
D = 1024
H = 16
HF = 32
NP = 8
SW = 65 * NP

ENGS = ("sync", "gpsimd", "vector", "scalar", "tensor")


def build_nc():
    nc = bass.Bass("TRN2", target_bir_lowering=False, debug=False)

    x_d = nc.dram_tensor("x", [TC, D], F32, kind="ExternalInput").ap()
    xT_d = nc.dram_tensor("xT", [D, TC], F32, kind="ExternalInput").ap()
    wq_d = nc.dram_tensor("wqT", [D, D], F32, kind="ExternalInput").ap()
    wk_d = nc.dram_tensor("wkT", [D, D], F32, kind="ExternalInput").ap()
    wv_d = nc.dram_tensor("wvT", [D, D], F32, kind="ExternalInput").ap()
    wo_d = nc.dram_tensor("woT", [D, D], F32, kind="ExternalInput").ap()
    rhsT_d = nc.dram_tensor("rhsT", [D, D], F32, kind="ExternalInput").ap()
    cs_d = nc.dram_tensor("cs", [TC, 3 * HF], F32, kind="ExternalInput").ap()
    tri_d = nc.dram_tensor("tri", [128, 128], F32, kind="ExternalInput").ap()
    pw_d = nc.dram_tensor("pw", [128, NCORES], F32, kind="ExternalInput").ap()
    cst_d = nc.dram_tensor("cstd", [128, 8], F32, kind="ExternalInput").ap()
    ones_d = nc.dram_tensor("ones1", [1, 128], F32, kind="ExternalInput").ap()
    out_d = nc.dram_tensor("out", [TC, D], F32, kind="ExternalOutput").ap()
    cc_in = nc.dram_tensor("cc_in", [128, SW], F32)
    cc_out = nc.dram_tensor("cc_out", [NCORES * 128, SW], F32)
    rs_sc = nc.dram_tensor("rs_sc", [128, 8], F32)
    v65_d = nc.dram_tensor("v65sp", [128, NT * 65 * H], BF16)

    streams = {e: [] for e in ENGS}
    cnt = {}
    es = ExitStack()
    sem = {}
    ENG_SEM = {"tensor": "pe", "vector": "vec", "scalar": "scl",
               "gpsimd": "pl", "sync": "syq"}

    def S(name):
        if name not in sem:
            sem[name] = es.enter_context(nc.semaphore("s_" + name))
            cnt[name] = 0
        return sem[name]

    def op(eng, fn, sm=None, inc=0):
        if sm is not None:
            S(sm)
            cnt[sm] += inc
            streams[eng].append(("op", fn, sm, inc))
            return (sm, cnt[sm])
        streams[eng].append(("op", fn, None, 0))
        return None

    def eop(eng, fn):
        own = ENG_SEM[eng]
        if eng != "tensor" and cnt.get(own, 0) > 0:
            streams[eng].append(("wait", own, cnt[own]))
        return op(eng, fn, own, 1)

    def wt(eng, spec):
        if spec is None:
            import sys
            print("WT-NONE-SKIP on", eng, file=sys.stderr)
            return
        sm, val = spec
        if val > 0:
            streams[eng].append(("wait", sm, val))

    def marks_now():
        return dict(cnt)

    def barrier(marks):
        for e in ENGS:
            own = ENG_SEM[e]
            for sm, val in marks.items():
                if sm != own and sm not in ("cc", "ccin") and val > 0:
                    streams[e].append(("wait", sm, val))

    Sg = es.enter_context(nc.sbuf_tensor("Sg", [128, SW], F32))
    tri_b = es.enter_context(nc.sbuf_tensor("tri_b", [128, 128], BF16))
    pw_t = es.enter_context(nc.sbuf_tensor("pw_t", [128, NCORES], F32))
    cst = es.enter_context(nc.sbuf_tensor("cst", [128, 8], F32))
    ones_f = es.enter_context(nc.sbuf_tensor("ones_f", [1, 128], F32))
    zb1 = es.enter_context(nc.sbuf_tensor("zb1", [1, 128], BF16))
    esAB = ExitStack()
    phk_fm = esAB.enter_context(nc.sbuf_tensor("phk_fm", [128, 8, TC], BF16))
    v65 = esAB.enter_context(nc.sbuf_tensor("v65", [128, 4, 65 * H], BF16))
    D_bf = esAB.enter_context(nc.sbuf_tensor("D_bf", [128, NT, SW], BF16))
    cs_t = esAB.enter_context(nc.sbuf_tensor("cs_t", [128, NT, 3 * HF], F32))
    wqb = esAB.enter_context(nc.sbuf_tensor("wqb", [128, 8, D], BF16))
    xtB = esAB.enter_context(nc.sbuf_tensor("xtB", [128, 2, 8, 128], BF16))

    C_INV1024, C_LNEPS, C_NHALF, C_NEG1, C_NINV1024 = 0, 1, 2, 3, 4

    def cstc(i):
        return cst[:, i:i + 1]

    TRI = op("gpsimd", lambda h: h.dma_start(out=tri_b[:], in_=tri_d),
             "tri", 16)
    op("sync", lambda h: h.dma_start(
        out=cs_t[:], in_=cs_d.rearrange("(n p) i -> p n i", p=128)),
        "const", 16)
    op("sync", lambda h: h.dma_start(out=pw_t[:], in_=pw_d), "const", 16)
    op("sync", lambda h: h.dma_start(out=cst[:], in_=cst_d), "const", 16)
    CONST = op("sync", lambda h: h.dma_start(out=ones_f[:], in_=ones_d),
               "const", 16)

    eop("vector", lambda h: h.memset(zb1[:], 0.0))
    eop("vector", lambda h: h.memset(zb5[:], 0.0))
    eop("vector", lambda h: h.memset(Sg[:], 0.0))
    V_INIT = eop("vector", lambda h: h.memset(
        v65[:].rearrange("p n (g w) -> p n g w", w=65)[:, :, :, 64:65], 1.0))

    esA = ExitStack()
    wkv = esA.enter_context(nc.sbuf_tensor("wkv", [128, 16, D], BF16))
    xtA = esA.enter_context(nc.sbuf_tensor("xtA", [128, 2, 8, 128], BF16))
    ktm = esA.enter_context(nc.sbuf_tensor("ktm", [128, 2, D], BF16))
    mA = esA.enter_context(nc.sbuf_tensor("mA", [128, 4, 512], F32))
    rk = esA.enter_context(nc.sbuf_tensor("rk", [128, 2, D], F32))
    mnk = esA.enter_context(nc.sbuf_tensor("mnk", [128, 1, D], F32))
    ek = esA.enter_context(nc.sbuf_tensor("ek", [128, 2, D], BF16))
    Stot = esA.enter_context(nc.sbuf_tensor("Stot", [128, SW], F32))
    zb5 = esA.enter_context(nc.sbuf_tensor("zb5", [1, 512], BF16))
    pjk = esA.enter_context(nc.psum_tensor("pjk", [128, D], F32))
    pjv = esA.enter_context(nc.psum_tensor("pjv", [128, D], F32))
    dps = [esA.enter_context(nc.psum_tensor(f"dps{i}", [128, 512], F32))
           for i in range(2)]

    WK = op("gpsimd", lambda h: h.dma_start(
        out=wkv[:, 0:8, :], in_=wk_d.rearrange("(c p) o -> p c o", p=128)),
        "wkv", 16)
    WKV = op("gpsimd", lambda h: h.dma_start(
        out=wkv[:, 8:16, :], in_=wv_d.rearrange("(c p) o -> p c o", p=128)),
        "wkv", 16)

    wt("tensor", V_INIT)
    for i in range(2):
        eop("tensor", lambda h, i=i: h.matmul(
            dps[i][:, 0:260], zb1[:], zb5[:, 0:260],
            start=True, stop=False, skip_group_check=True))

    PJ, VE, RMK, PHIK, DMM, DEV, TQK, XTA, EXK = \
        [None] * NT, [None] * NT, [None] * NT, [None] * NT, [None] * NT, \
        [None] * NT, [None] * NT, [None] * NT, [None] * NT
    MINKL = [None] * NT
    RKPL = [None] * NT
    PJKL = [None] * NT

    def a_xload(n):
        if not (0 <= n < NT):
            return
        if n >= 2:
            wt("gpsimd", PJ[n - 2])
        XTA[n] = op("gpsimd", lambda h, n=n, sl=n % 2: h.dma_start(
            out=xtA[:, sl, :, :],
            in_=xT_d[:, n * 128:(n + 1) * 128].rearrange(
                "(c p) t -> p c t", p=128)), "xta%d" % (n % 2), 16)

    def a_kvproj(n):
        if not (0 <= n < NT):
            return
        wt("tensor", XTA[n])
        if n == 0:
            wt("tensor", WK)
        if n >= 1:
            wt("tensor", RMK[n - 1])
        last = None
        for cch in range(8):
            for hf in range(2):
                last = eop("tensor", lambda h, sl=n % 2, hf=hf, c=cch:
                           h.matmul(
                    pjk[:, hf * 512:(hf + 1) * 512],
                    xtA[:, sl, c, :], wkv[:, c, hf * 512:(hf + 1) * 512],
                    start=(c == 0), stop=(c == 7),
                    skip_group_check=True))
        PJKL[n] = last
        if n == 0:
            wt("tensor", WKV)
        if n >= 1:
            wt("tensor", VE[n - 1])
        for cch in range(8):
            for hf in range(2):
                last = eop("tensor", lambda h, sl=n % 2, hf=hf, c=cch:
                           h.matmul(
                    pjv[:, hf * 512:(hf + 1) * 512],
                    xtA[:, sl, c, :], wkv[:, 8 + c, hf * 512:(hf + 1) * 512],
                    start=(c == 0), stop=(c == 7),
                    skip_group_check=True))
        PJ[n] = last

    def a_D(n):
        if not (0 <= n < NT):
            return
        wt("tensor", PHIK[n])
        wt("tensor", VE[n])
        if n >= 1:
            wt("tensor", DEV[n - 1])
        last = None
        for p in range(NP):
            bank, cb = p // 4, (p % 4) * 65
            for par in range(2):
                hh = 2 * p + par
                last = eop("tensor", lambda h, sl=n % 2, par=par, hh=hh,
                           bank=bank, cb=cb, n=n: h.matmul(
                    dps[bank][par * 64:(par + 1) * 64, cb:cb + 65],
                    ktm[:, sl, hh * 64:(hh + 1) * 64],
                    v65[:, n % 4, hh * 65:(hh + 1) * 65],
                    start=False,
                    stop=(n == NT - 1 and hh == 2 * NP - 1),
                    tile_position=(0, par * 64), skip_group_check=True))
        DMM[n] = last

    def a_m(n):
        if not (0 <= n < NT):
            return
        wt("vector", PJKL[n])
        if n == 0:
            wt("vector", CONST)
        if n >= 1:
            wt("vector", RKPL[n - 1])
        xx = pjk[:].rearrange("p (g t w) -> p t g w", t=2, w=32)
        last = None
        for half in range(2):
            tab = cs_t[:, n, half * 32:half * 32 + 64].rearrange(
                "p (t o i) -> p t o i", t=2, o=1).broadcast_to(
                [128, 2, 16, 32])
            last = eop("vector", lambda h, half=half, tab=tab:
                       h.tensor_tensor(
                mA[:, 2 * half:2 * half + 2, :].rearrange(
                    "p t (g i) -> p t g i", i=32),
                xx, tab, ALU.mult))
        RMK[n] = last

    def a_r(n):
        if not (0 <= n < NT):
            return
        wt("gpsimd", RMK[n])
        if n >= 2:
            wt("gpsimd", PHIK[n - 2])
        eop("gpsimd", lambda h, sl=n % 2: h.tensor_tensor(
            rk[:, sl, :].rearrange("p (g w) -> p g w", w=64)[:, :, 0:32],
            mA[:, 0, :].rearrange("p (g i) -> p g i", i=32),
            mA[:, 1, :].rearrange("p (g i) -> p g i", i=32), ALU.subtract))
        RKPL[n] = eop("gpsimd", lambda h, sl=n % 2: h.tensor_tensor(
            rk[:, sl, :].rearrange("p (g w) -> p g w", w=64)[:, :, 32:64],
            mA[:, 2, :].rearrange("p (g i) -> p g i", i=32),
            mA[:, 3, :].rearrange("p (g i) -> p g i", i=32), ALU.add))

    def a_exp(n):
        if not (0 <= n < NT):
            return
        wt("scalar", RKPL[n])
        if n == 0:
            wt("scalar", CONST)
        eop("scalar", lambda h, sl=n % 2: h.activation(
            mnk[:, 0, :], rk[:, sl, :], AF.Relu, scale=cstc(C_NEG1)))
        if n >= 2:
            wt("scalar", PHIK[n - 2])
        EXK[n] = eop("scalar", lambda h, sl=n % 2: h.activation(
            ek[:, sl, :], mnk[:, 0, :], AF.Exp, scale=cstc(C_NEG1)))

    def a_stt(n):
        if not (0 <= n < NT):
            return
        wt("vector", EXK[n])
        if n >= 2:
            wt("vector", TQK[n - 2])
        if n >= 2:
            wt("vector", DMM[n - 2])
        PHIK[n] = eop("vector", lambda h, sl=n % 2: h.scalar_tensor_tensor(
            ktm[:, sl, :], rk[:, sl, :], 0.0, ek[:, sl, :],
            ALU.max, ALU.add))

    SPV = [None] * NT

    def a_vevac(n):
        if not (0 <= n < NT):
            return
        wt("scalar", PJ[n])
        if n == 0:
            wt("scalar", V_INIT)
        if n >= 4:
            wt("scalar", SPV[n - 4])
            wt("scalar", DMM[n - 4])
        VE[n] = eop("scalar", lambda h, n=n, sl=n % 4: h.activation(
            v65[:, sl, :].rearrange("p (g w) -> p g w", w=65)[:, :, 0:64],
            pjv[:].rearrange("p (g w) -> p g w", w=64), AF.Copy))
        wt("scalar", VE[n])
        SPV[n] = op("scalar", lambda h, n=n, sl=n % 4: h.dma_start(
            out=v65_d.ap()[:, n * 1040:(n + 1) * 1040], in_=v65[:, sl, :]),
            "spv%d" % (n % 2), 16)

    def a_tq(n):
        if not (0 <= n < NT):
            return
        wt("sync", PHIK[n])
        TQK[n] = op("sync", lambda h, n=n, sl=n % 2: h.dma_start_transpose(
            phk_fm[:, :, n * 128:(n + 1) * 128], ktm[:, sl, :]),
            "tqk%d" % (n % 2), 16)

    def a_devac(n):
        if not (0 <= n < NT):
            return
        wt("vector", DMM[n])
        eop("vector", lambda h, n=n: h.tensor_copy(
            D_bf[:, n, 0:260], dps[0][:, 0:260]))
        DEV[n] = eop("vector", lambda h, n=n: h.tensor_copy(
            D_bf[:, n, 260:520], dps[1][:, 0:260]))

    XTB = [None] * NT
    WQ = None

    def b_xload(c):
        if not (0 <= c < NT):
            return
        if XTB[c] is not None:
            return
        if c >= 2:
            wt("gpsimd", PJQL[c - 2])
        XTB[c] = op("gpsimd", lambda h, c=c, sl=c % 2: h.dma_start(
            out=xtB[:, sl, :, :],
            in_=xT_d[:, c * 128:(c + 1) * 128].rearrange(
                "(c p) t -> p c t", p=128)), "xtb%d" % (c % 2), 16)

    a_xload(0)
    for j in range(NT + 2):
        a_xload(j + 1)
        a_kvproj(j)
        a_m(j)
        a_stt(j - 2)
        a_tq(j - 2)
        a_D(j - 2)
        a_devac(j - 2)
        a_r(j)
        a_vevac(j)
        a_exp(j - 1)
        if 8 <= j <= 15:
            cch = j - 8
            WQ = op("gpsimd", lambda h, c=cch: h.dma_start(
                out=wqb[:, c, :],
                in_=wq_d[c * 128:(c + 1) * 128, :]), "wq", 16)
        if j == 14:
            b_xload(0)
        if j == 15:
            b_xload(1)

    eop("vector", lambda h: h.tensor_copy(Stot[:, 0:260], dps[0][:, 0:260]))
    V_STOT = eop("vector", lambda h: h.tensor_copy(
        Stot[:, 260:520], dps[1][:, 0:260]))
    wt("gpsimd", V_STOT)
    CCW = op("gpsimd", lambda h: h.dma_start(out=cc_in.ap(), in_=Stot[:]),
             "ccin", 16)
    wt("gpsimd", CCW)
    CC = op("gpsimd", lambda h: h.collective_compute(
        "AllGather", ALU.bypass, replica_groups=[list(range(NCORES))],
        ins=[cc_in.ap().opt()], outs=[cc_out.ap().opt()]), "cc", 1)

    endA = marks_now()
    esA.close()
    esBC = ExitStack()
    YT = esBC.enter_context(nc.sbuf_tensor("YT", [128, 8, TC], BF16, side="right"))
    wo_b = esBC.enter_context(nc.sbuf_tensor("wo_b", [128, 8, D], BF16, side="right"))

    esB = ExitStack()
    Am = esB.enter_context(nc.sbuf_tensor("Am", [128, 4, 2048], BF16))
    qfm = esB.enter_context(nc.sbuf_tensor("qfm", [128, 4, 8, 128], BF16))
    Ytm = esB.enter_context(nc.sbuf_tensor("Ytm", [128, 2, D], BF16))
    S65 = esB.enter_context(nc.sbuf_tensor("S65", [128, 2, SW], BF16))
    denf = esB.enter_context(nc.sbuf_tensor("denf", [128, H], F32))
    rden = esB.enter_context(nc.sbuf_tensor("rden", [128, H], F32))
    denm = esB.enter_context(nc.sbuf_tensor("denm", [128, H], F32))
    gsl4 = esB.enter_context(nc.sbuf_tensor("gsl4", [128, 5, SW], F32))
    mB = esB.enter_context(nc.sbuf_tensor("mB", [128, 4, 512], F32))
    pjs = esB.enter_context(nc.sbuf_tensor("pjs", [128, 2, D], F32))
    rq = esB.enter_context(nc.sbuf_tensor("rq", [128, 2, D], F32))
    mnq = esB.enter_context(nc.sbuf_tensor("mnq", [128, 1, D], F32))
    eq = esB.enter_context(nc.sbuf_tensor("eq", [128, 1, D], BF16))
    qtm = esB.enter_context(nc.sbuf_tensor("qtm", [128, 2, D], BF16))
    pjq = esB.enter_context(nc.psum_tensor("pjq", [128, D], F32))
    apsm = esB.enter_context(nc.psum_tensor("apsm", [128, D], F32))
    yps = [esB.enter_context(nc.psum_tensor(f"yps{i}", [128, 512], F32))
           for i in range(3)]

    barrier(endA)

    WO = op("gpsimd", lambda h: h.dma_start(
        out=wo_b[:], in_=wo_d.rearrange("(c p) o -> p c o", p=128)), "wo", 16)

    SLW, SGP = [None] * NCORES, [None] * NCORES

    def b_slabs():
        ranks = [0, 1, 2, 4, 5, 6]
        wt("scalar", CC)
        for i, r in enumerate(ranks[:5]):
            SLW[r] = op("scalar", lambda h, i=i, r=r: h.dma_start(
                out=gsl4[:, i, :],
                in_=cc_out.ap()[r * 128:(r + 1) * 128, :]),
                "slab%d" % (i % 4), 16)
        for i, r in enumerate(ranks[:5]):
            wt("vector", SLW[r])
            SGP[r] = eop("vector", lambda h, i=i, r=r: h.scalar_tensor_tensor(
                Sg[:], gsl4[:, i, :], pw_t[:, r:r + 1], Sg[:],
                ALU.mult, ALU.add))
        wt("scalar", SGP[ranks[0]])
        SLW[6] = op("scalar", lambda h: h.dma_start(
            out=gsl4[:, 0, :], in_=cc_out.ap()[6 * 128:7 * 128, :]),
            "slab0", 16)
        wt("vector", SLW[6])
        SGP[6] = eop("vector", lambda h: h.scalar_tensor_tensor(
            Sg[:], gsl4[:, 0, :], pw_t[:, 6:7], Sg[:],
            ALU.mult, ALU.add))
        SGP[NCORES - 1] = SGP[6]

    S65R = [None] * NT

    def am_off(hh):
        p, par = hh // 2, hh % 2
        return (p // 4) * 1024 + par * 512 + (p % 4) * 128

    PJQ, RMQ, PHIQ, TQQ, YMM, YEV, TQY, EXQ = \
        [None] * NT, [None] * NT, [None] * NT, [None] * NT, [None] * NT, \
        [None] * NT, [None] * NT, [None] * NT
    MKh = [[None, None] for _ in range(NT)]
    AMMh = [[None, None] for _ in range(NT)]
    MINQL = [None] * NT
    RQPL = [None] * NT
    PJQL = [None] * NT
    PJS = [None] * NT

    def b_qproj(c):
        if not (0 <= c < NT):
            return
        wt("tensor", XTB[c])
        if c == 0:
            wt("tensor", WQ)
        if c >= 1:
            wt("tensor", PJS[c - 1])
        last = None
        for cch in range(8):
            for hf in range(2):
                last = eop("tensor", lambda h, sl=c % 2, hf=hf, c2=cch:
                           h.matmul(
                    pjq[:, hf * 512:(hf + 1) * 512],
                    xtB[:, sl, c2, :], wqb[:, c2, hf * 512:(hf + 1) * 512],
                    start=(c2 == 0), stop=(c2 == 7),
                    skip_group_check=True))
        PJQL[c] = last
        wt("scalar", PJQL[c])
        if c >= 2:
            wt("scalar", RMQ[c - 2])
        PJS[c] = eop("scalar", lambda h, sl=c % 2: h.activation(
            pjs[:, sl, :], pjq[:], AF.Copy))

    def b_m(c):
        if not (0 <= c < NT):
            return
        wt("vector", PJS[c])
        if c >= 1:
            wt("vector", RQPL[c - 1])
        xx = pjs[:, c % 2, :].rearrange("p (g t w) -> p t g w", t=2, w=32)
        last = None
        for half in range(2):
            tab = cs_t[:, c, half * 32:half * 32 + 64].rearrange(
                "p (t o i) -> p t o i", t=2, o=1).broadcast_to(
                [128, 2, 16, 32])
            last = eop("vector", lambda h, half=half, tab=tab:
                       h.tensor_tensor(
                mB[:, 2 * half:2 * half + 2, :].rearrange(
                    "p t (g i) -> p t g i", i=32),
                xx, tab, ALU.mult))
        RMQ[c] = last

    def b_r(c):
        if not (0 <= c < NT):
            return
        wt("gpsimd", RMQ[c])
        if c >= 2:
            wt("gpsimd", PHIQ[c - 2])
        eop("gpsimd", lambda h, sl=c % 2: h.tensor_tensor(
            rq[:, sl, :].rearrange("p (g w) -> p g w", w=64)[:, :, 0:32],
            mB[:, 0, :].rearrange("p (g i) -> p g i", i=32),
            mB[:, 1, :].rearrange("p (g i) -> p g i", i=32), ALU.subtract))
        RQPL[c] = eop("gpsimd", lambda h, sl=c % 2: h.tensor_tensor(
            rq[:, sl, :].rearrange("p (g w) -> p g w", w=64)[:, :, 32:64],
            mB[:, 2, :].rearrange("p (g i) -> p g i", i=32),
            mB[:, 3, :].rearrange("p (g i) -> p g i", i=32), ALU.add))

    def b_exp(c):
        if not (0 <= c < NT):
            return
        wt("scalar", RQPL[c])
        eop("scalar", lambda h, sl=c % 2: h.activation(
            mnq[:, 0, :], rq[:, sl, :], AF.Relu, scale=cstc(C_NEG1)))
        if c >= 1:
            wt("scalar", PHIQ[c - 1])
        EXQ[c] = eop("scalar", lambda h: h.activation(
            eq[:, 0, :], mnq[:, 0, :], AF.Exp, scale=cstc(C_NEG1)))

    def b_stt(c):
        if not (0 <= c < NT):
            return
        wt("vector", EXQ[c])
        if c >= 2:
            wt("vector", TQQ[c - 2])
        PHIQ[c] = eop("vector", lambda h, sl=c % 2: h.scalar_tensor_tensor(
            qtm[:, sl, :], rq[:, sl, :], 0.0, eq[:, 0, :],
            ALU.max, ALU.add))

    def b_tqq(c):
        if not (0 <= c < NT):
            return
        wt("sync", PHIQ[c])
        if c >= 4:
            wt("sync", YMM[c - 4])
        TQQ[c] = op("sync", lambda h, sl=c % 4, ql=c % 2: h.dma_start_transpose(
            qfm[:, sl, :, :], qtm[:, ql, :]), "tqq%d" % (c % 2), 16)

    def b_amm(c, half):
        if not (0 <= c < NT):
            return
        if half == 0:
            wt("tensor", TQK[c])
            wt("tensor", TQQ[c])
            if c >= 1:
                wt("tensor", MKh[c - 1][1])
        else:
            wt("tensor", MKh[c][0])
        last = None
        for pp in range(4):
            p = half * 4 + pp
            for par in range(2):
                last = eop("tensor", lambda h, sl=c % 4, p=p, pp=pp, par=par,
                           c=c: h.matmul(
                    apsm[:, par * 512 + pp * 128:par * 512 + pp * 128 + 128],
                    phk_fm[par * 64:(par + 1) * 64, p,
                           c * 128:(c + 1) * 128],
                    qfm[par * 64:(par + 1) * 64, sl, p, :],
                    start=(pp == 0), stop=(pp == 3),
                    skip_group_check=True))
        AMMh[c][half] = last

    def b_mask(c, half):
        if not (0 <= c < NT):
            return
        wt("vector", AMMh[c][half])
        if c == 0 and half == 0:
            wt("vector", TRI)
        if c >= 4:
            wt("vector", YMM[c - 4])
        tribc = tri_b[:].rearrange("p (o i) -> p o i", o=1).broadcast_to(
            [128, 8, 128])
        MKh[c][half] = eop("vector", lambda h, sl=c % 4, half=half,
                           tribc=tribc: h.tensor_tensor(
            Am[:, sl, half * 1024:half * 1024 + 1024].rearrange(
                "p (g i) -> p g i", i=128),
            apsm[:].rearrange("p (g i) -> p g i", i=128),
            tribc, ALU.mult))

    def b_s65(c):
        if not (1 <= c < NT):
            return
        if c == 1:
            wt("gpsimd", SGP[NCORES - 1])
        if c >= 2:
            wt("gpsimd", YMM[c - 2])
        S65R[c] = eop("gpsimd", lambda h, c=c, d2=c % 2: h.tensor_tensor(
            S65[:, d2, :], Sg[:], D_bf[:, c - 1, :], ALU.add))

    VB = [None] * NT

    def b_vload(c):
        if not (0 <= c < NT):
            return
        if c >= 4:
            wt("sync", YMM[c - 4])
        VB[c] = op("sync", lambda h, c=c, sl=c % 4: h.dma_start(
            out=v65[:, sl, :], in_=v65_d.ap()[:, c * 1040:(c + 1) * 1040]),
            "vb%d" % (c % 2), 16)

    def b_y(c):
        if not (0 <= c < NT):
            return
        wt("tensor", MKh[c][1])
        wt("tensor", S65R[c])
        wt("tensor", VB[c])
        if c >= 1:
            wt("tensor", YEV[c - 1])
        last = None
        for p in range(NP):
            g, off = p // 3, (p % 3) * 130
            for par in range(2):
                hh = 2 * p + par
                st = (p % 3 == 0 and par == 0)
                eop("tensor", lambda h, sl=c % 4, hh=hh, g=g, off=off,
                    par=par, c=c, st=st: h.matmul(
                    yps[g][:, off + par * 65: off + par * 65 + 65],
                    Am[:, sl, am_off(hh):am_off(hh) + 128],
                    v65[:, c % 4, hh * 65:(hh + 1) * 65],
                    start=st, stop=False, skip_group_check=True))
                sp2 = ((p % 3 == 2 or p == NP - 1) and par == 1)
                last = eop("tensor", lambda h, sl=c % 4, d2=c % 2, p=p,
                           par=par, g=g,
                           off=off, sp2=sp2: h.matmul(
                    yps[g][:, off + par * 65: off + par * 65 + 65],
                    qfm[par * 64:(par + 1) * 64, sl, p, :],
                    S65[par * 64:(par + 1) * 64, d2, p * 65:(p + 1) * 65],
                    start=False, stop=sp2, skip_group_check=True))
        YMM[c] = last

    def b_yev(c):
        if not (0 <= c < NT):
            return
        wt("vector", YMM[c])
        if c >= 2:
            wt("vector", TQY[c - 2])
        for g in range(3):
            ng = 3 if g < 2 else 2
            src2 = yps[g][:, 0:130 * ng].rearrange(
                "p (a b) -> p a b", b=65)[:, :, 64:65]
            eop("vector", lambda h, g=g, ng=ng, src2=src2: h.tensor_copy(
                denf[:, 6 * g: 6 * g + 2 * ng],
                src2.rearrange("p a b -> p (a b)")))
        eop("vector", lambda h: h.tensor_single_scalar(
            denm[:], denf[:], 1e-6, ALU.max))
        eop("vector", lambda h: h.reciprocal(rden[:], denm[:]))
        last = None
        for g in range(3):
            ng = 3 if g < 2 else 2
            ysrc = yps[g][:, 0:130 * ng].rearrange(
                "p (a b) -> p a b", b=65)[:, :, 0:64]
            ydst = Ytm[:, c % 2, 128 * 3 * g: 128 * 3 * g + 128 * ng].rearrange(
                "p (a b) -> p a b", b=64)
            rsrc = rden[:, 6 * g:6 * g + 2 * ng].rearrange(
                "p (a o) -> p a o", o=1).broadcast_to([128, 2 * ng, 64])
            last = eop("vector", lambda h, ysrc=ysrc, ydst=ydst,
                       rsrc=rsrc: h.tensor_tensor(
                ydst, ysrc, rsrc, ALU.mult))
        YEV[c] = last

    def b_tqy(c):
        if not (0 <= c < NT):
            return
        wt("sync", YEV[c])
        TQY[c] = op("sync", lambda h, c=c, sl=c % 2: h.dma_start_transpose(
            YT[:, :, c * 128:(c + 1) * 128], Ytm[:, sl, :]),
            "tqy%d" % (c % 2), 16)

    b_xload(0)
    for j in range(NT + 5):
        b_xload(j + 1)
        b_stt(j - 3)
        b_tqq(j - 3)
        b_amm(j - 4, 0)
        b_qproj(j)
        b_mask(j - 4, 0)
        b_amm(j - 4, 1)
        b_m(j - 1)
        b_mask(j - 4, 1)
        b_vload(j - 3)
        if j == 5:
            b_slabs()
            wt("vector", SGP[NCORES - 1])
            S65R[0] = eop("vector", lambda h: h.tensor_copy(
                S65[:, 0, :], Sg[:]))
        b_y(j - 5)
        b_s65(j - 4)
        b_yev(j - 5)
        b_r(j - 1)
        b_exp(j - 2)
        b_tqy(j - 5)

    endB = marks_now()
    esB.close()
    esAB.close()

    esC = ExitStack()
    rhq = esC.enter_context(nc.sbuf_tensor("rhq", [128, 8, D], BF16))
    qrT = esC.enter_context(nc.sbuf_tensor("qrT", [128, 2, D], BF16))
    rhf = esC.enter_context(nc.sbuf_tensor("rhf", [128, 2, D], F32))
    rsq = esC.enter_context(nc.sbuf_tensor("rsq", [128, 8], F32))
    rrq = esC.enter_context(nc.sbuf_tensor("rrq", [128, 8], F32))
    rs_row = esC.enter_context(nc.sbuf_tensor("rs_row", [1, D], F32))
    rsbc = esC.enter_context(nc.sbuf_tensor("rsbc", [128, D], F32))
    qfr = esC.enter_context(nc.sbuf_tensor("qfr", [128, D], F32))
    qir = esC.enter_context(nc.sbuf_tensor("qir", [128, D], I32))
    xC = esC.enter_context(nc.sbuf_tensor("xC", [128, 3, D], F32))
    z1 = esC.enter_context(nc.sbuf_tensor("z1", [128, 4, D], F32))
    h1 = esC.enter_context(nc.sbuf_tensor("h1", [128, 4, D], F32))
    z2 = esC.enter_context(nc.sbuf_tensor("z2", [128, 3, D], F32))
    sqd = esC.enter_context(nc.sbuf_tensor("sqd", [128, D], F32))
    t1 = esC.enter_context(nc.sbuf_tensor("t1", [128, 2, D], F32))
    t2 = esC.enter_context(nc.sbuf_tensor("t2", [128, 2, D], F32))
    qf = esC.enter_context(nc.sbuf_tensor("qf", [128, D], F32))
    qi = esC.enter_context(nc.sbuf_tensor("qi", [128, D], I32))
    qltm = esC.enter_context(nc.sbuf_tensor("qltm", [128, 2, D], BF16))
    qlT = esC.enter_context(nc.sbuf_tensor("qlT", [128, 4, 8, 128], BF16))
    lsv = esC.enter_context(nc.sbuf_tensor("lsv", [128, 4], F32))
    outb = esC.enter_context(nc.sbuf_tensor("outb", [128, 2, D], F32))
    sv = esC.enter_context(nc.sbuf_tensor("sv", [128, 60], F32))
    ops2 = [esC.enter_context(nc.psum_tensor(f"ops2_{i}", [128, D], F32))
            for i in range(2)]
    ffs2 = [esC.enter_context(nc.psum_tensor(f"ffs{i}", [128, D], F32))
            for i in range(2)]

    (SV_MU, SV_MS, SV_VAR, SV_L, SV_RSTD, SV_AM, SV_LS, SV_RLS,
     SV_S1, SV_S2, SV_MU2, SV_MS2, SV_VAR2, SV_L2, SV_RSTD2,
     SV_S3, SV_S4, SV_LNB, SV_LNB2, SV_MUN) = range(20)

    def svc(i, n):
        j = i + 20 * (n % 3)
        return sv[:, j:j + 1]

    def lsc(n):
        return lsv[:, (n % 4):(n % 4) + 1]

    RHL, QRT = [None] * 8, [None] * 8
    TQRL = [None] * 8

    def c_rhs_load(oc):
        if not (0 <= oc < 8):
            return
        if oc >= 2:
            wt("sync", QRT[oc - 2])
        RHL[oc] = op("sync", lambda h, oc=oc: h.dma_start(
            out=rhf[:, oc % 2, :], in_=rhsT_d[oc * 128:(oc + 1) * 128, :]),
            "rhf%d" % (oc % 2), 16)

    def c_rhs_quant(oc):
        if not (0 <= oc < 8):
            return
        wt("vector", RHL[oc])
        eop("vector", lambda h, oc=oc: h.tensor_reduce(
            rsq[:, oc:oc + 1], rhf[:, oc % 2, :], AXX,
            ALU.max, apply_absolute_value=True))
        eop("vector", lambda h, oc=oc: h.tensor_scalar(
            rsq[:, oc:oc + 1], rsq[:, oc:oc + 1], 1e-12, 1.0 / 127.0,
            ALU.max, ALU.mult))
        eop("vector", lambda h, oc=oc: h.reciprocal(
            rrq[:, oc:oc + 1], rsq[:, oc:oc + 1]))
        eop("vector", lambda h, oc=oc: h.tensor_tensor(
            qfr[:], rhf[:, oc % 2, :],
            rrq[:, oc:oc + 1].broadcast_to([128, D]), ALU.mult))
        eop("vector", lambda h: h.tensor_scalar(
            qir[:], qfr[:], 127.0, -127.0, ALU.min, ALU.max))
        if oc >= 2:
            wt("vector", TQRL[oc - 2])
        QRT[oc] = eop("vector", lambda h, oc=oc: h.tensor_copy(
            qrT[:, oc % 2, :], qir[:]))
        wt("sync", QRT[oc])
        TQRL[oc] = op("sync", lambda h, oc=oc: h.dma_start_transpose(
            rhq[:, :, oc * 128:(oc + 1) * 128], qrT[:, oc % 2, :]), "tqr", 16)

    def c_rs_row():
        RS1 = op("sync", lambda h: h.dma_start(out=rs_sc.ap(), in_=rsq[:]),
                 "rs1", 16)
        wt("sync", RS1)
        return op("sync", lambda h: h.dma_start(
            out=rs_row[:].rearrange("o (c p) -> o c p", p=128),
            in_=rs_sc.ap().rearrange("p (o c) -> o c p", o=1),
            allow_slow_non_contiguous=True), "rs2", 16)

    OPM, Z1V, SQ1, LNV1, LNR1, H1S = ([None] * NT for _ in range(6))
    LNB1V, LNB2V, SC3, T2P = ([None] * NT for _ in range(4))
    AMV, QIV, QBB, TQF, FFM = ([None] * NT for _ in range(5))
    EP0, Z2V, SQ2, LNV2, LNR2, FIN, STO, XCW = ([None] * NT for _ in range(8))
    cst_state = {}

    def c_xload(n):
        if not (0 <= n < NT):
            return
        if n >= 3:
            wt("sync", Z1V[n - 3])
        XCW[n] = op("sync", lambda h, n=n, sl=n % 3: h.dma_start(
            out=xC[:, sl, :], in_=x_d[n * 128:(n + 1) * 128, :]),
            "xc%d" % (n % 3), 16)

    def c_opm(n):
        if not (0 <= n < NT):
            return
        if n == 0:
            wt("tensor", WO)
        if n >= 2:
            wt("tensor", Z1V[n - 2])
        last = None
        for e in range(8):
            for hf in range(2):
                last = eop("tensor", lambda h, n=n, hf=hf, e=e: h.matmul(
                    ops2[n % 2][:, hf * 512:(hf + 1) * 512],
                    YT[:, e, n * 128:(n + 1) * 128],
                    wo_b[:, e, hf * 512:(hf + 1) * 512],
                    start=(e == 0), stop=(e == 7),
                    skip_group_check=True))
        OPM[n] = last

    def c_ln1(n):
        if not (0 <= n < NT):
            return
        wt("vector", OPM[n])
        wt("vector", XCW[n])
        if n >= 4:
            wt("vector", H1S[n - 4])
        Z1V[n] = eop("vector", lambda h, n=n, sl=n % 4, xl=n % 3:
                     h.scalar_tensor_tensor(
            z1[:, sl, :], ops2[n % 2][:], 1.0, xC[:, xl, :],
            ALU.mult, ALU.add, accum_out=svc(SV_S1, n)))
        wt("scalar", Z1V[n])
        if n == 0:
            wt("scalar", CONST)
        SQ1[n] = eop("scalar", lambda h, n=n, sl=n % 4: h.activation(
            sqd[:], z1[:, sl, :], AF.Square, accum_out=svc(SV_S2, n)))

    def c_lnr(n):
        if not (0 <= n < NT):
            return
        eop("scalar", lambda h, n=n: h.activation(
            svc(SV_MU, n), svc(SV_S1, n), AF.Copy, scale=cstc(C_INV1024)))
        eop("scalar", lambda h, n=n: h.activation(
            svc(SV_MS, n), svc(SV_S2, n), AF.Identity,
            scale=cstc(C_INV1024), bias=cstc(C_LNEPS)))
        eop("scalar", lambda h, n=n: h.activation(
            svc(SV_VAR, n), svc(SV_MU, n), AF.Square))
        eop("scalar", lambda h, n=n: h.activation(
            svc(SV_L, n), svc(SV_VAR, n), AF.Ln, scale=cstc(C_NEG1),
            bias=svc(SV_MS, n)))
        LNR1[n] = eop("scalar", lambda h, n=n: h.activation(
            svc(SV_RSTD, n), svc(SV_L, n), AF.Exp, scale=cstc(C_NHALF)))
        eop("scalar", lambda h, n=n: h.activation(
            svc(SV_MUN, n), svc(SV_S1, n), AF.Copy, scale=cstc(C_NINV1024)))
        LNB1V[n] = eop("scalar", lambda h, n=n: h.activation(
            svc(SV_LNB, n), svc(SV_MUN, n), AF.Copy, scale=svc(SV_RSTD, n)))

    def c_quant(n):
        if not (0 <= n < NT):
            return
        if n >= 4:
            wt("vector", Z2V[n - 4])
        H1S[n] = eop("vector", lambda h, n=n, sl=n % 4, zl=n % 4:
                     h.scalar_tensor_tensor(
            h1[:, sl, :], z1[:, zl, :], svc(SV_MU, n),
            rstd_bc(n), ALU.subtract, ALU.mult))
        eop("vector", lambda h, n=n, sl=n % 4: h.tensor_reduce(
            svc(SV_AM, n), h1[:, sl, :], AXX, ALU.max,
            apply_absolute_value=True))
        eop("vector", lambda h, n=n: h.tensor_scalar(
            lsc(n), svc(SV_AM, n), 1e-12, 1.0 / 127.0,
            ALU.max, ALU.mult))
        eop("vector", lambda h, n=n: h.reciprocal(
            svc(SV_RLS, n), lsc(n)))
        eop("vector", lambda h, n=n, sl=n % 4: h.tensor_scalar(
            qf[:], h1[:, sl, :], svc(SV_RLS, n), None, ALU.mult))
        QIV[n] = eop("vector", lambda h: h.tensor_scalar(
            qi[:], qf[:], 127.0, -127.0, ALU.min, ALU.max))
        if n >= 2:
            wt("vector", TQF[n - 2])
        QBB[n] = eop("vector", lambda h, sl=n % 2: h.tensor_copy(
            qltm[:, sl, :], qi[:]))
        wt("sync", QBB[n])
        if n >= 4:
            wt("sync", FFM[n - 4])
        TQF[n] = op("sync", lambda h, sl=n % 4, ql=n % 2: h.dma_start_transpose(
            qlT[:, sl, :, :], qltm[:, ql, :]), "tqf%d" % (n % 2), 16)

    def rstd_bc(n):
        return svc(SV_RSTD, n).broadcast_to([128, D])

    def rstd2_bc(n):
        return svc(SV_RSTD2, n).broadcast_to([128, D])

    def c_bc():
        wt("tensor", cst_state["RS2"])
        wt("tensor", CONST)
        last = None
        for hf in range(2):
            last = eop("tensor", lambda h, hf=hf: h.matmul(
                ffs2[1][:, hf * 512:(hf + 1) * 512], ones_f[:],
                rs_row[:, hf * 512:(hf + 1) * 512],
                start=True, stop=True, skip_group_check=True))
        wt("scalar", last)
        cst_state["RSBC"] = eop("scalar", lambda h: h.activation(
            rsbc[:], ffs2[1][:], AF.Copy))

    def c_ffn(n):
        if not (0 <= n < NT):
            return
        wt("tensor", TQF[n])
        if n == 0:
            wt("tensor", TQRL[7])
        if n == 1:
            wt("tensor", cst_state["RSBC"])
        if n >= 2:
            wt("tensor", EP0[n - 2])
        last = None
        for dch in range(8):
            for hf in range(2):
                last = eop("tensor", lambda h, sl=n % 4, fl=n % 2, hf=hf,
                           d2=dch: h.matmul(
                    ffs2[fl][:, hf * 512:(hf + 1) * 512],
                    qlT[:, sl, d2, :], rhq[:, d2, hf * 512:(hf + 1) * 512],
                    start=(d2 == 0), stop=(d2 == 7),
                    skip_group_check=True))
        FFM[n] = last

    def c_ln2a(n):
        if not (0 <= n < NT):
            return
        wt("scalar", FFM[n])
        if n >= 2:
            wt("scalar", T2P[n - 2])
        EP0[n] = eop("scalar", lambda h, n=n, fl=n % 2: h.activation(
            t1[:, n % 2, :], ffs2[fl][:], AF.Copy, scale=lsc(n)))
    def c_t2(n):
        if not (0 <= n < NT):
            return
        wt("vector", EP0[n])
        if n == 0:
            wt("vector", cst_state["RSBC"])
        T2P[n] = eop("vector", lambda h, n=n, sl=n % 2:
                     h.scalar_tensor_tensor(
            t2[:, sl, :], t1[:, sl, :], 1.0, rsbc[:], ALU.mult, ALU.mult,
            accum_out=svc(SV_S3, n)))

    def c_ln2z(n):
        if not (0 <= n < NT):
            return
        if n >= 3:
            wt("vector", FIN[n - 3])
        Z2V[n] = eop("vector", lambda h, n=n, sl=n % 3, hl=n % 4, tl=n % 2:
                     h.tensor_tensor(
            z2[:, sl, :], t2[:, tl, :], h1[:, hl, :], ALU.add))
        wt("scalar", Z2V[n])
        SQ2[n] = eop("scalar", lambda h, n=n, sl=n % 3: h.activation(
            sqd[:], z2[:, sl, :], AF.Square, accum_out=svc(SV_S4, n)))

    def c_ln2b(n):
        if not (0 <= n < NT):
            return
        eop("scalar", lambda h, n=n: h.activation(
            svc(SV_MU2, n), svc(SV_S3, n), AF.Copy, scale=cstc(C_INV1024)))
        eop("scalar", lambda h, n=n: h.activation(
            svc(SV_MS2, n), svc(SV_S4, n), AF.Identity,
            scale=cstc(C_INV1024), bias=cstc(C_LNEPS)))
        eop("scalar", lambda h, n=n: h.activation(
            svc(SV_VAR2, n), svc(SV_MU2, n), AF.Square))
        eop("scalar", lambda h, n=n: h.activation(
            svc(SV_L2, n), svc(SV_VAR2, n), AF.Ln, scale=cstc(C_NEG1),
            bias=svc(SV_MS2, n)))
        LNR2[n] = eop("scalar", lambda h, n=n: h.activation(
            svc(SV_RSTD2, n), svc(SV_L2, n), AF.Exp, scale=cstc(C_NHALF)))
        eop("scalar", lambda h, n=n: h.activation(
            svc(SV_MUN, n), svc(SV_S3, n), AF.Copy, scale=cstc(C_NINV1024)))
        LNB2V[n] = eop("scalar", lambda h, n=n: h.activation(
            svc(SV_LNB2, n), svc(SV_MUN, n), AF.Copy,
            scale=svc(SV_RSTD2, n)))

    def c_fin(n):
        if not (0 <= n < NT):
            return
        wt("vector", LNR2[n])
        if n >= 2:
            wt("vector", STO[n - 2])
        FIN[n] = eop("vector", lambda h, n=n, sl=n % 2, zl=n % 3:
                     h.scalar_tensor_tensor(
            outb[:, sl, :], z2[:, zl, :], svc(SV_MU2, n),
            rstd2_bc(n), ALU.subtract, ALU.mult))
        wt("sync", FIN[n])
        STO[n] = op("sync", lambda h, n=n, sl=n % 2: h.dma_start(
            out=out_d[n * 128:(n + 1) * 128, :], in_=outb[:, sl, :]),
            "sto%d" % (n % 2), 16)

    wt("sync", PHIQ[NT - 1])
    wt("sync", RQPL[NT - 1])
    wt("sync", EXQ[NT - 1])
    wt("vector", RQPL[NT - 1])
    wt("vector", EXQ[NT - 1])
    c_rhs_load(0)
    c_rhs_load(1)
    for oc in range(8):
        c_rhs_quant(oc)
        c_rhs_load(oc + 2)
    cst_state["RS2"] = c_rs_row()

    barrier(endB)

    c_xload(0)
    for j in range(NT + 8):
        c_xload(j + 1)
        c_ln1(j - 1)
        c_opm(j)
        if j == 1:
            c_bc()
        c_lnr(j - 2)
        c_quant(j - 3)
        c_ffn(j - 4)
        c_ln2a(j - 5)
        c_t2(j - 5)
        c_ln2z(j - 6)
        c_ln2b(j - 7)
        c_fin(j - 8)

    wt("sync", STO[NT - 2])
    wt("sync", STO[NT - 1])

    with nc.Block() as block:
        def runner(eng):
            def run(h):
                for item in streams[eng]:
                    if item[0] == "wait":
                        h.wait_ge(sem[item[1]], item[2])
                    else:
                        inst = item[1](h)
                        if item[2] is not None:
                            inst.then_inc(sem[item[2]], item[3])
            return run
        block.sync(runner("sync"))
        block.gpsimd(runner("gpsimd"))
        block.vector(runner("vector"))
        block.scalar(runner("scalar"))
        block.tensor(runner("tensor"))

    esC.close()
    esBC.close()
    es.close()
    return nc


_NC_CACHE = None


def kernel(L0_attn_x, L0_attn_Wq, L0_attn_Wk, L0_attn_Wv, L0_attn_Wo,
           L0_ffn_rhs):
    global _NC_CACHE
    x = np.ascontiguousarray(np.asarray(L0_attn_x, dtype=np.float32))
    wq = np.ascontiguousarray(np.asarray(L0_attn_Wq, dtype=np.float32).T)
    wk = np.ascontiguousarray(np.asarray(L0_attn_Wk, dtype=np.float32).T)
    wv = np.ascontiguousarray(np.asarray(L0_attn_Wv, dtype=np.float32).T)
    wo = np.ascontiguousarray(np.asarray(L0_attn_Wo, dtype=np.float32).T)
    rhsT = np.ascontiguousarray(np.asarray(L0_ffn_rhs, dtype=np.float32).T)

    inv = (10000.0 ** (-np.arange(HF, dtype=np.float64) / HF)).astype(np.float32)
    tri = np.ascontiguousarray(
        np.tril(np.ones((128, 128), dtype=np.float32)).T)
    cstv = np.zeros((128, 8), dtype=np.float32)
    cstv[:, 0] = 1.0 / 1024.0
    cstv[:, 1] = 1e-5
    cstv[:, 2] = -0.5
    cstv[:, 3] = -1.0
    cstv[:, 4] = -1.0 / 1024.0
    ones1 = np.ones((1, 128), dtype=np.float32)

    if _NC_CACHE is None:
        _NC_CACHE = build_nc()
    nc = _NC_CACHE

    in_maps = []
    for c in range(NCORES):
        b, qi2 = divmod(c, 4)
        t0 = qi2 * TC
        xs = np.ascontiguousarray(x[b, t0:t0 + TC, :])
        xTs = np.ascontiguousarray(xs.T)
        pos = (t0 + np.arange(TC)).astype(np.float32)
        ang = pos[:, None] * inv[None, :]
        cosv = np.cos(ang).astype(np.float32)
        sinv = np.sin(ang).astype(np.float32)
        cs = np.ascontiguousarray(np.concatenate(
            [cosv, sinv, cosv], axis=1))
        pw = np.zeros((128, NCORES), dtype=np.float32)
        for r in range(4 * b, 4 * b + qi2):
            pw[:, r] = 1.0
        in_maps.append({
            "x": xs, "xT": xTs, "wqT": wq, "wkT": wk, "wvT": wv, "woT": wo,
            "rhsT": rhsT, "cs": cs, "tri": tri, "pw": pw, "cstd": cstv,
            "ones1": ones1,
        })

    res = run_bass_kernel_spmd(nc, in_maps, core_ids=list(range(NCORES)))
    out = np.empty((2, 8192, 1024), dtype=np.float32)
    for c in range(NCORES):
        b, qi2 = divmod(c, 4)
        out[b, qi2 * TC:(qi2 + 1) * TC, :] = res.results[c]["out"]
    return out


# version 32
# speedup vs baseline: 1.0234x; 1.0093x over previous
import numpy as np
import concourse.bass as bass
import concourse.mybir as mybir
from concourse.bass_utils import run_bass_kernel_spmd
from contextlib import ExitStack

F32 = mybir.dt.float32
BF16 = mybir.dt.bfloat16
I32 = mybir.dt.int32
ALU = mybir.AluOpType
AF = mybir.ActivationFunctionType
AXX = mybir.AxisListType.X

NCORES = 8
TC = 2048
NT = 16
D = 1024
H = 16
HF = 32
NP = 8
SW = 65 * NP

ENGS = ("sync", "gpsimd", "vector", "scalar", "tensor")


def build_nc():
    nc = bass.Bass("TRN2", target_bir_lowering=False, debug=False)

    x_d = nc.dram_tensor("x", [TC, D], F32, kind="ExternalInput").ap()
    xT_d = nc.dram_tensor("xT", [D, TC], F32, kind="ExternalInput").ap()
    wq_d = nc.dram_tensor("wqT", [D, D], F32, kind="ExternalInput").ap()
    wk_d = nc.dram_tensor("wkT", [D, D], F32, kind="ExternalInput").ap()
    wv_d = nc.dram_tensor("wvT", [D, D], F32, kind="ExternalInput").ap()
    wo_d = nc.dram_tensor("woT", [D, D], F32, kind="ExternalInput").ap()
    rhsT_d = nc.dram_tensor("rhsT", [D, D], F32, kind="ExternalInput").ap()
    cs_d = nc.dram_tensor("cs", [TC, 3 * HF], F32, kind="ExternalInput").ap()
    tri_d = nc.dram_tensor("tri", [128, 128], F32, kind="ExternalInput").ap()
    pw_d = nc.dram_tensor("pw", [128, NCORES], F32, kind="ExternalInput").ap()
    cst_d = nc.dram_tensor("cstd", [128, 8], F32, kind="ExternalInput").ap()
    ones_d = nc.dram_tensor("ones1", [1, 128], F32, kind="ExternalInput").ap()
    out_d = nc.dram_tensor("out", [TC, D], F32, kind="ExternalOutput").ap()
    cc_in = nc.dram_tensor("cc_in", [128, SW], F32)
    cc_out = nc.dram_tensor("cc_out", [NCORES * 128, SW], F32)
    rs_sc = nc.dram_tensor("rs_sc", [128, 8], F32)
    v65_d = nc.dram_tensor("v65sp", [128, NT * 65 * H], BF16)

    streams = {e: [] for e in ENGS}
    cnt = {}
    es = ExitStack()
    sem = {}
    ENG_SEM = {"tensor": "pe", "vector": "vec", "scalar": "scl",
               "gpsimd": "pl", "sync": "syq"}

    def S(name):
        if name not in sem:
            sem[name] = es.enter_context(nc.semaphore("s_" + name))
            cnt[name] = 0
        return sem[name]

    def op(eng, fn, sm=None, inc=0):
        if sm is not None:
            S(sm)
            cnt[sm] += inc
            streams[eng].append(("op", fn, sm, inc))
            return (sm, cnt[sm])
        streams[eng].append(("op", fn, None, 0))
        return None

    def eop(eng, fn):
        own = ENG_SEM[eng]
        if eng != "tensor" and cnt.get(own, 0) > 0:
            streams[eng].append(("wait", own, cnt[own]))
        return op(eng, fn, own, 1)

    def wt(eng, spec):
        if spec is None:
            import sys
            print("WT-NONE-SKIP on", eng, file=sys.stderr)
            return
        sm, val = spec
        if val > 0:
            streams[eng].append(("wait", sm, val))

    def marks_now():
        return dict(cnt)

    def barrier(marks):
        for e in ENGS:
            own = ENG_SEM[e]
            for sm, val in marks.items():
                if sm != own and sm not in ("cc", "ccin") and val > 0:
                    streams[e].append(("wait", sm, val))

    Sg = es.enter_context(nc.sbuf_tensor("Sg", [128, SW], F32))
    tri_b = es.enter_context(nc.sbuf_tensor("tri_b", [128, 128], BF16))
    pw_t = es.enter_context(nc.sbuf_tensor("pw_t", [128, NCORES], F32))
    cst = es.enter_context(nc.sbuf_tensor("cst", [128, 8], F32))
    ones_f = es.enter_context(nc.sbuf_tensor("ones_f", [1, 128], F32))
    zb1 = es.enter_context(nc.sbuf_tensor("zb1", [1, 128], BF16))
    esAB = ExitStack()
    phk_fm = esAB.enter_context(nc.sbuf_tensor("phk_fm", [128, 8, TC], BF16))
    v65 = esAB.enter_context(nc.sbuf_tensor("v65", [128, 4, 65 * H], BF16))
    D_bf = esAB.enter_context(nc.sbuf_tensor("D_bf", [128, NT, SW], BF16))
    cs_t = esAB.enter_context(nc.sbuf_tensor("cs_t", [128, NT, 3 * HF], F32))
    wqb = esAB.enter_context(nc.sbuf_tensor("wqb", [128, 8, D], BF16))
    xtB = esAB.enter_context(nc.sbuf_tensor("xtB", [128, 2, 8, 128], BF16))

    C_INV1024, C_LNEPS, C_NHALF, C_NEG1, C_NINV1024 = 0, 1, 2, 3, 4

    def cstc(i):
        return cst[:, i:i + 1]

    TRI = op("gpsimd", lambda h: h.dma_start(out=tri_b[:], in_=tri_d),
             "tri", 16)
    op("sync", lambda h: h.dma_start(
        out=cs_t[:], in_=cs_d.rearrange("(n p) i -> p n i", p=128)),
        "const", 16)
    op("sync", lambda h: h.dma_start(out=pw_t[:], in_=pw_d), "const", 16)
    op("sync", lambda h: h.dma_start(out=cst[:], in_=cst_d), "const", 16)
    CONST = op("sync", lambda h: h.dma_start(out=ones_f[:], in_=ones_d),
               "const", 16)

    eop("vector", lambda h: h.memset(zb1[:], 0.0))
    eop("vector", lambda h: h.memset(zb5[:], 0.0))
    eop("vector", lambda h: h.memset(Sg[:], 0.0))
    V_INIT = eop("vector", lambda h: h.memset(
        v65[:].rearrange("p n (g w) -> p n g w", w=65)[:, :, :, 64:65], 1.0))

    esA = ExitStack()
    wkv = esA.enter_context(nc.sbuf_tensor("wkv", [128, 16, D], BF16))
    xtA = esA.enter_context(nc.sbuf_tensor("xtA", [128, 2, 8, 128], BF16))
    ktm = esA.enter_context(nc.sbuf_tensor("ktm", [128, 2, D], BF16))
    mA = esA.enter_context(nc.sbuf_tensor("mA", [128, 4, 512], F32))
    rk = esA.enter_context(nc.sbuf_tensor("rk", [128, 2, D], F32))
    mnk = esA.enter_context(nc.sbuf_tensor("mnk", [128, 1, D], F32))
    ek = esA.enter_context(nc.sbuf_tensor("ek", [128, 2, D], BF16))
    Stot = esA.enter_context(nc.sbuf_tensor("Stot", [128, SW], F32))
    zb5 = esA.enter_context(nc.sbuf_tensor("zb5", [1, 512], BF16))
    pjk = esA.enter_context(nc.psum_tensor("pjk", [128, D], F32))
    pjv = esA.enter_context(nc.psum_tensor("pjv", [128, D], F32))
    dps = [esA.enter_context(nc.psum_tensor(f"dps{i}", [128, 512], F32))
           for i in range(2)]

    WK = op("gpsimd", lambda h: h.dma_start(
        out=wkv[:, 0:8, :], in_=wk_d.rearrange("(c p) o -> p c o", p=128)),
        "wkv", 16)
    WKV = op("gpsimd", lambda h: h.dma_start(
        out=wkv[:, 8:16, :], in_=wv_d.rearrange("(c p) o -> p c o", p=128)),
        "wkv", 16)

    wt("tensor", V_INIT)
    for i in range(2):
        eop("tensor", lambda h, i=i: h.matmul(
            dps[i][:, 0:260], zb1[:], zb5[:, 0:260],
            start=True, stop=False, skip_group_check=True))

    PJ, VE, RMK, PHIK, DMM, DEV, TQK, XTA, EXK = \
        [None] * NT, [None] * NT, [None] * NT, [None] * NT, [None] * NT, \
        [None] * NT, [None] * NT, [None] * NT, [None] * NT
    MINKL = [None] * NT
    RKPL = [None] * NT
    PJKL = [None] * NT

    def a_xload(n):
        if not (0 <= n < NT):
            return
        if n >= 2:
            wt("gpsimd", PJ[n - 2])
        XTA[n] = op("gpsimd", lambda h, n=n, sl=n % 2: h.dma_start(
            out=xtA[:, sl, :, :],
            in_=xT_d[:, n * 128:(n + 1) * 128].rearrange(
                "(c p) t -> p c t", p=128)), "xta%d" % (n % 2), 16)

    def a_kvproj(n):
        if not (0 <= n < NT):
            return
        wt("tensor", XTA[n])
        if n == 0:
            wt("tensor", WK)
        if n >= 1:
            wt("tensor", RMK[n - 1])
        last = None
        for cch in range(8):
            for hf in range(2):
                last = eop("tensor", lambda h, sl=n % 2, hf=hf, c=cch:
                           h.matmul(
                    pjk[:, hf * 512:(hf + 1) * 512],
                    xtA[:, sl, c, :], wkv[:, c, hf * 512:(hf + 1) * 512],
                    start=(c == 0), stop=(c == 7),
                    skip_group_check=True))
        PJKL[n] = last
        if n == 0:
            wt("tensor", WKV)
        if n >= 1:
            wt("tensor", VE[n - 1])
        for cch in range(8):
            for hf in range(2):
                last = eop("tensor", lambda h, sl=n % 2, hf=hf, c=cch:
                           h.matmul(
                    pjv[:, hf * 512:(hf + 1) * 512],
                    xtA[:, sl, c, :], wkv[:, 8 + c, hf * 512:(hf + 1) * 512],
                    start=(c == 0), stop=(c == 7),
                    skip_group_check=True))
        PJ[n] = last

    def a_D(n):
        if not (0 <= n < NT):
            return
        wt("tensor", PHIK[n])
        wt("tensor", VE[n])
        if n >= 1:
            wt("tensor", DEV[n - 1])
        last = None
        for p in range(NP):
            bank, cb = p // 4, (p % 4) * 65
            for par in range(2):
                hh = 2 * p + par
                last = eop("tensor", lambda h, sl=n % 2, par=par, hh=hh,
                           bank=bank, cb=cb, n=n: h.matmul(
                    dps[bank][par * 64:(par + 1) * 64, cb:cb + 65],
                    ktm[:, sl, hh * 64:(hh + 1) * 64],
                    v65[:, n % 4, hh * 65:(hh + 1) * 65],
                    start=False,
                    stop=(n == NT - 1 and hh == 2 * NP - 1),
                    tile_position=(0, par * 64), skip_group_check=True))
        DMM[n] = last

    def a_m(n):
        if not (0 <= n < NT):
            return
        wt("vector", PJKL[n])
        if n == 0:
            wt("vector", CONST)
        if n >= 1:
            wt("vector", RKPL[n - 1])
        xx = pjk[:].rearrange("p (g t w) -> p t g w", t=2, w=32)
        last = None
        for half in range(2):
            tab = cs_t[:, n, half * 32:half * 32 + 64].rearrange(
                "p (t o i) -> p t o i", t=2, o=1).broadcast_to(
                [128, 2, 16, 32])
            last = eop("vector", lambda h, half=half, tab=tab:
                       h.tensor_tensor(
                mA[:, 2 * half:2 * half + 2, :].rearrange(
                    "p t (g i) -> p t g i", i=32),
                xx, tab, ALU.mult))
        RMK[n] = last

    def a_r(n):
        if not (0 <= n < NT):
            return
        wt("gpsimd", RMK[n])
        if n >= 2:
            wt("gpsimd", PHIK[n - 2])
        eop("gpsimd", lambda h, sl=n % 2: h.tensor_tensor(
            rk[:, sl, :].rearrange("p (g w) -> p g w", w=64)[:, :, 0:32],
            mA[:, 0, :].rearrange("p (g i) -> p g i", i=32),
            mA[:, 1, :].rearrange("p (g i) -> p g i", i=32), ALU.subtract))
        RKPL[n] = eop("gpsimd", lambda h, sl=n % 2: h.tensor_tensor(
            rk[:, sl, :].rearrange("p (g w) -> p g w", w=64)[:, :, 32:64],
            mA[:, 2, :].rearrange("p (g i) -> p g i", i=32),
            mA[:, 3, :].rearrange("p (g i) -> p g i", i=32), ALU.add))

    def a_exp(n):
        if not (0 <= n < NT):
            return
        wt("scalar", RKPL[n])
        if n == 0:
            wt("scalar", CONST)
        eop("scalar", lambda h, sl=n % 2: h.activation(
            mnk[:, 0, :], rk[:, sl, :], AF.Relu, scale=cstc(C_NEG1)))
        if n >= 2:
            wt("scalar", PHIK[n - 2])
        EXK[n] = eop("scalar", lambda h, sl=n % 2: h.activation(
            ek[:, sl, :], mnk[:, 0, :], AF.Exp, scale=cstc(C_NEG1)))

    def a_stt(n):
        if not (0 <= n < NT):
            return
        wt("vector", EXK[n])
        if n >= 2:
            wt("vector", TQK[n - 2])
        if n >= 2:
            wt("vector", DMM[n - 2])
        PHIK[n] = eop("vector", lambda h, sl=n % 2: h.scalar_tensor_tensor(
            ktm[:, sl, :], rk[:, sl, :], 0.0, ek[:, sl, :],
            ALU.max, ALU.add))

    SPV = [None] * NT

    def a_vevac(n):
        if not (0 <= n < NT):
            return
        wt("scalar", PJ[n])
        if n == 0:
            wt("scalar", V_INIT)
        if n >= 4:
            wt("scalar", SPV[n - 4])
            wt("scalar", DMM[n - 4])
        VE[n] = eop("scalar", lambda h, n=n, sl=n % 4: h.activation(
            v65[:, sl, :].rearrange("p (g w) -> p g w", w=65)[:, :, 0:64],
            pjv[:].rearrange("p (g w) -> p g w", w=64), AF.Copy))
        wt("scalar", VE[n])
        SPV[n] = op("scalar", lambda h, n=n, sl=n % 4: h.dma_start(
            out=v65_d.ap()[:, n * 1040:(n + 1) * 1040], in_=v65[:, sl, :]),
            "spv%d" % (n % 2), 16)

    def a_tq(n):
        if not (0 <= n < NT):
            return
        wt("sync", PHIK[n])
        TQK[n] = op("sync", lambda h, n=n, sl=n % 2: h.dma_start_transpose(
            phk_fm[:, :, n * 128:(n + 1) * 128], ktm[:, sl, :]),
            "tqk%d" % (n % 2), 16)

    def a_devac(n):
        if not (0 <= n < NT):
            return
        wt("vector", DMM[n])
        eop("vector", lambda h, n=n: h.tensor_copy(
            D_bf[:, n, 0:260], dps[0][:, 0:260]))
        DEV[n] = eop("vector", lambda h, n=n: h.tensor_copy(
            D_bf[:, n, 260:520], dps[1][:, 0:260]))

    XTB = [None] * NT
    WQ = None

    def b_xload(c):
        if not (0 <= c < NT):
            return
        if XTB[c] is not None:
            return
        if c >= 2:
            wt("gpsimd", PJQL[c - 2])
        XTB[c] = op("gpsimd", lambda h, c=c, sl=c % 2: h.dma_start(
            out=xtB[:, sl, :, :],
            in_=xT_d[:, c * 128:(c + 1) * 128].rearrange(
                "(c p) t -> p c t", p=128)), "xtb%d" % (c % 2), 16)

    a_xload(0)
    for j in range(NT + 2):
        a_xload(j + 1)
        a_kvproj(j)
        a_m(j)
        a_stt(j - 2)
        a_tq(j - 2)
        a_D(j - 2)
        a_devac(j - 2)
        a_r(j)
        a_vevac(j)
        a_exp(j - 1)
        if 8 <= j <= 15:
            cch = j - 8
            WQ = op("gpsimd", lambda h, c=cch: h.dma_start(
                out=wqb[:, c, :],
                in_=wq_d[c * 128:(c + 1) * 128, :]), "wq", 16)
        if j == 14:
            b_xload(0)
        if j == 15:
            b_xload(1)

    eop("vector", lambda h: h.tensor_copy(Stot[:, 0:260], dps[0][:, 0:260]))
    V_STOT = eop("vector", lambda h: h.tensor_copy(
        Stot[:, 260:520], dps[1][:, 0:260]))
    wt("gpsimd", V_STOT)
    CCW = op("gpsimd", lambda h: h.dma_start(out=cc_in.ap(), in_=Stot[:]),
             "ccin", 16)
    wt("gpsimd", CCW)
    CC = op("gpsimd", lambda h: h.collective_compute(
        "AllGather", ALU.bypass, replica_groups=[list(range(NCORES))],
        ins=[cc_in.ap().opt()], outs=[cc_out.ap().opt()]), "cc", 1)

    endA = marks_now()
    esA.close()
    esBC = ExitStack()
    YT = esBC.enter_context(nc.sbuf_tensor("YT", [128, 8, TC], BF16, side="right"))
    wo_b = esBC.enter_context(nc.sbuf_tensor("wo_b", [128, 8, D], BF16, side="right"))

    esB = ExitStack()
    Am = esB.enter_context(nc.sbuf_tensor("Am", [128, 4, 2048], BF16))
    qfm = esB.enter_context(nc.sbuf_tensor("qfm", [128, 4, 8, 128], BF16))
    Ytm = esB.enter_context(nc.sbuf_tensor("Ytm", [128, 2, D], BF16))
    S65 = esB.enter_context(nc.sbuf_tensor("S65", [128, 2, SW], BF16))
    denf = esB.enter_context(nc.sbuf_tensor("denf", [128, H], F32))
    rden = esB.enter_context(nc.sbuf_tensor("rden", [128, H], F32))
    denm = esB.enter_context(nc.sbuf_tensor("denm", [128, H], F32))
    gsl4 = esB.enter_context(nc.sbuf_tensor("gsl4", [128, 5, SW], F32))
    mB = esB.enter_context(nc.sbuf_tensor("mB", [128, 4, 512], F32))
    pjs = esB.enter_context(nc.sbuf_tensor("pjs", [128, 2, D], F32))
    rq = esB.enter_context(nc.sbuf_tensor("rq", [128, 2, D], F32))
    mnq = esB.enter_context(nc.sbuf_tensor("mnq", [128, 1, D], F32))
    eq = esB.enter_context(nc.sbuf_tensor("eq", [128, 1, D], BF16))
    qtm = esB.enter_context(nc.sbuf_tensor("qtm", [128, 2, D], BF16))
    pjq = esB.enter_context(nc.psum_tensor("pjq", [128, D], F32))
    apsm = esB.enter_context(nc.psum_tensor("apsm", [128, D], F32))
    yps = [esB.enter_context(nc.psum_tensor(f"yps{i}", [128, 512], F32))
           for i in range(3)]

    barrier(endA)

    WO = op("gpsimd", lambda h: h.dma_start(
        out=wo_b[:], in_=wo_d.rearrange("(c p) o -> p c o", p=128)), "wo", 16)

    SLW, SGP = [None] * NCORES, [None] * NCORES

    def b_slabs():
        ranks = [0, 1, 2, 4, 5, 6]
        wt("scalar", CC)
        for i, r in enumerate(ranks[:5]):
            SLW[r] = op("scalar", lambda h, i=i, r=r: h.dma_start(
                out=gsl4[:, i, :],
                in_=cc_out.ap()[r * 128:(r + 1) * 128, :]),
                "slab%d" % (i % 4), 16)
        for i, r in enumerate(ranks[:5]):
            wt("vector", SLW[r])
            SGP[r] = eop("vector", lambda h, i=i, r=r: h.scalar_tensor_tensor(
                Sg[:], gsl4[:, i, :], pw_t[:, r:r + 1], Sg[:],
                ALU.mult, ALU.add))
        wt("scalar", SGP[ranks[0]])
        SLW[6] = op("scalar", lambda h: h.dma_start(
            out=gsl4[:, 0, :], in_=cc_out.ap()[6 * 128:7 * 128, :]),
            "slab0", 16)
        wt("vector", SLW[6])
        SGP[6] = eop("vector", lambda h: h.scalar_tensor_tensor(
            Sg[:], gsl4[:, 0, :], pw_t[:, 6:7], Sg[:],
            ALU.mult, ALU.add))
        SGP[NCORES - 1] = SGP[6]

    S65R = [None] * NT

    def am_off(hh):
        p, par = hh // 2, hh % 2
        return (p // 4) * 1024 + par * 512 + (p % 4) * 128

    PJQ, RMQ, PHIQ, TQQ, YMM, YEV, TQY, EXQ = \
        [None] * NT, [None] * NT, [None] * NT, [None] * NT, [None] * NT, \
        [None] * NT, [None] * NT, [None] * NT
    MKh = [[None, None] for _ in range(NT)]
    AMMh = [[None, None] for _ in range(NT)]
    MINQL = [None] * NT
    RQPL = [None] * NT
    PJQL = [None] * NT
    PJS = [None] * NT

    def b_qproj(c):
        if not (0 <= c < NT):
            return
        wt("tensor", XTB[c])
        if c == 0:
            wt("tensor", WQ)
        if c >= 1:
            wt("tensor", PJS[c - 1])
        last = None
        for cch in range(8):
            for hf in range(2):
                last = eop("tensor", lambda h, sl=c % 2, hf=hf, c2=cch:
                           h.matmul(
                    pjq[:, hf * 512:(hf + 1) * 512],
                    xtB[:, sl, c2, :], wqb[:, c2, hf * 512:(hf + 1) * 512],
                    start=(c2 == 0), stop=(c2 == 7),
                    skip_group_check=True))
        PJQL[c] = last
        wt("scalar", PJQL[c])
        if c >= 2:
            wt("scalar", RMQ[c - 2])
        PJS[c] = eop("scalar", lambda h, sl=c % 2: h.activation(
            pjs[:, sl, :], pjq[:], AF.Copy))

    def b_m(c):
        if not (0 <= c < NT):
            return
        wt("vector", PJS[c])
        if c >= 1:
            wt("vector", RQPL[c - 1])
        xx = pjs[:, c % 2, :].rearrange("p (g t w) -> p t g w", t=2, w=32)
        last = None
        for half in range(2):
            tab = cs_t[:, c, half * 32:half * 32 + 64].rearrange(
                "p (t o i) -> p t o i", t=2, o=1).broadcast_to(
                [128, 2, 16, 32])
            last = eop("vector", lambda h, half=half, tab=tab:
                       h.tensor_tensor(
                mB[:, 2 * half:2 * half + 2, :].rearrange(
                    "p t (g i) -> p t g i", i=32),
                xx, tab, ALU.mult))
        RMQ[c] = last

    def b_r(c):
        if not (0 <= c < NT):
            return
        wt("gpsimd", RMQ[c])
        if c >= 2:
            wt("gpsimd", PHIQ[c - 2])
        eop("gpsimd", lambda h, sl=c % 2: h.tensor_tensor(
            rq[:, sl, :].rearrange("p (g w) -> p g w", w=64)[:, :, 0:32],
            mB[:, 0, :].rearrange("p (g i) -> p g i", i=32),
            mB[:, 1, :].rearrange("p (g i) -> p g i", i=32), ALU.subtract))
        RQPL[c] = eop("gpsimd", lambda h, sl=c % 2: h.tensor_tensor(
            rq[:, sl, :].rearrange("p (g w) -> p g w", w=64)[:, :, 32:64],
            mB[:, 2, :].rearrange("p (g i) -> p g i", i=32),
            mB[:, 3, :].rearrange("p (g i) -> p g i", i=32), ALU.add))

    def b_exp(c):
        if not (0 <= c < NT):
            return
        wt("scalar", RQPL[c])
        eop("scalar", lambda h, sl=c % 2: h.activation(
            mnq[:, 0, :], rq[:, sl, :], AF.Relu, scale=cstc(C_NEG1)))
        if c >= 1:
            wt("scalar", PHIQ[c - 1])
        EXQ[c] = eop("scalar", lambda h: h.activation(
            eq[:, 0, :], mnq[:, 0, :], AF.Exp, scale=cstc(C_NEG1)))

    def b_stt(c):
        if not (0 <= c < NT):
            return
        wt("vector", EXQ[c])
        if c >= 2:
            wt("vector", TQQ[c - 2])
        PHIQ[c] = eop("vector", lambda h, sl=c % 2: h.scalar_tensor_tensor(
            qtm[:, sl, :], rq[:, sl, :], 0.0, eq[:, 0, :],
            ALU.max, ALU.add))

    def b_tqq(c):
        if not (0 <= c < NT):
            return
        wt("sync", PHIQ[c])
        if c >= 4:
            wt("sync", YMM[c - 4])
        TQQ[c] = op("sync", lambda h, sl=c % 4, ql=c % 2: h.dma_start_transpose(
            qfm[:, sl, :, :], qtm[:, ql, :]), "tqq%d" % (c % 2), 16)

    def b_amm(c, half):
        if not (0 <= c < NT):
            return
        if half == 0:
            wt("tensor", TQK[c])
            wt("tensor", TQQ[c])
            if c >= 1:
                wt("tensor", MKh[c - 1][1])
        else:
            wt("tensor", MKh[c][0])
        last = None
        for pp in range(4):
            p = half * 4 + pp
            for par in range(2):
                last = eop("tensor", lambda h, sl=c % 4, p=p, pp=pp, par=par,
                           c=c: h.matmul(
                    apsm[:, par * 512 + pp * 128:par * 512 + pp * 128 + 128],
                    phk_fm[par * 64:(par + 1) * 64, p,
                           c * 128:(c + 1) * 128],
                    qfm[par * 64:(par + 1) * 64, sl, p, :],
                    start=(pp == 0), stop=(pp == 3),
                    skip_group_check=True))
        AMMh[c][half] = last

    def b_mask(c, half):
        if not (0 <= c < NT):
            return
        wt("vector", AMMh[c][half])
        if c == 0 and half == 0:
            wt("vector", TRI)
        if c >= 4:
            wt("vector", YMM[c - 4])
        tribc = tri_b[:].rearrange("p (o i) -> p o i", o=1).broadcast_to(
            [128, 8, 128])
        MKh[c][half] = eop("vector", lambda h, sl=c % 4, half=half,
                           tribc=tribc: h.tensor_tensor(
            Am[:, sl, half * 1024:half * 1024 + 1024].rearrange(
                "p (g i) -> p g i", i=128),
            apsm[:].rearrange("p (g i) -> p g i", i=128),
            tribc, ALU.mult))

    def b_s65(c):
        if not (1 <= c < NT):
            return
        if c == 1:
            wt("gpsimd", SGP[NCORES - 1])
        if c >= 2:
            wt("gpsimd", YMM[c - 2])
        S65R[c] = eop("gpsimd", lambda h, c=c, d2=c % 2: h.tensor_tensor(
            S65[:, d2, :], Sg[:], D_bf[:, c - 1, :], ALU.add))

    VB = [None] * NT

    def b_vload(c):
        if not (0 <= c < NT):
            return
        if c >= 4:
            wt("sync", YMM[c - 4])
        VB[c] = op("sync", lambda h, c=c, sl=c % 4: h.dma_start(
            out=v65[:, sl, :], in_=v65_d.ap()[:, c * 1040:(c + 1) * 1040]),
            "vb%d" % (c % 2), 16)

    def b_y(c):
        if not (0 <= c < NT):
            return
        wt("tensor", MKh[c][1])
        wt("tensor", S65R[c])
        wt("tensor", VB[c])
        if c >= 1:
            wt("tensor", YEV[c - 1])
        last = None
        for p in range(NP):
            g, off = p // 3, (p % 3) * 130
            for par in range(2):
                hh = 2 * p + par
                st = (p % 3 == 0 and par == 0)
                eop("tensor", lambda h, sl=c % 4, hh=hh, g=g, off=off,
                    par=par, c=c, st=st: h.matmul(
                    yps[g][:, off + par * 65: off + par * 65 + 65],
                    Am[:, sl, am_off(hh):am_off(hh) + 128],
                    v65[:, c % 4, hh * 65:(hh + 1) * 65],
                    start=st, stop=False, skip_group_check=True))
                sp2 = ((p % 3 == 2 or p == NP - 1) and par == 1)
                last = eop("tensor", lambda h, sl=c % 4, d2=c % 2, p=p,
                           par=par, g=g,
                           off=off, sp2=sp2: h.matmul(
                    yps[g][:, off + par * 65: off + par * 65 + 65],
                    qfm[par * 64:(par + 1) * 64, sl, p, :],
                    S65[par * 64:(par + 1) * 64, d2, p * 65:(p + 1) * 65],
                    start=False, stop=sp2, skip_group_check=True))
        YMM[c] = last

    def b_yev(c):
        if not (0 <= c < NT):
            return
        wt("vector", YMM[c])
        if c >= 2:
            wt("vector", TQY[c - 2])
        for g in range(3):
            ng = 3 if g < 2 else 2
            src2 = yps[g][:, 0:130 * ng].rearrange(
                "p (a b) -> p a b", b=65)[:, :, 64:65]
            eop("vector", lambda h, g=g, ng=ng, src2=src2: h.tensor_copy(
                denf[:, 6 * g: 6 * g + 2 * ng],
                src2.rearrange("p a b -> p (a b)")))
        eop("vector", lambda h: h.tensor_single_scalar(
            denm[:], denf[:], 1e-6, ALU.max))
        eop("vector", lambda h: h.reciprocal(rden[:], denm[:]))
        last = None
        for g in range(3):
            ng = 3 if g < 2 else 2
            ysrc = yps[g][:, 0:130 * ng].rearrange(
                "p (a b) -> p a b", b=65)[:, :, 0:64]
            ydst = Ytm[:, c % 2, 128 * 3 * g: 128 * 3 * g + 128 * ng].rearrange(
                "p (a b) -> p a b", b=64)
            rsrc = rden[:, 6 * g:6 * g + 2 * ng].rearrange(
                "p (a o) -> p a o", o=1).broadcast_to([128, 2 * ng, 64])
            last = eop("vector", lambda h, ysrc=ysrc, ydst=ydst,
                       rsrc=rsrc: h.tensor_tensor(
                ydst, ysrc, rsrc, ALU.mult))
        YEV[c] = last

    def b_tqy(c):
        if not (0 <= c < NT):
            return
        wt("sync", YEV[c])
        TQY[c] = op("sync", lambda h, c=c, sl=c % 2: h.dma_start_transpose(
            YT[:, :, c * 128:(c + 1) * 128], Ytm[:, sl, :]),
            "tqy%d" % (c % 2), 16)

    b_xload(0)
    for j in range(NT + 5):
        b_xload(j + 1)
        b_stt(j - 3)
        b_tqq(j - 3)
        b_amm(j - 4, 0)
        b_qproj(j)
        b_mask(j - 4, 0)
        b_amm(j - 4, 1)
        b_m(j - 1)
        b_mask(j - 4, 1)
        b_vload(j - 3)
        if j == 5:
            b_slabs()
            wt("vector", SGP[NCORES - 1])
            S65R[0] = eop("vector", lambda h: h.tensor_copy(
                S65[:, 0, :], Sg[:]))
        b_y(j - 5)
        b_s65(j - 4)
        b_yev(j - 5)
        b_r(j - 1)
        b_exp(j - 2)
        b_tqy(j - 5)

    endB = marks_now()
    esB.close()
    esAB.close()

    esC = ExitStack()
    rhq = esC.enter_context(nc.sbuf_tensor("rhq", [128, 8, D], BF16))
    qrT = esC.enter_context(nc.sbuf_tensor("qrT", [128, 2, D], BF16))
    rhf = esC.enter_context(nc.sbuf_tensor("rhf", [128, 2, D], F32))
    rsq = esC.enter_context(nc.sbuf_tensor("rsq", [128, 8], F32))
    rrq = esC.enter_context(nc.sbuf_tensor("rrq", [128, 8], F32))
    rs_row = esC.enter_context(nc.sbuf_tensor("rs_row", [1, D], F32))
    rsbc = esC.enter_context(nc.sbuf_tensor("rsbc", [128, D], F32))
    qfr = esC.enter_context(nc.sbuf_tensor("qfr", [128, D], F32))
    qir = esC.enter_context(nc.sbuf_tensor("qir", [128, D], I32))
    xC = esC.enter_context(nc.sbuf_tensor("xC", [128, 3, D], F32))
    z1 = esC.enter_context(nc.sbuf_tensor("z1", [128, 4, D], F32))
    h1 = esC.enter_context(nc.sbuf_tensor("h1", [128, 4, D], F32))
    z2 = esC.enter_context(nc.sbuf_tensor("z2", [128, 3, D], F32))
    sqd = esC.enter_context(nc.sbuf_tensor("sqd", [128, D], F32))
    t1 = esC.enter_context(nc.sbuf_tensor("t1", [128, 2, D], F32))
    t2 = esC.enter_context(nc.sbuf_tensor("t2", [128, 2, D], F32))
    qf = esC.enter_context(nc.sbuf_tensor("qf", [128, D], F32))
    qi = esC.enter_context(nc.sbuf_tensor("qi", [128, D], I32))
    qltm = esC.enter_context(nc.sbuf_tensor("qltm", [128, 2, D], BF16))
    qlT = esC.enter_context(nc.sbuf_tensor("qlT", [128, 4, 8, 128], BF16))
    lsv = esC.enter_context(nc.sbuf_tensor("lsv", [128, 4], F32))
    outb = esC.enter_context(nc.sbuf_tensor("outb", [128, 2, D], F32))
    sv = esC.enter_context(nc.sbuf_tensor("sv", [128, 60], F32))
    ops2 = [esC.enter_context(nc.psum_tensor(f"ops2_{i}", [128, D], F32))
            for i in range(2)]
    ffs2 = [esC.enter_context(nc.psum_tensor(f"ffs{i}", [128, D], F32))
            for i in range(2)]

    (SV_MU, SV_MS, SV_VAR, SV_L, SV_RSTD, SV_AM, SV_LS, SV_RLS,
     SV_S1, SV_S2, SV_MU2, SV_MS2, SV_VAR2, SV_L2, SV_RSTD2,
     SV_S3, SV_S4, SV_LNB, SV_LNB2, SV_MUN) = range(20)

    def svc(i, n):
        j = i + 20 * (n % 3)
        return sv[:, j:j + 1]

    def lsc(n):
        return lsv[:, (n % 4):(n % 4) + 1]

    RHL, QRT = [None] * 8, [None] * 8
    TQRL = [None] * 8

    def c_rhs_load(oc):
        if not (0 <= oc < 8):
            return
        if oc >= 2:
            wt("sync", QRT[oc - 2])
        RHL[oc] = op("sync", lambda h, oc=oc: h.dma_start(
            out=rhf[:, oc % 2, :], in_=rhsT_d[oc * 128:(oc + 1) * 128, :]),
            "rhf%d" % (oc % 2), 16)

    def c_rhs_quant(oc):
        if not (0 <= oc < 8):
            return
        wt("vector", RHL[oc])
        eop("vector", lambda h, oc=oc: h.tensor_reduce(
            rsq[:, oc:oc + 1], rhf[:, oc % 2, :], AXX,
            ALU.max, apply_absolute_value=True))
        eop("vector", lambda h, oc=oc: h.tensor_scalar(
            rsq[:, oc:oc + 1], rsq[:, oc:oc + 1], 1e-12, 1.0 / 127.0,
            ALU.max, ALU.mult))
        eop("vector", lambda h, oc=oc: h.reciprocal(
            rrq[:, oc:oc + 1], rsq[:, oc:oc + 1]))
        eop("vector", lambda h, oc=oc: h.tensor_tensor(
            qfr[:], rhf[:, oc % 2, :],
            rrq[:, oc:oc + 1].broadcast_to([128, D]), ALU.mult))
        eop("vector", lambda h: h.tensor_scalar(
            qir[:], qfr[:], 127.0, -127.0, ALU.min, ALU.max))
        if oc >= 2:
            wt("vector", TQRL[oc - 2])
        QRT[oc] = eop("vector", lambda h, oc=oc: h.tensor_copy(
            qrT[:, oc % 2, :], qir[:]))
        wt("sync", QRT[oc])
        TQRL[oc] = op("sync", lambda h, oc=oc: h.dma_start_transpose(
            rhq[:, :, oc * 128:(oc + 1) * 128], qrT[:, oc % 2, :]), "tqr", 16)

    def c_rs_row():
        RS1 = op("sync", lambda h: h.dma_start(out=rs_sc.ap(), in_=rsq[:]),
                 "rs1", 16)
        wt("sync", RS1)
        return op("sync", lambda h: h.dma_start(
            out=rs_row[:].rearrange("o (c p) -> o c p", p=128),
            in_=rs_sc.ap().rearrange("p (o c) -> o c p", o=1),
            allow_slow_non_contiguous=True), "rs2", 16)

    OPM, Z1V, SQ1, LNV1, LNR1, H1S = ([None] * NT for _ in range(6))
    LNB1V, LNB2V, SC3, T2P = ([None] * NT for _ in range(4))
    AMV, QIV, QBB, TQF, FFM = ([None] * NT for _ in range(5))
    EP0, Z2V, SQ2, LNV2, LNR2, FIN, STO, XCW = ([None] * NT for _ in range(8))
    cst_state = {}

    def c_xload(n):
        if not (0 <= n < NT):
            return
        if n >= 3:
            wt("sync", Z1V[n - 3])
        XCW[n] = op("sync", lambda h, n=n, sl=n % 3: h.dma_start(
            out=xC[:, sl, :], in_=x_d[n * 128:(n + 1) * 128, :]),
            "xc%d" % (n % 3), 16)

    def c_opm(n):
        if not (0 <= n < NT):
            return
        if n == 0:
            wt("tensor", WO)
        if n >= 2:
            wt("tensor", Z1V[n - 2])
        last = None
        for e in range(8):
            for hf in range(2):
                last = eop("tensor", lambda h, n=n, hf=hf, e=e: h.matmul(
                    ops2[n % 2][:, hf * 512:(hf + 1) * 512],
                    YT[:, e, n * 128:(n + 1) * 128],
                    wo_b[:, e, hf * 512:(hf + 1) * 512],
                    start=(e == 0), stop=(e == 7),
                    skip_group_check=True))
        OPM[n] = last

    def c_ln1(n):
        if not (0 <= n < NT):
            return
        wt("vector", OPM[n])
        wt("vector", XCW[n])
        if n >= 4:
            wt("vector", H1S[n - 4])
        Z1V[n] = eop("vector", lambda h, n=n, sl=n % 4, xl=n % 3:
                     h.scalar_tensor_tensor(
            z1[:, sl, :], ops2[n % 2][:], 1.0, xC[:, xl, :],
            ALU.mult, ALU.add, accum_out=svc(SV_S1, n)))
        wt("scalar", Z1V[n])
        if n == 0:
            wt("scalar", CONST)
        SQ1[n] = eop("scalar", lambda h, n=n, sl=n % 4: h.activation(
            sqd[:], z1[:, sl, :], AF.Square, accum_out=svc(SV_S2, n)))

    def c_lnr(n):
        if not (0 <= n < NT):
            return
        eop("scalar", lambda h, n=n: h.activation(
            svc(SV_MU, n), svc(SV_S1, n), AF.Copy, scale=cstc(C_INV1024)))
        eop("scalar", lambda h, n=n: h.activation(
            svc(SV_MS, n), svc(SV_S2, n), AF.Identity,
            scale=cstc(C_INV1024), bias=cstc(C_LNEPS)))
        eop("scalar", lambda h, n=n: h.activation(
            svc(SV_VAR, n), svc(SV_MU, n), AF.Square))
        eop("scalar", lambda h, n=n: h.activation(
            svc(SV_L, n), svc(SV_VAR, n), AF.Ln, scale=cstc(C_NEG1),
            bias=svc(SV_MS, n)))
        LNR1[n] = eop("scalar", lambda h, n=n: h.activation(
            svc(SV_RSTD, n), svc(SV_L, n), AF.Exp, scale=cstc(C_NHALF)))

    def c_quant(n):
        if not (0 <= n < NT):
            return
        if n >= 4:
            wt("vector", Z2V[n - 4])
        H1S[n] = eop("vector", lambda h, n=n, sl=n % 4, zl=n % 4:
                     h.scalar_tensor_tensor(
            h1[:, sl, :], z1[:, zl, :], svc(SV_MU, n),
            rstd_bc(n), ALU.subtract, ALU.mult))
        eop("vector", lambda h, n=n, sl=n % 4: h.tensor_reduce(
            svc(SV_AM, n), h1[:, sl, :], AXX, ALU.max,
            apply_absolute_value=True))
        eop("vector", lambda h, n=n: h.tensor_scalar(
            lsc(n), svc(SV_AM, n), 1e-12, 1.0 / 127.0,
            ALU.max, ALU.mult))
        eop("vector", lambda h, n=n: h.reciprocal(
            svc(SV_RLS, n), lsc(n)))
        eop("vector", lambda h, n=n, sl=n % 4: h.tensor_scalar(
            qf[:], h1[:, sl, :], svc(SV_RLS, n), None, ALU.mult))
        QIV[n] = eop("vector", lambda h: h.tensor_scalar(
            qi[:], qf[:], 127.0, -127.0, ALU.min, ALU.max))
        if n >= 2:
            wt("vector", TQF[n - 2])
        QBB[n] = eop("vector", lambda h, sl=n % 2: h.tensor_copy(
            qltm[:, sl, :], qi[:]))
        wt("sync", QBB[n])
        if n >= 4:
            wt("sync", FFM[n - 4])
        TQF[n] = op("sync", lambda h, sl=n % 4, ql=n % 2: h.dma_start_transpose(
            qlT[:, sl, :, :], qltm[:, ql, :]), "tqf%d" % (n % 2), 16)

    def rstd_bc(n):
        return svc(SV_RSTD, n).broadcast_to([128, D])

    def rstd2_bc(n):
        return svc(SV_RSTD2, n).broadcast_to([128, D])

    def c_bc():
        wt("tensor", cst_state["RS2"])
        wt("tensor", CONST)
        last = None
        for hf in range(2):
            last = eop("tensor", lambda h, hf=hf: h.matmul(
                ffs2[1][:, hf * 512:(hf + 1) * 512], ones_f[:],
                rs_row[:, hf * 512:(hf + 1) * 512],
                start=True, stop=True, skip_group_check=True))
        wt("scalar", last)
        cst_state["RSBC"] = eop("scalar", lambda h: h.activation(
            rsbc[:], ffs2[1][:], AF.Copy))

    def c_ffn(n):
        if not (0 <= n < NT):
            return
        wt("tensor", TQF[n])
        if n == 0:
            wt("tensor", TQRL[7])
        if n == 1:
            wt("tensor", cst_state["RSBC"])
        if n >= 2:
            wt("tensor", EP0[n - 2])
        last = None
        for dch in range(8):
            for hf in range(2):
                last = eop("tensor", lambda h, sl=n % 4, fl=n % 2, hf=hf,
                           d2=dch: h.matmul(
                    ffs2[fl][:, hf * 512:(hf + 1) * 512],
                    qlT[:, sl, d2, :], rhq[:, d2, hf * 512:(hf + 1) * 512],
                    start=(d2 == 0), stop=(d2 == 7),
                    skip_group_check=True))
        FFM[n] = last

    def c_ln2a(n):
        if not (0 <= n < NT):
            return
        wt("scalar", FFM[n])
        if n >= 2:
            wt("scalar", T2P[n - 2])
        EP0[n] = eop("scalar", lambda h, n=n, fl=n % 2: h.activation(
            t1[:, n % 2, :], ffs2[fl][:], AF.Copy, scale=lsc(n)))
    def c_t2(n):
        if not (0 <= n < NT):
            return
        wt("vector", EP0[n])
        if n == 0:
            wt("vector", cst_state["RSBC"])
        T2P[n] = eop("vector", lambda h, n=n, sl=n % 2:
                     h.scalar_tensor_tensor(
            t2[:, sl, :], t1[:, sl, :], 1.0, rsbc[:], ALU.mult, ALU.mult,
            accum_out=svc(SV_S3, n)))

    def c_ln2z(n):
        if not (0 <= n < NT):
            return
        if n >= 3:
            wt("vector", FIN[n - 3])
        Z2V[n] = eop("vector", lambda h, n=n, sl=n % 3, hl=n % 4, tl=n % 2:
                     h.tensor_tensor(
            z2[:, sl, :], t2[:, tl, :], h1[:, hl, :], ALU.add))
        wt("scalar", Z2V[n])
        SQ2[n] = eop("scalar", lambda h, n=n, sl=n % 3: h.activation(
            sqd[:], z2[:, sl, :], AF.Square, accum_out=svc(SV_S4, n)))

    def c_ln2b(n):
        if not (0 <= n < NT):
            return
        eop("scalar", lambda h, n=n: h.activation(
            svc(SV_MU2, n), svc(SV_S3, n), AF.Copy, scale=cstc(C_INV1024)))
        eop("scalar", lambda h, n=n: h.activation(
            svc(SV_MS2, n), svc(SV_S4, n), AF.Identity,
            scale=cstc(C_INV1024), bias=cstc(C_LNEPS)))
        eop("scalar", lambda h, n=n: h.activation(
            svc(SV_VAR2, n), svc(SV_MU2, n), AF.Square))
        eop("scalar", lambda h, n=n: h.activation(
            svc(SV_L2, n), svc(SV_VAR2, n), AF.Ln, scale=cstc(C_NEG1),
            bias=svc(SV_MS2, n)))
        LNR2[n] = eop("scalar", lambda h, n=n: h.activation(
            svc(SV_RSTD2, n), svc(SV_L2, n), AF.Exp, scale=cstc(C_NHALF)))
        eop("scalar", lambda h, n=n: h.activation(
            svc(SV_MUN, n), svc(SV_S3, n), AF.Copy, scale=cstc(C_NINV1024)))
        LNB2V[n] = eop("scalar", lambda h, n=n: h.activation(
            svc(SV_LNB2, n), svc(SV_MUN, n), AF.Copy,
            scale=svc(SV_RSTD2, n)))

    def c_fin(n):
        if not (0 <= n < NT):
            return
        wt("vector", LNR2[n])
        if n >= 2:
            wt("vector", STO[n - 2])
        FIN[n] = eop("vector", lambda h, n=n, sl=n % 2, zl=n % 3:
                     h.scalar_tensor_tensor(
            outb[:, sl, :], z2[:, zl, :], svc(SV_MU2, n),
            rstd2_bc(n), ALU.subtract, ALU.mult))
        wt("sync", FIN[n])
        STO[n] = op("sync", lambda h, n=n, sl=n % 2: h.dma_start(
            out=out_d[n * 128:(n + 1) * 128, :], in_=outb[:, sl, :]),
            "sto%d" % (n % 2), 16)

    wt("sync", PHIQ[NT - 1])
    wt("sync", RQPL[NT - 1])
    wt("sync", EXQ[NT - 1])
    wt("vector", RQPL[NT - 1])
    wt("vector", EXQ[NT - 1])
    c_rhs_load(0)
    c_rhs_load(1)
    for oc in range(8):
        c_rhs_quant(oc)
        c_rhs_load(oc + 2)
    cst_state["RS2"] = c_rs_row()

    barrier(endB)

    c_xload(0)
    for j in range(NT + 8):
        c_xload(j + 1)
        c_ln1(j - 1)
        c_opm(j)
        if j == 1:
            c_bc()
        c_lnr(j - 2)
        c_quant(j - 3)
        c_ffn(j - 4)
        c_ln2a(j - 5)
        c_t2(j - 5)
        c_ln2z(j - 6)
        c_ln2b(j - 7)
        c_fin(j - 8)

    wt("sync", STO[NT - 2])
    wt("sync", STO[NT - 1])

    with nc.Block() as block:
        def runner(eng):
            def run(h):
                for item in streams[eng]:
                    if item[0] == "wait":
                        h.wait_ge(sem[item[1]], item[2])
                    else:
                        inst = item[1](h)
                        if item[2] is not None:
                            inst.then_inc(sem[item[2]], item[3])
            return run
        block.sync(runner("sync"))
        block.gpsimd(runner("gpsimd"))
        block.vector(runner("vector"))
        block.scalar(runner("scalar"))
        block.tensor(runner("tensor"))

    esC.close()
    esBC.close()
    es.close()
    return nc


_NC_CACHE = None


def kernel(L0_attn_x, L0_attn_Wq, L0_attn_Wk, L0_attn_Wv, L0_attn_Wo,
           L0_ffn_rhs):
    global _NC_CACHE
    x = np.ascontiguousarray(np.asarray(L0_attn_x, dtype=np.float32))
    wq = np.ascontiguousarray(np.asarray(L0_attn_Wq, dtype=np.float32).T)
    wk = np.ascontiguousarray(np.asarray(L0_attn_Wk, dtype=np.float32).T)
    wv = np.ascontiguousarray(np.asarray(L0_attn_Wv, dtype=np.float32).T)
    wo = np.ascontiguousarray(np.asarray(L0_attn_Wo, dtype=np.float32).T)
    rhsT = np.ascontiguousarray(np.asarray(L0_ffn_rhs, dtype=np.float32).T)

    inv = (10000.0 ** (-np.arange(HF, dtype=np.float64) / HF)).astype(np.float32)
    tri = np.ascontiguousarray(
        np.tril(np.ones((128, 128), dtype=np.float32)).T)
    cstv = np.zeros((128, 8), dtype=np.float32)
    cstv[:, 0] = 1.0 / 1024.0
    cstv[:, 1] = 1e-5
    cstv[:, 2] = -0.5
    cstv[:, 3] = -1.0
    cstv[:, 4] = -1.0 / 1024.0
    ones1 = np.ones((1, 128), dtype=np.float32)

    if _NC_CACHE is None:
        _NC_CACHE = build_nc()
    nc = _NC_CACHE

    in_maps = []
    for c in range(NCORES):
        b, qi2 = divmod(c, 4)
        t0 = qi2 * TC
        xs = np.ascontiguousarray(x[b, t0:t0 + TC, :])
        xTs = np.ascontiguousarray(xs.T)
        pos = (t0 + np.arange(TC)).astype(np.float32)
        ang = pos[:, None] * inv[None, :]
        cosv = np.cos(ang).astype(np.float32)
        sinv = np.sin(ang).astype(np.float32)
        cs = np.ascontiguousarray(np.concatenate(
            [cosv, sinv, cosv], axis=1))
        pw = np.zeros((128, NCORES), dtype=np.float32)
        for r in range(4 * b, 4 * b + qi2):
            pw[:, r] = 1.0
        in_maps.append({
            "x": xs, "xT": xTs, "wqT": wq, "wkT": wk, "wvT": wv, "woT": wo,
            "rhsT": rhsT, "cs": cs, "tri": tri, "pw": pw, "cstd": cstv,
            "ones1": ones1,
        })

    res = run_bass_kernel_spmd(nc, in_maps, core_ids=list(range(NCORES)))
    out = np.empty((2, 8192, 1024), dtype=np.float32)
    for c in range(NCORES):
        b, qi2 = divmod(c, 4)
        out[b, qi2 * TC:(qi2 + 1) * TC, :] = res.results[c]["out"]
    return out


# version 33
# speedup vs baseline: 1.0336x; 1.0100x over previous
import numpy as np
import concourse.bass as bass
import concourse.mybir as mybir
from concourse.bass_utils import run_bass_kernel_spmd
from contextlib import ExitStack

F32 = mybir.dt.float32
BF16 = mybir.dt.bfloat16
I32 = mybir.dt.int32
ALU = mybir.AluOpType
AF = mybir.ActivationFunctionType
AXX = mybir.AxisListType.X

NCORES = 8
TC = 2048
NT = 16
D = 1024
H = 16
HF = 32
NP = 8
SW = 65 * NP

ENGS = ("sync", "gpsimd", "vector", "scalar", "tensor")


def build_nc():
    nc = bass.Bass("TRN2", target_bir_lowering=False, debug=False)

    x_d = nc.dram_tensor("x", [TC, D], F32, kind="ExternalInput").ap()
    xT_d = nc.dram_tensor("xT", [D, TC], F32, kind="ExternalInput").ap()
    wq_d = nc.dram_tensor("wqT", [D, D], F32, kind="ExternalInput").ap()
    wk_d = nc.dram_tensor("wkT", [D, D], F32, kind="ExternalInput").ap()
    wv_d = nc.dram_tensor("wvT", [D, D], F32, kind="ExternalInput").ap()
    wo_d = nc.dram_tensor("woT", [D, D], F32, kind="ExternalInput").ap()
    rhsT_d = nc.dram_tensor("rhsT", [D, D], F32, kind="ExternalInput").ap()
    cs_d = nc.dram_tensor("cs", [TC, 3 * HF], F32, kind="ExternalInput").ap()
    tri_d = nc.dram_tensor("tri", [128, 128], F32, kind="ExternalInput").ap()
    pw_d = nc.dram_tensor("pw", [128, NCORES], F32, kind="ExternalInput").ap()
    cst_d = nc.dram_tensor("cstd", [128, 8], F32, kind="ExternalInput").ap()
    ones_d = nc.dram_tensor("ones1", [1, 128], F32, kind="ExternalInput").ap()
    out_d = nc.dram_tensor("out", [TC, D], F32, kind="ExternalOutput").ap()
    cc_in = nc.dram_tensor("cc_in", [128, SW], F32)
    cc_out = nc.dram_tensor("cc_out", [NCORES * 128, SW], F32)
    rs_sc = nc.dram_tensor("rs_sc", [128, 8], F32)
    v65_d = nc.dram_tensor("v65sp", [128, NT * 65 * H], BF16)

    streams = {e: [] for e in ENGS}
    cnt = {}
    es = ExitStack()
    sem = {}
    ENG_SEM = {"tensor": "pe", "vector": "vec", "scalar": "scl",
               "gpsimd": "pl", "sync": "syq"}

    def S(name):
        if name not in sem:
            sem[name] = es.enter_context(nc.semaphore("s_" + name))
            cnt[name] = 0
        return sem[name]

    def op(eng, fn, sm=None, inc=0):
        if sm is not None:
            S(sm)
            cnt[sm] += inc
            streams[eng].append(("op", fn, sm, inc))
            return (sm, cnt[sm])
        streams[eng].append(("op", fn, None, 0))
        return None

    def eop(eng, fn):
        own = ENG_SEM[eng]
        if eng != "tensor" and cnt.get(own, 0) > 0:
            streams[eng].append(("wait", own, cnt[own]))
        return op(eng, fn, own, 1)

    def wt(eng, spec):
        if spec is None:
            import sys
            print("WT-NONE-SKIP on", eng, file=sys.stderr)
            return
        sm, val = spec
        if val > 0:
            streams[eng].append(("wait", sm, val))

    def marks_now():
        return dict(cnt)

    def barrier(marks):
        for e in ENGS:
            own = ENG_SEM[e]
            for sm, val in marks.items():
                if sm != own and sm not in ("cc", "ccin") and val > 0:
                    streams[e].append(("wait", sm, val))

    Sg = es.enter_context(nc.sbuf_tensor("Sg", [128, SW], F32))
    tri_b = es.enter_context(nc.sbuf_tensor("tri_b", [128, 128], BF16))
    pw_t = es.enter_context(nc.sbuf_tensor("pw_t", [128, NCORES], F32))
    cst = es.enter_context(nc.sbuf_tensor("cst", [128, 8], F32))
    ones_f = es.enter_context(nc.sbuf_tensor("ones_f", [1, 128], F32))
    zb1 = es.enter_context(nc.sbuf_tensor("zb1", [1, 128], BF16))
    esAB = ExitStack()
    phk_fm = esAB.enter_context(nc.sbuf_tensor("phk_fm", [128, 8, TC], BF16))
    v65 = esAB.enter_context(nc.sbuf_tensor("v65", [128, 4, 65 * H], BF16))
    D_bf = esAB.enter_context(nc.sbuf_tensor("D_bf", [128, NT, SW], BF16))
    cs_t = esAB.enter_context(nc.sbuf_tensor("cs_t", [128, NT, 3 * HF], F32))
    wqb = esAB.enter_context(nc.sbuf_tensor("wqb", [128, 8, D], BF16))
    xtB = esAB.enter_context(nc.sbuf_tensor("xtB", [128, 2, 8, 128], BF16))

    C_INV1024, C_LNEPS, C_NHALF, C_NEG1, C_NINV1024 = 0, 1, 2, 3, 4

    def cstc(i):
        return cst[:, i:i + 1]

    TRI = op("gpsimd", lambda h: h.dma_start(out=tri_b[:], in_=tri_d),
             "tri", 16)
    op("sync", lambda h: h.dma_start(
        out=cs_t[:], in_=cs_d.rearrange("(n p) i -> p n i", p=128)),
        "const", 16)
    op("sync", lambda h: h.dma_start(out=pw_t[:], in_=pw_d), "const", 16)
    op("sync", lambda h: h.dma_start(out=cst[:], in_=cst_d), "const", 16)
    CONST = op("sync", lambda h: h.dma_start(out=ones_f[:], in_=ones_d),
               "const", 16)

    eop("vector", lambda h: h.memset(zb1[:], 0.0))
    eop("vector", lambda h: h.memset(zb5[:], 0.0))
    eop("vector", lambda h: h.memset(Sg[:], 0.0))
    V_INIT = eop("vector", lambda h: h.memset(
        v65[:].rearrange("p n (g w) -> p n g w", w=65)[:, :, :, 64:65], 1.0))

    esA = ExitStack()
    wkv = esA.enter_context(nc.sbuf_tensor("wkv", [128, 16, D], BF16))
    xtA = esA.enter_context(nc.sbuf_tensor("xtA", [128, 2, 8, 128], BF16))
    ktm = esA.enter_context(nc.sbuf_tensor("ktm", [128, 2, D], BF16))
    mA = esA.enter_context(nc.sbuf_tensor("mA", [128, 4, 512], F32))
    rk = esA.enter_context(nc.sbuf_tensor("rk", [128, 2, D], F32))
    mnk = esA.enter_context(nc.sbuf_tensor("mnk", [128, 1, D], F32))
    ek = esA.enter_context(nc.sbuf_tensor("ek", [128, 2, D], BF16))
    Stot = esA.enter_context(nc.sbuf_tensor("Stot", [128, SW], F32))
    zb5 = esA.enter_context(nc.sbuf_tensor("zb5", [1, 512], BF16))
    pjk = esA.enter_context(nc.psum_tensor("pjk", [128, D], F32))
    pjv = esA.enter_context(nc.psum_tensor("pjv", [128, D], F32))
    dps = [esA.enter_context(nc.psum_tensor(f"dps{i}", [128, 512], F32))
           for i in range(2)]

    WK = op("gpsimd", lambda h: h.dma_start(
        out=wkv[:, 0:8, :], in_=wk_d.rearrange("(c p) o -> p c o", p=128)),
        "wkv", 16)
    WKV = op("gpsimd", lambda h: h.dma_start(
        out=wkv[:, 8:16, :], in_=wv_d.rearrange("(c p) o -> p c o", p=128)),
        "wkv", 16)

    wt("tensor", V_INIT)
    for i in range(2):
        eop("tensor", lambda h, i=i: h.matmul(
            dps[i][:, 0:260], zb1[:], zb5[:, 0:260],
            start=True, stop=False, skip_group_check=True))

    PJ, VE, RMK, PHIK, DMM, DEV, TQK, XTA, EXK = \
        [None] * NT, [None] * NT, [None] * NT, [None] * NT, [None] * NT, \
        [None] * NT, [None] * NT, [None] * NT, [None] * NT
    MINKL = [None] * NT
    RKPL = [None] * NT
    PJKL = [None] * NT

    def a_xload(n):
        if not (0 <= n < NT):
            return
        if n >= 2:
            wt("gpsimd", PJ[n - 2])
        XTA[n] = op("gpsimd", lambda h, n=n, sl=n % 2: h.dma_start(
            out=xtA[:, sl, :, :],
            in_=xT_d[:, n * 128:(n + 1) * 128].rearrange(
                "(c p) t -> p c t", p=128)), "xta%d" % (n % 2), 16)

    def a_kvproj(n):
        if not (0 <= n < NT):
            return
        wt("tensor", XTA[n])
        if n == 0:
            wt("tensor", WK)
        if n >= 1:
            wt("tensor", RMK[n - 1])
        last = None
        for cch in range(8):
            for hf in range(2):
                last = eop("tensor", lambda h, sl=n % 2, hf=hf, c=cch:
                           h.matmul(
                    pjk[:, hf * 512:(hf + 1) * 512],
                    xtA[:, sl, c, :], wkv[:, c, hf * 512:(hf + 1) * 512],
                    start=(c == 0), stop=(c == 7),
                    skip_group_check=True))
        PJKL[n] = last
        if n == 0:
            wt("tensor", WKV)
        if n >= 1:
            wt("tensor", VE[n - 1])
        for cch in range(8):
            for hf in range(2):
                last = eop("tensor", lambda h, sl=n % 2, hf=hf, c=cch:
                           h.matmul(
                    pjv[:, hf * 512:(hf + 1) * 512],
                    xtA[:, sl, c, :], wkv[:, 8 + c, hf * 512:(hf + 1) * 512],
                    start=(c == 0), stop=(c == 7),
                    skip_group_check=True))
        PJ[n] = last

    def a_D(n):
        if not (0 <= n < NT):
            return
        wt("tensor", PHIK[n])
        wt("tensor", VE[n])
        if n >= 1:
            wt("tensor", DEV[n - 1])
        last = None
        for p in range(NP):
            bank, cb = p // 4, (p % 4) * 65
            for par in range(2):
                hh = 2 * p + par
                last = eop("tensor", lambda h, sl=n % 2, par=par, hh=hh,
                           bank=bank, cb=cb, n=n: h.matmul(
                    dps[bank][par * 64:(par + 1) * 64, cb:cb + 65],
                    ktm[:, sl, hh * 64:(hh + 1) * 64],
                    v65[:, n % 4, hh * 65:(hh + 1) * 65],
                    start=False,
                    stop=(n == NT - 1 and hh == 2 * NP - 1),
                    tile_position=(0, par * 64), skip_group_check=True))
        DMM[n] = last

    def a_m(n):
        if not (0 <= n < NT):
            return
        wt("vector", PJKL[n])
        if n == 0:
            wt("vector", CONST)
        if n >= 1:
            wt("vector", RKPL[n - 1])
        xx = pjk[:].rearrange("p (g t w) -> p t g w", t=2, w=32)
        last = None
        for half in range(2):
            tab = cs_t[:, n, half * 32:half * 32 + 64].rearrange(
                "p (t o i) -> p t o i", t=2, o=1).broadcast_to(
                [128, 2, 16, 32])
            last = eop("vector", lambda h, half=half, tab=tab:
                       h.tensor_tensor(
                mA[:, 2 * half:2 * half + 2, :].rearrange(
                    "p t (g i) -> p t g i", i=32),
                xx, tab, ALU.mult))
        RMK[n] = last

    def a_r(n):
        if not (0 <= n < NT):
            return
        wt("gpsimd", RMK[n])
        if n >= 2:
            wt("gpsimd", PHIK[n - 2])
        eop("gpsimd", lambda h, sl=n % 2: h.tensor_tensor(
            rk[:, sl, :].rearrange("p (g w) -> p g w", w=64)[:, :, 0:32],
            mA[:, 0, :].rearrange("p (g i) -> p g i", i=32),
            mA[:, 1, :].rearrange("p (g i) -> p g i", i=32), ALU.subtract))
        RKPL[n] = eop("gpsimd", lambda h, sl=n % 2: h.tensor_tensor(
            rk[:, sl, :].rearrange("p (g w) -> p g w", w=64)[:, :, 32:64],
            mA[:, 2, :].rearrange("p (g i) -> p g i", i=32),
            mA[:, 3, :].rearrange("p (g i) -> p g i", i=32), ALU.add))

    def a_exp(n):
        if not (0 <= n < NT):
            return
        wt("scalar", RKPL[n])
        if n == 0:
            wt("scalar", CONST)
        eop("scalar", lambda h, sl=n % 2: h.activation(
            mnk[:, 0, :], rk[:, sl, :], AF.Relu, scale=cstc(C_NEG1)))
        if n >= 2:
            wt("scalar", PHIK[n - 2])
        EXK[n] = eop("scalar", lambda h, sl=n % 2: h.activation(
            ek[:, sl, :], mnk[:, 0, :], AF.Exp, scale=cstc(C_NEG1)))

    def a_stt(n):
        if not (0 <= n < NT):
            return
        wt("vector", EXK[n])
        if n >= 2:
            wt("vector", TQK[n - 2])
        if n >= 2:
            wt("vector", DMM[n - 2])
        PHIK[n] = eop("vector", lambda h, sl=n % 2: h.scalar_tensor_tensor(
            ktm[:, sl, :], rk[:, sl, :], 0.0, ek[:, sl, :],
            ALU.max, ALU.add))

    SPV = [None] * NT

    def a_vevac(n):
        if not (0 <= n < NT):
            return
        wt("scalar", PJ[n])
        if n == 0:
            wt("scalar", V_INIT)
        if n >= 4:
            wt("scalar", SPV[n - 4])
            wt("scalar", DMM[n - 4])
        VE[n] = eop("scalar", lambda h, n=n, sl=n % 4: h.activation(
            v65[:, sl, :].rearrange("p (g w) -> p g w", w=65)[:, :, 0:64],
            pjv[:].rearrange("p (g w) -> p g w", w=64), AF.Copy))
        wt("scalar", VE[n])
        SPV[n] = op("scalar", lambda h, n=n, sl=n % 4: h.dma_start(
            out=v65_d.ap()[:, n * 1040:(n + 1) * 1040], in_=v65[:, sl, :]),
            "spv%d" % (n % 2), 16)

    def a_tq(n):
        if not (0 <= n < NT):
            return
        wt("sync", PHIK[n])
        TQK[n] = op("sync", lambda h, n=n, sl=n % 2: h.dma_start_transpose(
            phk_fm[:, :, n * 128:(n + 1) * 128], ktm[:, sl, :]),
            "tqk%d" % (n % 2), 16)

    def a_devac(n):
        if not (0 <= n < NT):
            return
        wt("vector", DMM[n])
        eop("vector", lambda h, n=n: h.tensor_copy(
            D_bf[:, n, 0:260], dps[0][:, 0:260]))
        DEV[n] = eop("vector", lambda h, n=n: h.tensor_copy(
            D_bf[:, n, 260:520], dps[1][:, 0:260]))

    XTB = [None] * NT
    WQ = None

    def b_xload(c):
        if not (0 <= c < NT):
            return
        if XTB[c] is not None:
            return
        if c >= 2:
            wt("gpsimd", PJQL[c - 2])
        XTB[c] = op("gpsimd", lambda h, c=c, sl=c % 2: h.dma_start(
            out=xtB[:, sl, :, :],
            in_=xT_d[:, c * 128:(c + 1) * 128].rearrange(
                "(c p) t -> p c t", p=128)), "xtb%d" % (c % 2), 16)

    a_xload(0)
    for j in range(NT + 2):
        a_xload(j + 1)
        a_kvproj(j)
        a_m(j)
        a_stt(j - 2)
        a_tq(j - 2)
        a_D(j - 2)
        a_devac(j - 2)
        a_r(j)
        a_vevac(j)
        a_exp(j - 1)
        if 8 <= j <= 15:
            cch = j - 8
            WQ = op("gpsimd", lambda h, c=cch: h.dma_start(
                out=wqb[:, c, :],
                in_=wq_d[c * 128:(c + 1) * 128, :]), "wq", 16)
        if j == 14:
            b_xload(0)
        if j == 15:
            b_xload(1)

    eop("vector", lambda h: h.tensor_copy(Stot[:, 0:260], dps[0][:, 0:260]))
    V_STOT = eop("vector", lambda h: h.tensor_copy(
        Stot[:, 260:520], dps[1][:, 0:260]))
    wt("gpsimd", V_STOT)
    CCW = op("gpsimd", lambda h: h.dma_start(out=cc_in.ap(), in_=Stot[:]),
             "ccin", 16)
    wt("gpsimd", CCW)
    CC = op("gpsimd", lambda h: h.collective_compute(
        "AllGather", ALU.bypass, replica_groups=[list(range(NCORES))],
        ins=[cc_in.ap().opt()], outs=[cc_out.ap().opt()]), "cc", 1)

    endA = marks_now()
    esA.close()
    esBC = ExitStack()
    YT = esBC.enter_context(nc.sbuf_tensor("YT", [128, 8, TC], BF16, side="right"))
    wo_b = esBC.enter_context(nc.sbuf_tensor("wo_b", [128, 8, D], BF16, side="right"))

    esB = ExitStack()
    Am = esB.enter_context(nc.sbuf_tensor("Am", [128, 4, 2048], BF16))
    qfm = esB.enter_context(nc.sbuf_tensor("qfm", [128, 4, 8, 128], BF16))
    Ytm = esB.enter_context(nc.sbuf_tensor("Ytm", [128, 2, D], BF16))
    S65 = esB.enter_context(nc.sbuf_tensor("S65", [128, 2, SW], BF16))
    denf = esB.enter_context(nc.sbuf_tensor("denf", [128, H], F32))
    rden = esB.enter_context(nc.sbuf_tensor("rden", [128, H], F32))
    denm = esB.enter_context(nc.sbuf_tensor("denm", [128, H], F32))
    gsl4 = esB.enter_context(nc.sbuf_tensor("gsl4", [128, 5, SW], F32))
    mB = esB.enter_context(nc.sbuf_tensor("mB", [128, 4, 512], F32))
    pjs = esB.enter_context(nc.sbuf_tensor("pjs", [128, 2, D], F32))
    rq = esB.enter_context(nc.sbuf_tensor("rq", [128, 2, D], F32))
    mnq = esB.enter_context(nc.sbuf_tensor("mnq", [128, 1, D], F32))
    eq = esB.enter_context(nc.sbuf_tensor("eq", [128, 1, D], BF16))
    qtm = esB.enter_context(nc.sbuf_tensor("qtm", [128, 2, D], BF16))
    pjq = esB.enter_context(nc.psum_tensor("pjq", [128, D], F32))
    apsm = esB.enter_context(nc.psum_tensor("apsm", [128, D], F32))
    yps = [esB.enter_context(nc.psum_tensor(f"yps{i}", [128, 512], F32))
           for i in range(3)]

    barrier(endA)

    WO = op("gpsimd", lambda h: h.dma_start(
        out=wo_b[:], in_=wo_d.rearrange("(c p) o -> p c o", p=128)), "wo", 16)

    SLW, SGP = [None] * NCORES, [None] * NCORES

    def b_slabs():
        ranks = [0, 1, 2, 4, 5, 6]
        wt("scalar", CC)
        for i, r in enumerate(ranks[:5]):
            SLW[r] = op("scalar", lambda h, i=i, r=r: h.dma_start(
                out=gsl4[:, i, :],
                in_=cc_out.ap()[r * 128:(r + 1) * 128, :]),
                "slab%d" % (i % 4), 16)
        for i, r in enumerate(ranks[:5]):
            wt("vector", SLW[r])
            SGP[r] = eop("vector", lambda h, i=i, r=r: h.scalar_tensor_tensor(
                Sg[:], gsl4[:, i, :], pw_t[:, r:r + 1], Sg[:],
                ALU.mult, ALU.add))
        wt("scalar", SGP[ranks[0]])
        SLW[6] = op("scalar", lambda h: h.dma_start(
            out=gsl4[:, 0, :], in_=cc_out.ap()[6 * 128:7 * 128, :]),
            "slab0", 16)
        wt("vector", SLW[6])
        SGP[6] = eop("vector", lambda h: h.scalar_tensor_tensor(
            Sg[:], gsl4[:, 0, :], pw_t[:, 6:7], Sg[:],
            ALU.mult, ALU.add))
        SGP[NCORES - 1] = SGP[6]

    S65R = [None] * NT

    def am_off(hh):
        p, par = hh // 2, hh % 2
        return (p // 4) * 1024 + par * 512 + (p % 4) * 128

    PJQ, RMQ, PHIQ, TQQ, YMM, YEV, TQY, EXQ = \
        [None] * NT, [None] * NT, [None] * NT, [None] * NT, [None] * NT, \
        [None] * NT, [None] * NT, [None] * NT
    MKh = [[None, None] for _ in range(NT)]
    AMMh = [[None, None] for _ in range(NT)]
    MINQL = [None] * NT
    RQPL = [None] * NT
    PJQL = [None] * NT
    PJS = [None] * NT

    def b_qproj(c):
        if not (0 <= c < NT):
            return
        wt("tensor", XTB[c])
        if c == 0:
            wt("tensor", WQ)
        if c >= 1:
            wt("tensor", PJS[c - 1])
        last = None
        for cch in range(8):
            for hf in range(2):
                last = eop("tensor", lambda h, sl=c % 2, hf=hf, c2=cch:
                           h.matmul(
                    pjq[:, hf * 512:(hf + 1) * 512],
                    xtB[:, sl, c2, :], wqb[:, c2, hf * 512:(hf + 1) * 512],
                    start=(c2 == 0), stop=(c2 == 7),
                    skip_group_check=True))
        PJQL[c] = last
        wt("scalar", PJQL[c])
        if c >= 2:
            wt("scalar", RMQ[c - 2])
        PJS[c] = eop("scalar", lambda h, sl=c % 2: h.activation(
            pjs[:, sl, :], pjq[:], AF.Copy))

    def b_m(c):
        if not (0 <= c < NT):
            return
        wt("vector", PJS[c])
        if c >= 1:
            wt("vector", RQPL[c - 1])
        xx = pjs[:, c % 2, :].rearrange("p (g t w) -> p t g w", t=2, w=32)
        last = None
        for half in range(2):
            tab = cs_t[:, c, half * 32:half * 32 + 64].rearrange(
                "p (t o i) -> p t o i", t=2, o=1).broadcast_to(
                [128, 2, 16, 32])
            last = eop("vector", lambda h, half=half, tab=tab:
                       h.tensor_tensor(
                mB[:, 2 * half:2 * half + 2, :].rearrange(
                    "p t (g i) -> p t g i", i=32),
                xx, tab, ALU.mult))
        RMQ[c] = last

    def b_r(c):
        if not (0 <= c < NT):
            return
        wt("gpsimd", RMQ[c])
        if c >= 2:
            wt("gpsimd", PHIQ[c - 2])
        eop("gpsimd", lambda h, sl=c % 2: h.tensor_tensor(
            rq[:, sl, :].rearrange("p (g w) -> p g w", w=64)[:, :, 0:32],
            mB[:, 0, :].rearrange("p (g i) -> p g i", i=32),
            mB[:, 1, :].rearrange("p (g i) -> p g i", i=32), ALU.subtract))
        RQPL[c] = eop("gpsimd", lambda h, sl=c % 2: h.tensor_tensor(
            rq[:, sl, :].rearrange("p (g w) -> p g w", w=64)[:, :, 32:64],
            mB[:, 2, :].rearrange("p (g i) -> p g i", i=32),
            mB[:, 3, :].rearrange("p (g i) -> p g i", i=32), ALU.add))

    def b_exp(c):
        if not (0 <= c < NT):
            return
        wt("scalar", RQPL[c])
        eop("scalar", lambda h, sl=c % 2: h.activation(
            mnq[:, 0, :], rq[:, sl, :], AF.Relu, scale=cstc(C_NEG1)))
        if c >= 1:
            wt("scalar", PHIQ[c - 1])
        EXQ[c] = eop("scalar", lambda h: h.activation(
            eq[:, 0, :], mnq[:, 0, :], AF.Exp, scale=cstc(C_NEG1)))

    def b_stt(c):
        if not (0 <= c < NT):
            return
        wt("vector", EXQ[c])
        if c >= 2:
            wt("vector", TQQ[c - 2])
        PHIQ[c] = eop("vector", lambda h, sl=c % 2: h.scalar_tensor_tensor(
            qtm[:, sl, :], rq[:, sl, :], 0.0, eq[:, 0, :],
            ALU.max, ALU.add))

    def b_tqq(c):
        if not (0 <= c < NT):
            return
        wt("sync", PHIQ[c])
        if c >= 4:
            wt("sync", YMM[c - 4])
        TQQ[c] = op("sync", lambda h, sl=c % 4, ql=c % 2: h.dma_start_transpose(
            qfm[:, sl, :, :], qtm[:, ql, :]), "tqq%d" % (c % 2), 16)

    def b_amm(c, half):
        if not (0 <= c < NT):
            return
        if half == 0:
            wt("tensor", TQK[c])
            wt("tensor", TQQ[c])
            if c >= 1:
                wt("tensor", MKh[c - 1][1])
        else:
            wt("tensor", MKh[c][0])
        last = None
        for pp in range(4):
            p = half * 4 + pp
            for par in range(2):
                last = eop("tensor", lambda h, sl=c % 4, p=p, pp=pp, par=par,
                           c=c: h.matmul(
                    apsm[:, par * 512 + pp * 128:par * 512 + pp * 128 + 128],
                    phk_fm[par * 64:(par + 1) * 64, p,
                           c * 128:(c + 1) * 128],
                    qfm[par * 64:(par + 1) * 64, sl, p, :],
                    start=(pp == 0), stop=(pp == 3),
                    skip_group_check=True))
        AMMh[c][half] = last

    def b_mask(c, half):
        if not (0 <= c < NT):
            return
        wt("vector", AMMh[c][half])
        if c == 0 and half == 0:
            wt("vector", TRI)
        if c >= 4:
            wt("vector", YMM[c - 4])
        tribc = tri_b[:].rearrange("p (o i) -> p o i", o=1).broadcast_to(
            [128, 8, 128])
        MKh[c][half] = eop("vector", lambda h, sl=c % 4, half=half,
                           tribc=tribc: h.tensor_tensor(
            Am[:, sl, half * 1024:half * 1024 + 1024].rearrange(
                "p (g i) -> p g i", i=128),
            apsm[:].rearrange("p (g i) -> p g i", i=128),
            tribc, ALU.mult))

    def b_s65(c):
        if not (1 <= c < NT):
            return
        if c == 1:
            wt("gpsimd", SGP[NCORES - 1])
        if c >= 2:
            wt("gpsimd", YMM[c - 2])
        S65R[c] = eop("gpsimd", lambda h, c=c, d2=c % 2: h.tensor_tensor(
            S65[:, d2, :], Sg[:], D_bf[:, c - 1, :], ALU.add))

    VB = [None] * NT

    def b_vload(c):
        if not (0 <= c < NT):
            return
        if c >= 4:
            wt("sync", YMM[c - 4])
        VB[c] = op("sync", lambda h, c=c, sl=c % 4: h.dma_start(
            out=v65[:, sl, :], in_=v65_d.ap()[:, c * 1040:(c + 1) * 1040]),
            "vb%d" % (c % 2), 16)

    def b_y(c):
        if not (0 <= c < NT):
            return
        wt("tensor", MKh[c][1])
        wt("tensor", S65R[c])
        wt("tensor", VB[c])
        if c >= 1:
            wt("tensor", YEV[c - 1])
        last = None
        for p in range(NP):
            g, off = p // 3, (p % 3) * 130
            for par in range(2):
                hh = 2 * p + par
                st = (p % 3 == 0 and par == 0)
                eop("tensor", lambda h, sl=c % 4, hh=hh, g=g, off=off,
                    par=par, c=c, st=st: h.matmul(
                    yps[g][:, off + par * 65: off + par * 65 + 65],
                    Am[:, sl, am_off(hh):am_off(hh) + 128],
                    v65[:, c % 4, hh * 65:(hh + 1) * 65],
                    start=st, stop=False, skip_group_check=True))
                sp2 = ((p % 3 == 2 or p == NP - 1) and par == 1)
                last = eop("tensor", lambda h, sl=c % 4, d2=c % 2, p=p,
                           par=par, g=g,
                           off=off, sp2=sp2: h.matmul(
                    yps[g][:, off + par * 65: off + par * 65 + 65],
                    qfm[par * 64:(par + 1) * 64, sl, p, :],
                    S65[par * 64:(par + 1) * 64, d2, p * 65:(p + 1) * 65],
                    start=False, stop=sp2, skip_group_check=True))
        YMM[c] = last

    def b_yev(c):
        if not (0 <= c < NT):
            return
        wt("vector", YMM[c])
        if c >= 2:
            wt("vector", TQY[c - 2])
        for g in range(3):
            ng = 3 if g < 2 else 2
            src2 = yps[g][:, 0:130 * ng].rearrange(
                "p (a b) -> p a b", b=65)[:, :, 64:65]
            eop("vector", lambda h, g=g, ng=ng, src2=src2: h.tensor_copy(
                denf[:, 6 * g: 6 * g + 2 * ng],
                src2.rearrange("p a b -> p (a b)")))
        eop("vector", lambda h: h.tensor_single_scalar(
            denm[:], denf[:], 1e-6, ALU.max))
        eop("vector", lambda h: h.reciprocal(rden[:], denm[:]))
        last = None
        for g in range(3):
            ng = 3 if g < 2 else 2
            ysrc = yps[g][:, 0:130 * ng].rearrange(
                "p (a b) -> p a b", b=65)[:, :, 0:64]
            ydst = Ytm[:, c % 2, 128 * 3 * g: 128 * 3 * g + 128 * ng].rearrange(
                "p (a b) -> p a b", b=64)
            rsrc = rden[:, 6 * g:6 * g + 2 * ng].rearrange(
                "p (a o) -> p a o", o=1).broadcast_to([128, 2 * ng, 64])
            last = eop("vector", lambda h, ysrc=ysrc, ydst=ydst,
                       rsrc=rsrc: h.tensor_tensor(
                ydst, ysrc, rsrc, ALU.mult))
        YEV[c] = last

    def b_tqy(c):
        if not (0 <= c < NT):
            return
        wt("sync", YEV[c])
        TQY[c] = op("sync", lambda h, c=c, sl=c % 2: h.dma_start_transpose(
            YT[:, :, c * 128:(c + 1) * 128], Ytm[:, sl, :]),
            "tqy%d" % (c % 2), 16)

    b_xload(0)
    for j in range(NT + 5):
        b_xload(j + 1)
        b_stt(j - 3)
        b_tqq(j - 3)
        b_amm(j - 4, 0)
        b_qproj(j)
        b_mask(j - 4, 0)
        b_amm(j - 4, 1)
        b_m(j - 1)
        b_mask(j - 4, 1)
        b_vload(j - 3)
        if j == 5:
            b_slabs()
            wt("vector", SGP[NCORES - 1])
            S65R[0] = eop("vector", lambda h: h.tensor_copy(
                S65[:, 0, :], Sg[:]))
        b_y(j - 5)
        b_yev(j - 5)
        b_r(j - 1)
        b_s65(j - 4)
        b_exp(j - 2)
        b_tqy(j - 5)

    endB = marks_now()
    esB.close()
    esAB.close()

    esC = ExitStack()
    rhq = esC.enter_context(nc.sbuf_tensor("rhq", [128, 8, D], BF16))
    qrT = esC.enter_context(nc.sbuf_tensor("qrT", [128, 2, D], BF16))
    rhf = esC.enter_context(nc.sbuf_tensor("rhf", [128, 2, D], F32))
    rsq = esC.enter_context(nc.sbuf_tensor("rsq", [128, 8], F32))
    rrq = esC.enter_context(nc.sbuf_tensor("rrq", [128, 8], F32))
    rs_row = esC.enter_context(nc.sbuf_tensor("rs_row", [1, D], F32))
    rsbc = esC.enter_context(nc.sbuf_tensor("rsbc", [128, D], F32))
    qfr = esC.enter_context(nc.sbuf_tensor("qfr", [128, D], F32))
    qir = esC.enter_context(nc.sbuf_tensor("qir", [128, D], I32))
    xC = esC.enter_context(nc.sbuf_tensor("xC", [128, 3, D], F32))
    z1 = esC.enter_context(nc.sbuf_tensor("z1", [128, 4, D], F32))
    h1 = esC.enter_context(nc.sbuf_tensor("h1", [128, 4, D], F32))
    z2 = esC.enter_context(nc.sbuf_tensor("z2", [128, 3, D], F32))
    sqd = esC.enter_context(nc.sbuf_tensor("sqd", [128, D], F32))
    t1 = esC.enter_context(nc.sbuf_tensor("t1", [128, 2, D], F32))
    t2 = esC.enter_context(nc.sbuf_tensor("t2", [128, 2, D], F32))
    qf = esC.enter_context(nc.sbuf_tensor("qf", [128, D], F32))
    qi = esC.enter_context(nc.sbuf_tensor("qi", [128, D], I32))
    qltm = esC.enter_context(nc.sbuf_tensor("qltm", [128, 2, D], BF16))
    qlT = esC.enter_context(nc.sbuf_tensor("qlT", [128, 4, 8, 128], BF16))
    lsv = esC.enter_context(nc.sbuf_tensor("lsv", [128, 4], F32))
    outb = esC.enter_context(nc.sbuf_tensor("outb", [128, 2, D], F32))
    sv = esC.enter_context(nc.sbuf_tensor("sv", [128, 60], F32))
    ops2 = [esC.enter_context(nc.psum_tensor(f"ops2_{i}", [128, D], F32))
            for i in range(2)]
    ffs2 = [esC.enter_context(nc.psum_tensor(f"ffs{i}", [128, D], F32))
            for i in range(2)]

    (SV_MU, SV_MS, SV_VAR, SV_L, SV_RSTD, SV_AM, SV_LS, SV_RLS,
     SV_S1, SV_S2, SV_MU2, SV_MS2, SV_VAR2, SV_L2, SV_RSTD2,
     SV_S3, SV_S4, SV_LNB, SV_LNB2, SV_MUN) = range(20)

    def svc(i, n):
        j = i + 20 * (n % 3)
        return sv[:, j:j + 1]

    def lsc(n):
        return lsv[:, (n % 4):(n % 4) + 1]

    RHL, QRT = [None] * 8, [None] * 8
    TQRL = [None] * 8

    def c_rhs_load(oc):
        if not (0 <= oc < 8):
            return
        if oc >= 2:
            wt("sync", QRT[oc - 2])
        RHL[oc] = op("sync", lambda h, oc=oc: h.dma_start(
            out=rhf[:, oc % 2, :], in_=rhsT_d[oc * 128:(oc + 1) * 128, :]),
            "rhf%d" % (oc % 2), 16)

    def c_rhs_quant(oc):
        if not (0 <= oc < 8):
            return
        wt("vector", RHL[oc])
        eop("vector", lambda h, oc=oc: h.tensor_reduce(
            rsq[:, oc:oc + 1], rhf[:, oc % 2, :], AXX,
            ALU.max, apply_absolute_value=True))
        eop("vector", lambda h, oc=oc: h.tensor_scalar(
            rsq[:, oc:oc + 1], rsq[:, oc:oc + 1], 1e-12, 1.0 / 127.0,
            ALU.max, ALU.mult))
        eop("vector", lambda h, oc=oc: h.reciprocal(
            rrq[:, oc:oc + 1], rsq[:, oc:oc + 1]))
        eop("vector", lambda h, oc=oc: h.tensor_tensor(
            qfr[:], rhf[:, oc % 2, :],
            rrq[:, oc:oc + 1].broadcast_to([128, D]), ALU.mult))
        eop("vector", lambda h: h.tensor_scalar(
            qir[:], qfr[:], 127.0, -127.0, ALU.min, ALU.max))
        if oc >= 2:
            wt("vector", TQRL[oc - 2])
        QRT[oc] = eop("vector", lambda h, oc=oc: h.tensor_copy(
            qrT[:, oc % 2, :], qir[:]))
        wt("sync", QRT[oc])
        TQRL[oc] = op("sync", lambda h, oc=oc: h.dma_start_transpose(
            rhq[:, :, oc * 128:(oc + 1) * 128], qrT[:, oc % 2, :]), "tqr", 16)

    def c_rs_row():
        RS1 = op("sync", lambda h: h.dma_start(out=rs_sc.ap(), in_=rsq[:]),
                 "rs1", 16)
        wt("sync", RS1)
        return op("sync", lambda h: h.dma_start(
            out=rs_row[:].rearrange("o (c p) -> o c p", p=128),
            in_=rs_sc.ap().rearrange("p (o c) -> o c p", o=1),
            allow_slow_non_contiguous=True), "rs2", 16)

    OPM, Z1V, SQ1, LNV1, LNR1, H1S = ([None] * NT for _ in range(6))
    LNB1V, LNB2V, SC3, T2P = ([None] * NT for _ in range(4))
    AMV, QIV, QBB, TQF, FFM = ([None] * NT for _ in range(5))
    EP0, Z2V, SQ2, LNV2, LNR2, FIN, STO, XCW = ([None] * NT for _ in range(8))
    cst_state = {}

    def c_xload(n):
        if not (0 <= n < NT):
            return
        if n >= 3:
            wt("sync", Z1V[n - 3])
        XCW[n] = op("sync", lambda h, n=n, sl=n % 3: h.dma_start(
            out=xC[:, sl, :], in_=x_d[n * 128:(n + 1) * 128, :]),
            "xc%d" % (n % 3), 16)

    def c_opm(n):
        if not (0 <= n < NT):
            return
        if n == 0:
            wt("tensor", WO)
        if n >= 2:
            wt("tensor", Z1V[n - 2])
        last = None
        for e in range(8):
            for hf in range(2):
                last = eop("tensor", lambda h, n=n, hf=hf, e=e: h.matmul(
                    ops2[n % 2][:, hf * 512:(hf + 1) * 512],
                    YT[:, e, n * 128:(n + 1) * 128],
                    wo_b[:, e, hf * 512:(hf + 1) * 512],
                    start=(e == 0), stop=(e == 7),
                    skip_group_check=True))
        OPM[n] = last

    def c_ln1(n):
        if not (0 <= n < NT):
            return
        wt("vector", OPM[n])
        wt("vector", XCW[n])
        if n >= 4:
            wt("vector", H1S[n - 4])
        Z1V[n] = eop("vector", lambda h, n=n, sl=n % 4, xl=n % 3:
                     h.scalar_tensor_tensor(
            z1[:, sl, :], ops2[n % 2][:], 1.0, xC[:, xl, :],
            ALU.mult, ALU.add, accum_out=svc(SV_S1, n)))
        wt("scalar", Z1V[n])
        if n == 0:
            wt("scalar", CONST)
        SQ1[n] = eop("scalar", lambda h, n=n, sl=n % 4: h.activation(
            sqd[:], z1[:, sl, :], AF.Square, accum_out=svc(SV_S2, n)))

    def c_lnr(n):
        if not (0 <= n < NT):
            return
        eop("scalar", lambda h, n=n: h.activation(
            svc(SV_MU, n), svc(SV_S1, n), AF.Copy, scale=cstc(C_INV1024)))
        eop("scalar", lambda h, n=n: h.activation(
            svc(SV_MS, n), svc(SV_S2, n), AF.Identity,
            scale=cstc(C_INV1024), bias=cstc(C_LNEPS)))
        eop("scalar", lambda h, n=n: h.activation(
            svc(SV_VAR, n), svc(SV_MU, n), AF.Square))
        eop("scalar", lambda h, n=n: h.activation(
            svc(SV_L, n), svc(SV_VAR, n), AF.Ln, scale=cstc(C_NEG1),
            bias=svc(SV_MS, n)))
        LNR1[n] = eop("scalar", lambda h, n=n: h.activation(
            svc(SV_RSTD, n), svc(SV_L, n), AF.Exp, scale=cstc(C_NHALF)))

    def c_quant(n):
        if not (0 <= n < NT):
            return
        if n >= 4:
            wt("vector", Z2V[n - 4])
        H1S[n] = eop("vector", lambda h, n=n, sl=n % 4, zl=n % 4:
                     h.scalar_tensor_tensor(
            h1[:, sl, :], z1[:, zl, :], svc(SV_MU, n),
            rstd_bc(n), ALU.subtract, ALU.mult))
        eop("vector", lambda h, n=n, sl=n % 4: h.tensor_reduce(
            svc(SV_AM, n), h1[:, sl, :], AXX, ALU.max,
            apply_absolute_value=True))
        eop("vector", lambda h, n=n: h.tensor_scalar(
            lsc(n), svc(SV_AM, n), 1e-12, 1.0 / 127.0,
            ALU.max, ALU.mult))
        eop("vector", lambda h, n=n: h.reciprocal(
            svc(SV_RLS, n), lsc(n)))
        eop("vector", lambda h, n=n, sl=n % 4: h.tensor_scalar(
            qf[:], h1[:, sl, :], svc(SV_RLS, n), None, ALU.mult))
        QIV[n] = eop("vector", lambda h: h.tensor_scalar(
            qi[:], qf[:], 127.0, -127.0, ALU.min, ALU.max))
        if n >= 2:
            wt("vector", TQF[n - 2])
        QBB[n] = eop("vector", lambda h, sl=n % 2: h.tensor_copy(
            qltm[:, sl, :], qi[:]))
        wt("sync", QBB[n])
        if n >= 4:
            wt("sync", FFM[n - 4])
        TQF[n] = op("sync", lambda h, sl=n % 4, ql=n % 2: h.dma_start_transpose(
            qlT[:, sl, :, :], qltm[:, ql, :]), "tqf%d" % (n % 2), 16)

    def rstd_bc(n):
        return svc(SV_RSTD, n).broadcast_to([128, D])

    def rstd2_bc(n):
        return svc(SV_RSTD2, n).broadcast_to([128, D])

    def c_bc():
        wt("tensor", cst_state["RS2"])
        wt("tensor", CONST)
        last = None
        for hf in range(2):
            last = eop("tensor", lambda h, hf=hf: h.matmul(
                ffs2[1][:, hf * 512:(hf + 1) * 512], ones_f[:],
                rs_row[:, hf * 512:(hf + 1) * 512],
                start=True, stop=True, skip_group_check=True))
        wt("scalar", last)
        cst_state["RSBC"] = eop("scalar", lambda h: h.activation(
            rsbc[:], ffs2[1][:], AF.Copy))

    def c_ffn(n):
        if not (0 <= n < NT):
            return
        wt("tensor", TQF[n])
        if n == 0:
            wt("tensor", TQRL[7])
        if n == 1:
            wt("tensor", cst_state["RSBC"])
        if n >= 2:
            wt("tensor", EP0[n - 2])
        last = None
        for dch in range(8):
            for hf in range(2):
                last = eop("tensor", lambda h, sl=n % 4, fl=n % 2, hf=hf,
                           d2=dch: h.matmul(
                    ffs2[fl][:, hf * 512:(hf + 1) * 512],
                    qlT[:, sl, d2, :], rhq[:, d2, hf * 512:(hf + 1) * 512],
                    start=(d2 == 0), stop=(d2 == 7),
                    skip_group_check=True))
        FFM[n] = last

    def c_ln2a(n):
        if not (0 <= n < NT):
            return
        wt("scalar", FFM[n])
        if n >= 2:
            wt("scalar", T2P[n - 2])
        EP0[n] = eop("scalar", lambda h, n=n, fl=n % 2: h.activation(
            t1[:, n % 2, :], ffs2[fl][:], AF.Copy, scale=lsc(n)))
    def c_t2(n):
        if not (0 <= n < NT):
            return
        wt("vector", EP0[n])
        if n == 0:
            wt("vector", cst_state["RSBC"])
        T2P[n] = eop("vector", lambda h, n=n, sl=n % 2:
                     h.scalar_tensor_tensor(
            t2[:, sl, :], t1[:, sl, :], 1.0, rsbc[:], ALU.mult, ALU.mult,
            accum_out=svc(SV_S3, n)))

    def c_ln2z(n):
        if not (0 <= n < NT):
            return
        if n >= 3:
            wt("vector", FIN[n - 3])
        Z2V[n] = eop("vector", lambda h, n=n, sl=n % 3, hl=n % 4, tl=n % 2:
                     h.tensor_tensor(
            z2[:, sl, :], t2[:, tl, :], h1[:, hl, :], ALU.add))
        wt("scalar", Z2V[n])
        SQ2[n] = eop("scalar", lambda h, n=n, sl=n % 3: h.activation(
            sqd[:], z2[:, sl, :], AF.Square, accum_out=svc(SV_S4, n)))

    def c_ln2b(n):
        if not (0 <= n < NT):
            return
        eop("scalar", lambda h, n=n: h.activation(
            svc(SV_MU2, n), svc(SV_S3, n), AF.Copy, scale=cstc(C_INV1024)))
        eop("scalar", lambda h, n=n: h.activation(
            svc(SV_MS2, n), svc(SV_S4, n), AF.Identity,
            scale=cstc(C_INV1024), bias=cstc(C_LNEPS)))
        eop("scalar", lambda h, n=n: h.activation(
            svc(SV_VAR2, n), svc(SV_MU2, n), AF.Square))
        eop("scalar", lambda h, n=n: h.activation(
            svc(SV_L2, n), svc(SV_VAR2, n), AF.Ln, scale=cstc(C_NEG1),
            bias=svc(SV_MS2, n)))
        LNR2[n] = eop("scalar", lambda h, n=n: h.activation(
            svc(SV_RSTD2, n), svc(SV_L2, n), AF.Exp, scale=cstc(C_NHALF)))
        eop("scalar", lambda h, n=n: h.activation(
            svc(SV_MUN, n), svc(SV_S3, n), AF.Copy, scale=cstc(C_NINV1024)))
        LNB2V[n] = eop("scalar", lambda h, n=n: h.activation(
            svc(SV_LNB2, n), svc(SV_MUN, n), AF.Copy,
            scale=svc(SV_RSTD2, n)))

    def c_fin(n):
        if not (0 <= n < NT):
            return
        wt("vector", LNR2[n])
        if n >= 2:
            wt("vector", STO[n - 2])
        FIN[n] = eop("vector", lambda h, n=n, sl=n % 2, zl=n % 3:
                     h.scalar_tensor_tensor(
            outb[:, sl, :], z2[:, zl, :], svc(SV_MU2, n),
            rstd2_bc(n), ALU.subtract, ALU.mult))
        wt("sync", FIN[n])
        STO[n] = op("sync", lambda h, n=n, sl=n % 2: h.dma_start(
            out=out_d[n * 128:(n + 1) * 128, :], in_=outb[:, sl, :]),
            "sto%d" % (n % 2), 16)

    wt("sync", PHIQ[NT - 1])
    wt("sync", RQPL[NT - 1])
    wt("sync", EXQ[NT - 1])
    wt("vector", RQPL[NT - 1])
    wt("vector", EXQ[NT - 1])
    c_rhs_load(0)
    c_rhs_load(1)
    for oc in range(8):
        c_rhs_quant(oc)
        c_rhs_load(oc + 2)
    cst_state["RS2"] = c_rs_row()

    barrier(endB)

    c_xload(0)
    for j in range(NT + 8):
        c_xload(j + 1)
        c_ln1(j - 1)
        c_opm(j)
        if j == 1:
            c_bc()
        c_lnr(j - 2)
        c_quant(j - 3)
        c_ffn(j - 4)
        c_ln2a(j - 5)
        c_t2(j - 5)
        c_ln2z(j - 6)
        c_ln2b(j - 7)
        c_fin(j - 8)

    wt("sync", STO[NT - 2])
    wt("sync", STO[NT - 1])

    with nc.Block() as block:
        def runner(eng):
            def run(h):
                for item in streams[eng]:
                    if item[0] == "wait":
                        h.wait_ge(sem[item[1]], item[2])
                    else:
                        inst = item[1](h)
                        if item[2] is not None:
                            inst.then_inc(sem[item[2]], item[3])
            return run
        block.sync(runner("sync"))
        block.gpsimd(runner("gpsimd"))
        block.vector(runner("vector"))
        block.scalar(runner("scalar"))
        block.tensor(runner("tensor"))

    esC.close()
    esBC.close()
    es.close()
    return nc


_NC_CACHE = None


def kernel(L0_attn_x, L0_attn_Wq, L0_attn_Wk, L0_attn_Wv, L0_attn_Wo,
           L0_ffn_rhs):
    global _NC_CACHE
    x = np.ascontiguousarray(np.asarray(L0_attn_x, dtype=np.float32))
    wq = np.ascontiguousarray(np.asarray(L0_attn_Wq, dtype=np.float32).T)
    wk = np.ascontiguousarray(np.asarray(L0_attn_Wk, dtype=np.float32).T)
    wv = np.ascontiguousarray(np.asarray(L0_attn_Wv, dtype=np.float32).T)
    wo = np.ascontiguousarray(np.asarray(L0_attn_Wo, dtype=np.float32).T)
    rhsT = np.ascontiguousarray(np.asarray(L0_ffn_rhs, dtype=np.float32).T)

    inv = (10000.0 ** (-np.arange(HF, dtype=np.float64) / HF)).astype(np.float32)
    tri = np.ascontiguousarray(
        np.tril(np.ones((128, 128), dtype=np.float32)).T)
    cstv = np.zeros((128, 8), dtype=np.float32)
    cstv[:, 0] = 1.0 / 1024.0
    cstv[:, 1] = 1e-5
    cstv[:, 2] = -0.5
    cstv[:, 3] = -1.0
    cstv[:, 4] = -1.0 / 1024.0
    ones1 = np.ones((1, 128), dtype=np.float32)

    if _NC_CACHE is None:
        _NC_CACHE = build_nc()
    nc = _NC_CACHE

    in_maps = []
    for c in range(NCORES):
        b, qi2 = divmod(c, 4)
        t0 = qi2 * TC
        xs = np.ascontiguousarray(x[b, t0:t0 + TC, :])
        xTs = np.ascontiguousarray(xs.T)
        pos = (t0 + np.arange(TC)).astype(np.float32)
        ang = pos[:, None] * inv[None, :]
        cosv = np.cos(ang).astype(np.float32)
        sinv = np.sin(ang).astype(np.float32)
        cs = np.ascontiguousarray(np.concatenate(
            [cosv, sinv, cosv], axis=1))
        pw = np.zeros((128, NCORES), dtype=np.float32)
        for r in range(4 * b, 4 * b + qi2):
            pw[:, r] = 1.0
        in_maps.append({
            "x": xs, "xT": xTs, "wqT": wq, "wkT": wk, "wvT": wv, "woT": wo,
            "rhsT": rhsT, "cs": cs, "tri": tri, "pw": pw, "cstd": cstv,
            "ones1": ones1,
        })

    res = run_bass_kernel_spmd(nc, in_maps, core_ids=list(range(NCORES)))
    out = np.empty((2, 8192, 1024), dtype=np.float32)
    for c in range(NCORES):
        b, qi2 = divmod(c, 4)
        out[b, qi2 * TC:(qi2 + 1) * TC, :] = res.results[c]["out"]
    return out


# version 36
# speedup vs baseline: 1.0360x; 1.0023x over previous
import numpy as np
import concourse.bass as bass
import concourse.mybir as mybir
from concourse.bass_utils import run_bass_kernel_spmd
from contextlib import ExitStack

F32 = mybir.dt.float32
BF16 = mybir.dt.bfloat16
I32 = mybir.dt.int32
ALU = mybir.AluOpType
AF = mybir.ActivationFunctionType
AXX = mybir.AxisListType.X

NCORES = 8
TC = 2048
NT = 16
D = 1024
H = 16
HF = 32
NP = 8
SW = 65 * NP

ENGS = ("sync", "gpsimd", "vector", "scalar", "tensor")


def build_nc():
    nc = bass.Bass("TRN2", target_bir_lowering=False, debug=False)

    x_d = nc.dram_tensor("x", [TC, D], F32, kind="ExternalInput").ap()
    xT_d = nc.dram_tensor("xT", [D, TC], F32, kind="ExternalInput").ap()
    wq_d = nc.dram_tensor("wqT", [D, D], F32, kind="ExternalInput").ap()
    wk_d = nc.dram_tensor("wkT", [D, D], F32, kind="ExternalInput").ap()
    wv_d = nc.dram_tensor("wvT", [D, D], F32, kind="ExternalInput").ap()
    wo_d = nc.dram_tensor("woT", [D, D], F32, kind="ExternalInput").ap()
    rhsT_d = nc.dram_tensor("rhsT", [D, D], F32, kind="ExternalInput").ap()
    cs_d = nc.dram_tensor("cs", [TC, 3 * HF], F32, kind="ExternalInput").ap()
    tri_d = nc.dram_tensor("tri", [128, 128], F32, kind="ExternalInput").ap()
    pw_d = nc.dram_tensor("pw", [128, NCORES], F32, kind="ExternalInput").ap()
    cst_d = nc.dram_tensor("cstd", [128, 8], F32, kind="ExternalInput").ap()
    ones_d = nc.dram_tensor("ones1", [1, 128], F32, kind="ExternalInput").ap()
    out_d = nc.dram_tensor("out", [TC, D], F32, kind="ExternalOutput").ap()
    cc_in = nc.dram_tensor("cc_in", [128, SW], F32)
    cc_out = nc.dram_tensor("cc_out", [NCORES * 128, SW], F32)
    rs_sc = nc.dram_tensor("rs_sc", [128, 8], F32)
    v65_d = nc.dram_tensor("v65sp", [128, NT * 65 * H], BF16)

    streams = {e: [] for e in ENGS}
    cnt = {}
    es = ExitStack()
    sem = {}
    ENG_SEM = {"tensor": "pe", "vector": "vec", "scalar": "scl",
               "gpsimd": "pl", "sync": "syq"}

    def S(name):
        if name not in sem:
            sem[name] = es.enter_context(nc.semaphore("s_" + name))
            cnt[name] = 0
        return sem[name]

    def op(eng, fn, sm=None, inc=0):
        if sm is not None:
            S(sm)
            cnt[sm] += inc
            streams[eng].append(("op", fn, sm, inc))
            return (sm, cnt[sm])
        streams[eng].append(("op", fn, None, 0))
        return None

    def eop(eng, fn):
        own = ENG_SEM[eng]
        if eng != "tensor" and cnt.get(own, 0) > 0:
            streams[eng].append(("wait", own, cnt[own]))
        return op(eng, fn, own, 1)

    def wt(eng, spec):
        if spec is None:
            import sys
            print("WT-NONE-SKIP on", eng, file=sys.stderr)
            return
        sm, val = spec
        if val > 0:
            streams[eng].append(("wait", sm, val))

    def marks_now():
        return dict(cnt)

    def barrier(marks):
        for e in ENGS:
            own = ENG_SEM[e]
            for sm, val in marks.items():
                if sm != own and sm not in ("cc", "ccin") and val > 0:
                    streams[e].append(("wait", sm, val))

    Sg = es.enter_context(nc.sbuf_tensor("Sg", [128, SW], F32))
    tri_b = es.enter_context(nc.sbuf_tensor("tri_b", [128, 128], BF16))
    pw_t = es.enter_context(nc.sbuf_tensor("pw_t", [128, NCORES], F32))
    cst = es.enter_context(nc.sbuf_tensor("cst", [128, 8], F32))
    ones_f = es.enter_context(nc.sbuf_tensor("ones_f", [1, 128], F32))
    zb1 = es.enter_context(nc.sbuf_tensor("zb1", [1, 128], BF16))
    esAB = ExitStack()
    phk_fm = esAB.enter_context(nc.sbuf_tensor("phk_fm", [128, 8, TC], BF16))
    v65 = esAB.enter_context(nc.sbuf_tensor("v65", [128, 4, 65 * H], BF16))
    D_bf = esAB.enter_context(nc.sbuf_tensor("D_bf", [128, NT, SW], BF16))
    cs_t = esAB.enter_context(nc.sbuf_tensor("cs_t", [128, NT, 3 * HF], F32))
    wqb = esAB.enter_context(nc.sbuf_tensor("wqb", [128, 8, D], BF16))
    xtB = esAB.enter_context(nc.sbuf_tensor("xtB", [128, 2, 8, 128], BF16))

    C_INV1024, C_LNEPS, C_NHALF, C_NEG1, C_NINV1024 = 0, 1, 2, 3, 4

    def cstc(i):
        return cst[:, i:i + 1]

    TRI = op("gpsimd", lambda h: h.dma_start(out=tri_b[:], in_=tri_d),
             "tri", 16)
    op("sync", lambda h: h.dma_start(
        out=cs_t[:], in_=cs_d.rearrange("(n p) i -> p n i", p=128)),
        "const", 16)
    op("sync", lambda h: h.dma_start(out=pw_t[:], in_=pw_d), "const", 16)
    op("sync", lambda h: h.dma_start(out=cst[:], in_=cst_d), "const", 16)
    CONST = op("sync", lambda h: h.dma_start(out=ones_f[:], in_=ones_d),
               "const", 16)

    eop("vector", lambda h: h.memset(zb1[:], 0.0))
    eop("vector", lambda h: h.memset(zb5[:], 0.0))
    eop("vector", lambda h: h.memset(Sg[:], 0.0))
    V_INIT = eop("vector", lambda h: h.memset(
        v65[:].rearrange("p n (g w) -> p n g w", w=65)[:, :, :, 64:65], 1.0))

    esA = ExitStack()
    wkv = esA.enter_context(nc.sbuf_tensor("wkv", [128, 16, D], BF16))
    xtA = esA.enter_context(nc.sbuf_tensor("xtA", [128, 2, 8, 128], BF16))
    ktm = esA.enter_context(nc.sbuf_tensor("ktm", [128, 2, D], BF16))
    mA = esA.enter_context(nc.sbuf_tensor("mA", [128, 4, 512], F32))
    rk = esA.enter_context(nc.sbuf_tensor("rk", [128, 2, D], F32))
    mnk = esA.enter_context(nc.sbuf_tensor("mnk", [128, 1, D], F32))
    ek = esA.enter_context(nc.sbuf_tensor("ek", [128, 2, D], BF16))
    Stot = esA.enter_context(nc.sbuf_tensor("Stot", [128, SW], F32))
    zb5 = esA.enter_context(nc.sbuf_tensor("zb5", [1, 512], BF16))
    pjk = esA.enter_context(nc.psum_tensor("pjk", [128, D], F32))
    pjv = esA.enter_context(nc.psum_tensor("pjv", [128, D], F32))
    dps = [esA.enter_context(nc.psum_tensor(f"dps{i}", [128, 512], F32))
           for i in range(2)]

    WKA = op("gpsimd", lambda h: h.dma_start(
        out=wkv[:, 0:4, :],
        in_=wk_d[0:512, :].rearrange("(c p) o -> p c o", p=128)), "wka", 16)
    WK = op("gpsimd", lambda h: h.dma_start(
        out=wkv[:, 4:8, :],
        in_=wk_d[512:1024, :].rearrange("(c p) o -> p c o", p=128)),
        "wkb", 16)
    WVA = op("gpsimd", lambda h: h.dma_start(
        out=wkv[:, 8:12, :],
        in_=wv_d[0:512, :].rearrange("(c p) o -> p c o", p=128)), "wva", 16)
    WKV = op("gpsimd", lambda h: h.dma_start(
        out=wkv[:, 12:16, :],
        in_=wv_d[512:1024, :].rearrange("(c p) o -> p c o", p=128)),
        "wvb", 16)

    wt("tensor", V_INIT)
    for i in range(2):
        eop("tensor", lambda h, i=i: h.matmul(
            dps[i][:, 0:260], zb1[:], zb5[:, 0:260],
            start=True, stop=False, skip_group_check=True))

    PJ, VE, RMK, PHIK, DMM, DEV, TQK, XTA, EXK = \
        [None] * NT, [None] * NT, [None] * NT, [None] * NT, [None] * NT, \
        [None] * NT, [None] * NT, [None] * NT, [None] * NT
    MINKL = [None] * NT
    RKPL = [None] * NT
    PJKL = [None] * NT

    def a_xload(n):
        if not (0 <= n < NT):
            return
        if n >= 2:
            wt("gpsimd", PJ[n - 2])
        XTA[n] = op("gpsimd", lambda h, n=n, sl=n % 2: h.dma_start(
            out=xtA[:, sl, :, :],
            in_=xT_d[:, n * 128:(n + 1) * 128].rearrange(
                "(c p) t -> p c t", p=128)), "xta%d" % (n % 2), 16)

    def a_kvproj(n):
        if not (0 <= n < NT):
            return
        wt("tensor", XTA[n])
        if n == 0:
            wt("tensor", WKA)
            wt("tensor", WK)
        if n >= 1:
            wt("tensor", RMK[n - 1])
        last = None
        for cch in range(8):
            for hf in range(2):
                last = eop("tensor", lambda h, sl=n % 2, hf=hf, c=cch:
                           h.matmul(
                    pjk[:, hf * 512:(hf + 1) * 512],
                    xtA[:, sl, c, :], wkv[:, c, hf * 512:(hf + 1) * 512],
                    start=(c == 0), stop=(c == 7),
                    skip_group_check=True))
        PJKL[n] = last
        if n == 0:
            wt("tensor", WVA)
            wt("tensor", WKV)
        if n >= 1:
            wt("tensor", VE[n - 1])
        for cch in range(8):
            for hf in range(2):
                last = eop("tensor", lambda h, sl=n % 2, hf=hf, c=cch:
                           h.matmul(
                    pjv[:, hf * 512:(hf + 1) * 512],
                    xtA[:, sl, c, :], wkv[:, 8 + c, hf * 512:(hf + 1) * 512],
                    start=(c == 0), stop=(c == 7),
                    skip_group_check=True))
        PJ[n] = last

    def a_D(n):
        if not (0 <= n < NT):
            return
        wt("tensor", PHIK[n])
        wt("tensor", VE[n])
        if n >= 1:
            wt("tensor", DEV[n - 1])
        last = None
        for p in range(NP):
            bank, cb = p // 4, (p % 4) * 65
            for par in range(2):
                hh = 2 * p + par
                last = eop("tensor", lambda h, sl=n % 2, par=par, hh=hh,
                           bank=bank, cb=cb, n=n: h.matmul(
                    dps[bank][par * 64:(par + 1) * 64, cb:cb + 65],
                    ktm[:, sl, hh * 64:(hh + 1) * 64],
                    v65[:, n % 4, hh * 65:(hh + 1) * 65],
                    start=False,
                    stop=(n == NT - 1 and hh == 2 * NP - 1),
                    tile_position=(0, par * 64), skip_group_check=True))
        DMM[n] = last

    def a_m(n):
        if not (0 <= n < NT):
            return
        wt("vector", PJKL[n])
        if n == 0:
            wt("vector", CONST)
        if n >= 1:
            wt("vector", RKPL[n - 1])
        xx = pjk[:].rearrange("p (g t w) -> p t g w", t=2, w=32)
        last = None
        for half in range(2):
            tab = cs_t[:, n, half * 32:half * 32 + 64].rearrange(
                "p (t o i) -> p t o i", t=2, o=1).broadcast_to(
                [128, 2, 16, 32])
            last = eop("vector", lambda h, half=half, tab=tab:
                       h.tensor_tensor(
                mA[:, 2 * half:2 * half + 2, :].rearrange(
                    "p t (g i) -> p t g i", i=32),
                xx, tab, ALU.mult))
        RMK[n] = last

    def a_r(n):
        if not (0 <= n < NT):
            return
        wt("gpsimd", RMK[n])
        if n >= 2:
            wt("gpsimd", PHIK[n - 2])
        eop("gpsimd", lambda h, sl=n % 2: h.tensor_tensor(
            rk[:, sl, :].rearrange("p (g w) -> p g w", w=64)[:, :, 0:32],
            mA[:, 0, :].rearrange("p (g i) -> p g i", i=32),
            mA[:, 1, :].rearrange("p (g i) -> p g i", i=32), ALU.subtract))
        RKPL[n] = eop("gpsimd", lambda h, sl=n % 2: h.tensor_tensor(
            rk[:, sl, :].rearrange("p (g w) -> p g w", w=64)[:, :, 32:64],
            mA[:, 2, :].rearrange("p (g i) -> p g i", i=32),
            mA[:, 3, :].rearrange("p (g i) -> p g i", i=32), ALU.add))

    def a_exp(n):
        if not (0 <= n < NT):
            return
        wt("scalar", RKPL[n])
        if n == 0:
            wt("scalar", CONST)
        eop("scalar", lambda h, sl=n % 2: h.activation(
            mnk[:, 0, :], rk[:, sl, :], AF.Relu, scale=cstc(C_NEG1)))
        if n >= 2:
            wt("scalar", PHIK[n - 2])
        EXK[n] = eop("scalar", lambda h, sl=n % 2: h.activation(
            ek[:, sl, :], mnk[:, 0, :], AF.Exp, scale=cstc(C_NEG1)))

    def a_stt(n):
        if not (0 <= n < NT):
            return
        wt("vector", EXK[n])
        if n >= 2:
            wt("vector", TQK[n - 2])
        if n >= 2:
            wt("vector", DMM[n - 2])
        PHIK[n] = eop("vector", lambda h, sl=n % 2: h.scalar_tensor_tensor(
            ktm[:, sl, :], rk[:, sl, :], 0.0, ek[:, sl, :],
            ALU.max, ALU.add))

    SPV = [None] * NT

    def a_vevac(n):
        if not (0 <= n < NT):
            return
        wt("scalar", PJ[n])
        if n == 0:
            wt("scalar", V_INIT)
        if n >= 4:
            wt("scalar", SPV[n - 4])
            wt("scalar", DMM[n - 4])
        VE[n] = eop("scalar", lambda h, n=n, sl=n % 4: h.activation(
            v65[:, sl, :].rearrange("p (g w) -> p g w", w=65)[:, :, 0:64],
            pjv[:].rearrange("p (g w) -> p g w", w=64), AF.Copy))
        wt("scalar", VE[n])
        SPV[n] = op("scalar", lambda h, n=n, sl=n % 4: h.dma_start(
            out=v65_d.ap()[:, n * 1040:(n + 1) * 1040], in_=v65[:, sl, :]),
            "spv%d" % (n % 2), 16)

    def a_tq(n):
        if not (0 <= n < NT):
            return
        wt("sync", PHIK[n])
        TQK[n] = op("sync", lambda h, n=n, sl=n % 2: h.dma_start_transpose(
            phk_fm[:, :, n * 128:(n + 1) * 128], ktm[:, sl, :]),
            "tqk%d" % (n % 2), 16)

    def a_devac(n):
        if not (0 <= n < NT):
            return
        wt("vector", DMM[n])
        eop("vector", lambda h, n=n: h.tensor_copy(
            D_bf[:, n, 0:260], dps[0][:, 0:260]))
        DEV[n] = eop("vector", lambda h, n=n: h.tensor_copy(
            D_bf[:, n, 260:520], dps[1][:, 0:260]))

    XTB = [None] * NT
    WQ = None

    def b_xload(c):
        if not (0 <= c < NT):
            return
        if XTB[c] is not None:
            return
        if c >= 2:
            wt("gpsimd", PJQL[c - 2])
        XTB[c] = op("gpsimd", lambda h, c=c, sl=c % 2: h.dma_start(
            out=xtB[:, sl, :, :],
            in_=xT_d[:, c * 128:(c + 1) * 128].rearrange(
                "(c p) t -> p c t", p=128)), "xtb%d" % (c % 2), 16)

    a_xload(0)
    for j in range(NT + 2):
        a_xload(j + 1)
        a_kvproj(j)
        a_m(j)
        a_stt(j - 2)
        a_tq(j - 2)
        a_D(j - 2)
        a_devac(j - 2)
        a_r(j)
        a_vevac(j)
        a_exp(j - 1)
        if 8 <= j <= 15:
            cch = j - 8
            WQ = op("gpsimd", lambda h, c=cch: h.dma_start(
                out=wqb[:, c, :],
                in_=wq_d[c * 128:(c + 1) * 128, :]), "wq", 16)
        if j == 14:
            b_xload(0)
        if j == 15:
            b_xload(1)

    eop("vector", lambda h: h.tensor_copy(Stot[:, 0:260], dps[0][:, 0:260]))
    V_STOT = eop("vector", lambda h: h.tensor_copy(
        Stot[:, 260:520], dps[1][:, 0:260]))
    wt("gpsimd", V_STOT)
    CCW = op("gpsimd", lambda h: h.dma_start(out=cc_in.ap(), in_=Stot[:]),
             "ccin", 16)
    wt("gpsimd", CCW)
    CC = op("gpsimd", lambda h: h.collective_compute(
        "AllGather", ALU.bypass, replica_groups=[list(range(NCORES))],
        ins=[cc_in.ap().opt()], outs=[cc_out.ap().opt()]), "cc", 1)

    endA = marks_now()
    esA.close()
    esBC = ExitStack()
    YT = esBC.enter_context(nc.sbuf_tensor("YT", [128, 8, TC], BF16, side="right"))
    wo_b = esBC.enter_context(nc.sbuf_tensor("wo_b", [128, 8, D], BF16, side="right"))

    esB = ExitStack()
    Am = esB.enter_context(nc.sbuf_tensor("Am", [128, 4, 2048], BF16))
    qfm = esB.enter_context(nc.sbuf_tensor("qfm", [128, 4, 8, 128], BF16))
    Ytm = esB.enter_context(nc.sbuf_tensor("Ytm", [128, 2, D], BF16))
    S65 = esB.enter_context(nc.sbuf_tensor("S65", [128, 2, SW], BF16))
    denf = esB.enter_context(nc.sbuf_tensor("denf", [128, H], F32))
    rden = esB.enter_context(nc.sbuf_tensor("rden", [128, H], F32))
    denm = esB.enter_context(nc.sbuf_tensor("denm", [128, H], F32))
    gsl4 = esB.enter_context(nc.sbuf_tensor("gsl4", [128, 5, SW], F32))
    mB = esB.enter_context(nc.sbuf_tensor("mB", [128, 4, 512], F32))
    pjs = esB.enter_context(nc.sbuf_tensor("pjs", [128, 2, D], F32))
    rq = esB.enter_context(nc.sbuf_tensor("rq", [128, 2, D], F32))
    mnq = esB.enter_context(nc.sbuf_tensor("mnq", [128, 1, D], F32))
    eq = esB.enter_context(nc.sbuf_tensor("eq", [128, 1, D], BF16))
    qtm = esB.enter_context(nc.sbuf_tensor("qtm", [128, 2, D], BF16))
    pjq = esB.enter_context(nc.psum_tensor("pjq", [128, D], F32))
    apsm = esB.enter_context(nc.psum_tensor("apsm", [128, D], F32))
    yps = [esB.enter_context(nc.psum_tensor(f"yps{i}", [128, 512], F32))
           for i in range(3)]

    barrier(endA)

    WO = op("gpsimd", lambda h: h.dma_start(
        out=wo_b[:], in_=wo_d.rearrange("(c p) o -> p c o", p=128)), "wo", 16)

    SLW, SGP = [None] * NCORES, [None] * NCORES

    def b_slabs():
        ranks = [0, 1, 2, 4, 5, 6]
        wt("scalar", CC)
        for i, r in enumerate(ranks[:5]):
            SLW[r] = op("scalar", lambda h, i=i, r=r: h.dma_start(
                out=gsl4[:, i, :],
                in_=cc_out.ap()[r * 128:(r + 1) * 128, :]),
                "slab%d" % (i % 4), 16)
        for i, r in enumerate(ranks[:5]):
            wt("vector", SLW[r])
            SGP[r] = eop("vector", lambda h, i=i, r=r: h.scalar_tensor_tensor(
                Sg[:], gsl4[:, i, :], pw_t[:, r:r + 1], Sg[:],
                ALU.mult, ALU.add))
        wt("scalar", SGP[ranks[0]])
        SLW[6] = op("scalar", lambda h: h.dma_start(
            out=gsl4[:, 0, :], in_=cc_out.ap()[6 * 128:7 * 128, :]),
            "slab0", 16)
        wt("vector", SLW[6])
        SGP[6] = eop("vector", lambda h: h.scalar_tensor_tensor(
            Sg[:], gsl4[:, 0, :], pw_t[:, 6:7], Sg[:],
            ALU.mult, ALU.add))
        SGP[NCORES - 1] = SGP[6]

    S65R = [None] * NT

    def am_off(hh):
        p, par = hh // 2, hh % 2
        return (p // 4) * 1024 + par * 512 + (p % 4) * 128

    PJQ, RMQ, PHIQ, TQQ, YMM, YEV, TQY, EXQ = \
        [None] * NT, [None] * NT, [None] * NT, [None] * NT, [None] * NT, \
        [None] * NT, [None] * NT, [None] * NT
    MKh = [[None, None] for _ in range(NT)]
    AMMh = [[None, None] for _ in range(NT)]
    MINQL = [None] * NT
    RQPL = [None] * NT
    PJQL = [None] * NT
    PJS = [None] * NT

    def b_qproj(c):
        if not (0 <= c < NT):
            return
        wt("tensor", XTB[c])
        if c == 0:
            wt("tensor", WQ)
        if c >= 1:
            wt("tensor", PJS[c - 1])
        last = None
        for cch in range(8):
            for hf in range(2):
                last = eop("tensor", lambda h, sl=c % 2, hf=hf, c2=cch:
                           h.matmul(
                    pjq[:, hf * 512:(hf + 1) * 512],
                    xtB[:, sl, c2, :], wqb[:, c2, hf * 512:(hf + 1) * 512],
                    start=(c2 == 0), stop=(c2 == 7),
                    skip_group_check=True))
        PJQL[c] = last
        wt("scalar", PJQL[c])
        if c >= 2:
            wt("scalar", RMQ[c - 2])
        PJS[c] = eop("scalar", lambda h, sl=c % 2: h.activation(
            pjs[:, sl, :], pjq[:], AF.Copy))

    def b_m(c):
        if not (0 <= c < NT):
            return
        wt("vector", PJS[c])
        if c >= 1:
            wt("vector", RQPL[c - 1])
        xx = pjs[:, c % 2, :].rearrange("p (g t w) -> p t g w", t=2, w=32)
        last = None
        for half in range(2):
            tab = cs_t[:, c, half * 32:half * 32 + 64].rearrange(
                "p (t o i) -> p t o i", t=2, o=1).broadcast_to(
                [128, 2, 16, 32])
            last = eop("vector", lambda h, half=half, tab=tab:
                       h.tensor_tensor(
                mB[:, 2 * half:2 * half + 2, :].rearrange(
                    "p t (g i) -> p t g i", i=32),
                xx, tab, ALU.mult))
        RMQ[c] = last

    def b_r(c):
        if not (0 <= c < NT):
            return
        wt("gpsimd", RMQ[c])
        if c >= 2:
            wt("gpsimd", PHIQ[c - 2])
        eop("gpsimd", lambda h, sl=c % 2: h.tensor_tensor(
            rq[:, sl, :].rearrange("p (g w) -> p g w", w=64)[:, :, 0:32],
            mB[:, 0, :].rearrange("p (g i) -> p g i", i=32),
            mB[:, 1, :].rearrange("p (g i) -> p g i", i=32), ALU.subtract))
        RQPL[c] = eop("gpsimd", lambda h, sl=c % 2: h.tensor_tensor(
            rq[:, sl, :].rearrange("p (g w) -> p g w", w=64)[:, :, 32:64],
            mB[:, 2, :].rearrange("p (g i) -> p g i", i=32),
            mB[:, 3, :].rearrange("p (g i) -> p g i", i=32), ALU.add))

    def b_exp(c):
        if not (0 <= c < NT):
            return
        wt("scalar", RQPL[c])
        eop("scalar", lambda h, sl=c % 2: h.activation(
            mnq[:, 0, :], rq[:, sl, :], AF.Relu, scale=cstc(C_NEG1)))
        if c >= 1:
            wt("scalar", PHIQ[c - 1])
        EXQ[c] = eop("scalar", lambda h: h.activation(
            eq[:, 0, :], mnq[:, 0, :], AF.Exp, scale=cstc(C_NEG1)))

    def b_stt(c):
        if not (0 <= c < NT):
            return
        wt("vector", EXQ[c])
        if c >= 2:
            wt("vector", TQQ[c - 2])
        PHIQ[c] = eop("vector", lambda h, sl=c % 2: h.scalar_tensor_tensor(
            qtm[:, sl, :], rq[:, sl, :], 0.0, eq[:, 0, :],
            ALU.max, ALU.add))

    def b_tqq(c):
        if not (0 <= c < NT):
            return
        wt("sync", PHIQ[c])
        if c >= 4:
            wt("sync", YMM[c - 4])
        TQQ[c] = op("sync", lambda h, sl=c % 4, ql=c % 2: h.dma_start_transpose(
            qfm[:, sl, :, :], qtm[:, ql, :]), "tqq%d" % (c % 2), 16)

    def b_amm(c, half):
        if not (0 <= c < NT):
            return
        if half == 0:
            wt("tensor", TQK[c])
            wt("tensor", TQQ[c])
            if c >= 1:
                wt("tensor", MKh[c - 1][1])
        else:
            wt("tensor", MKh[c][0])
        last = None
        for pp in range(4):
            p = half * 4 + pp
            for par in range(2):
                last = eop("tensor", lambda h, sl=c % 4, p=p, pp=pp, par=par,
                           c=c: h.matmul(
                    apsm[:, par * 512 + pp * 128:par * 512 + pp * 128 + 128],
                    phk_fm[par * 64:(par + 1) * 64, p,
                           c * 128:(c + 1) * 128],
                    qfm[par * 64:(par + 1) * 64, sl, p, :],
                    start=(pp == 0), stop=(pp == 3),
                    skip_group_check=True))
        AMMh[c][half] = last

    def b_mask(c, half):
        if not (0 <= c < NT):
            return
        wt("vector", AMMh[c][half])
        if c == 0 and half == 0:
            wt("vector", TRI)
        if c >= 4:
            wt("vector", YMM[c - 4])
        tribc = tri_b[:].rearrange("p (o i) -> p o i", o=1).broadcast_to(
            [128, 8, 128])
        MKh[c][half] = eop("vector", lambda h, sl=c % 4, half=half,
                           tribc=tribc: h.tensor_tensor(
            Am[:, sl, half * 1024:half * 1024 + 1024].rearrange(
                "p (g i) -> p g i", i=128),
            apsm[:].rearrange("p (g i) -> p g i", i=128),
            tribc, ALU.mult))

    def b_s65(c):
        if not (1 <= c < NT):
            return
        if c == 1:
            wt("gpsimd", SGP[NCORES - 1])
        if c >= 2:
            wt("gpsimd", YMM[c - 2])
        S65R[c] = eop("gpsimd", lambda h, c=c, d2=c % 2: h.tensor_tensor(
            S65[:, d2, :], Sg[:], D_bf[:, c - 1, :], ALU.add))

    VB = [None] * NT

    def b_vload(c):
        if not (0 <= c < NT):
            return
        if c >= 4:
            wt("sync", YMM[c - 4])
        VB[c] = op("sync", lambda h, c=c, sl=c % 4: h.dma_start(
            out=v65[:, sl, :], in_=v65_d.ap()[:, c * 1040:(c + 1) * 1040]),
            "vb%d" % (c % 2), 16)

    def b_y(c):
        if not (0 <= c < NT):
            return
        wt("tensor", MKh[c][1])
        wt("tensor", S65R[c])
        wt("tensor", VB[c])
        if c >= 1:
            wt("tensor", YEV[c - 1])
        last = None
        for p in range(NP):
            g, off = p // 3, (p % 3) * 130
            for par in range(2):
                hh = 2 * p + par
                st = (p % 3 == 0 and par == 0)
                eop("tensor", lambda h, sl=c % 4, hh=hh, g=g, off=off,
                    par=par, c=c, st=st: h.matmul(
                    yps[g][:, off + par * 65: off + par * 65 + 65],
                    Am[:, sl, am_off(hh):am_off(hh) + 128],
                    v65[:, c % 4, hh * 65:(hh + 1) * 65],
                    start=st, stop=False, skip_group_check=True))
                sp2 = ((p % 3 == 2 or p == NP - 1) and par == 1)
                last = eop("tensor", lambda h, sl=c % 4, d2=c % 2, p=p,
                           par=par, g=g,
                           off=off, sp2=sp2: h.matmul(
                    yps[g][:, off + par * 65: off + par * 65 + 65],
                    qfm[par * 64:(par + 1) * 64, sl, p, :],
                    S65[par * 64:(par + 1) * 64, d2, p * 65:(p + 1) * 65],
                    start=False, stop=sp2, skip_group_check=True))
        YMM[c] = last

    def b_yev(c):
        if not (0 <= c < NT):
            return
        wt("vector", YMM[c])
        if c >= 2:
            wt("vector", TQY[c - 2])
        for g in range(3):
            ng = 3 if g < 2 else 2
            src2 = yps[g][:, 0:130 * ng].rearrange(
                "p (a b) -> p a b", b=65)[:, :, 64:65]
            eop("vector", lambda h, g=g, ng=ng, src2=src2: h.tensor_copy(
                denf[:, 6 * g: 6 * g + 2 * ng],
                src2.rearrange("p a b -> p (a b)")))
        eop("vector", lambda h: h.tensor_single_scalar(
            denm[:], denf[:], 1e-6, ALU.max))
        eop("vector", lambda h: h.reciprocal(rden[:], denm[:]))
        last = None
        for g in range(3):
            ng = 3 if g < 2 else 2
            ysrc = yps[g][:, 0:130 * ng].rearrange(
                "p (a b) -> p a b", b=65)[:, :, 0:64]
            ydst = Ytm[:, c % 2, 128 * 3 * g: 128 * 3 * g + 128 * ng].rearrange(
                "p (a b) -> p a b", b=64)
            rsrc = rden[:, 6 * g:6 * g + 2 * ng].rearrange(
                "p (a o) -> p a o", o=1).broadcast_to([128, 2 * ng, 64])
            last = eop("vector", lambda h, ysrc=ysrc, ydst=ydst,
                       rsrc=rsrc: h.tensor_tensor(
                ydst, ysrc, rsrc, ALU.mult))
        YEV[c] = last

    def b_tqy(c):
        if not (0 <= c < NT):
            return
        wt("sync", YEV[c])
        TQY[c] = op("sync", lambda h, c=c, sl=c % 2: h.dma_start_transpose(
            YT[:, :, c * 128:(c + 1) * 128], Ytm[:, sl, :]),
            "tqy%d" % (c % 2), 16)

    b_xload(0)
    for j in range(NT + 5):
        b_xload(j + 1)
        b_stt(j - 3)
        b_tqq(j - 3)
        b_amm(j - 4, 0)
        b_qproj(j)
        b_mask(j - 4, 0)
        b_amm(j - 4, 1)
        b_m(j - 1)
        b_mask(j - 4, 1)
        b_vload(j - 3)
        if j == 5:
            b_slabs()
            wt("vector", SGP[NCORES - 1])
            S65R[0] = eop("vector", lambda h: h.tensor_copy(
                S65[:, 0, :], Sg[:]))
        b_y(j - 5)
        b_yev(j - 5)
        b_r(j - 1)
        b_s65(j - 4)
        b_exp(j - 2)
        b_tqy(j - 5)

    endB = marks_now()
    esB.close()
    esAB.close()

    esC = ExitStack()
    rhq = esC.enter_context(nc.sbuf_tensor("rhq", [128, 8, D], BF16))
    qrT = esC.enter_context(nc.sbuf_tensor("qrT", [128, 2, D], BF16))
    rhf = esC.enter_context(nc.sbuf_tensor("rhf", [128, 2, D], F32))
    rsq = esC.enter_context(nc.sbuf_tensor("rsq", [128, 8], F32))
    rrq = esC.enter_context(nc.sbuf_tensor("rrq", [128, 8], F32))
    rs_row = esC.enter_context(nc.sbuf_tensor("rs_row", [1, D], F32))
    rsbc = esC.enter_context(nc.sbuf_tensor("rsbc", [128, D], F32))
    qfr = esC.enter_context(nc.sbuf_tensor("qfr", [128, D], F32))
    qir = esC.enter_context(nc.sbuf_tensor("qir", [128, D], I32))
    xC = esC.enter_context(nc.sbuf_tensor("xC", [128, 3, D], F32))
    z1 = esC.enter_context(nc.sbuf_tensor("z1", [128, 4, D], F32))
    h1 = esC.enter_context(nc.sbuf_tensor("h1", [128, 4, D], F32))
    z2 = esC.enter_context(nc.sbuf_tensor("z2", [128, 3, D], F32))
    sqd = esC.enter_context(nc.sbuf_tensor("sqd", [128, D], F32))
    t1 = esC.enter_context(nc.sbuf_tensor("t1", [128, 2, D], F32))
    t2 = esC.enter_context(nc.sbuf_tensor("t2", [128, 2, D], F32))
    qf = esC.enter_context(nc.sbuf_tensor("qf", [128, D], F32))
    qi = esC.enter_context(nc.sbuf_tensor("qi", [128, D], I32))
    qltm = esC.enter_context(nc.sbuf_tensor("qltm", [128, 2, D], BF16))
    qlT = esC.enter_context(nc.sbuf_tensor("qlT", [128, 4, 8, 128], BF16))
    lsv = esC.enter_context(nc.sbuf_tensor("lsv", [128, 4], F32))
    outb = esC.enter_context(nc.sbuf_tensor("outb", [128, 2, D], F32))
    sv = esC.enter_context(nc.sbuf_tensor("sv", [128, 60], F32))
    ops2 = [esC.enter_context(nc.psum_tensor(f"ops2_{i}", [128, D], F32))
            for i in range(2)]
    ffs2 = [esC.enter_context(nc.psum_tensor(f"ffs{i}", [128, D], F32))
            for i in range(2)]

    (SV_MU, SV_MS, SV_VAR, SV_L, SV_RSTD, SV_AM, SV_LS, SV_RLS,
     SV_S1, SV_S2, SV_MU2, SV_MS2, SV_VAR2, SV_L2, SV_RSTD2,
     SV_S3, SV_S4, SV_LNB, SV_LNB2, SV_MUN) = range(20)

    def svc(i, n):
        j = i + 20 * (n % 3)
        return sv[:, j:j + 1]

    def lsc(n):
        return lsv[:, (n % 4):(n % 4) + 1]

    RHL, QRT = [None] * 8, [None] * 8
    TQRL = [None] * 8

    def c_rhs_load(oc):
        if not (0 <= oc < 8):
            return
        if oc >= 2:
            wt("sync", QRT[oc - 2])
        RHL[oc] = op("sync", lambda h, oc=oc: h.dma_start(
            out=rhf[:, oc % 2, :], in_=rhsT_d[oc * 128:(oc + 1) * 128, :]),
            "rhf%d" % (oc % 2), 16)

    def c_rhs_quant(oc):
        if not (0 <= oc < 8):
            return
        wt("vector", RHL[oc])
        eop("vector", lambda h, oc=oc: h.tensor_reduce(
            rsq[:, oc:oc + 1], rhf[:, oc % 2, :], AXX,
            ALU.max, apply_absolute_value=True))
        eop("vector", lambda h, oc=oc: h.tensor_scalar(
            rsq[:, oc:oc + 1], rsq[:, oc:oc + 1], 1e-12, 1.0 / 127.0,
            ALU.max, ALU.mult))
        eop("vector", lambda h, oc=oc: h.reciprocal(
            rrq[:, oc:oc + 1], rsq[:, oc:oc + 1]))
        eop("vector", lambda h, oc=oc: h.tensor_tensor(
            qfr[:], rhf[:, oc % 2, :],
            rrq[:, oc:oc + 1].broadcast_to([128, D]), ALU.mult))
        eop("vector", lambda h: h.tensor_scalar(
            qir[:], qfr[:], 127.0, -127.0, ALU.min, ALU.max))
        if oc >= 2:
            wt("vector", TQRL[oc - 2])
        QRT[oc] = eop("vector", lambda h, oc=oc: h.tensor_copy(
            qrT[:, oc % 2, :], qir[:]))
        wt("sync", QRT[oc])
        TQRL[oc] = op("sync", lambda h, oc=oc: h.dma_start_transpose(
            rhq[:, :, oc * 128:(oc + 1) * 128], qrT[:, oc % 2, :]), "tqr", 16)

    def c_rs_row():
        RS1 = op("sync", lambda h: h.dma_start(out=rs_sc.ap(), in_=rsq[:]),
                 "rs1", 16)
        wt("sync", RS1)
        return op("sync", lambda h: h.dma_start(
            out=rs_row[:].rearrange("o (c p) -> o c p", p=128),
            in_=rs_sc.ap().rearrange("p (o c) -> o c p", o=1),
            allow_slow_non_contiguous=True), "rs2", 16)

    OPM, Z1V, SQ1, LNV1, LNR1, H1S = ([None] * NT for _ in range(6))
    LNB1V, LNB2V, SC3, T2P = ([None] * NT for _ in range(4))
    AMV, QIV, QBB, TQF, FFM = ([None] * NT for _ in range(5))
    EP0, Z2V, SQ2, LNV2, LNR2, FIN, STO, XCW = ([None] * NT for _ in range(8))
    cst_state = {}

    def c_xload(n):
        if not (0 <= n < NT):
            return
        if n >= 3:
            wt("sync", Z1V[n - 3])
        XCW[n] = op("sync", lambda h, n=n, sl=n % 3: h.dma_start(
            out=xC[:, sl, :], in_=x_d[n * 128:(n + 1) * 128, :]),
            "xc%d" % (n % 3), 16)

    def c_opm(n):
        if not (0 <= n < NT):
            return
        if n == 0:
            wt("tensor", WO)
        if n >= 2:
            wt("tensor", Z1V[n - 2])
        last = None
        for e in range(8):
            for hf in range(2):
                last = eop("tensor", lambda h, n=n, hf=hf, e=e: h.matmul(
                    ops2[n % 2][:, hf * 512:(hf + 1) * 512],
                    YT[:, e, n * 128:(n + 1) * 128],
                    wo_b[:, e, hf * 512:(hf + 1) * 512],
                    start=(e == 0), stop=(e == 7),
                    skip_group_check=True))
        OPM[n] = last

    def c_ln1(n):
        if not (0 <= n < NT):
            return
        wt("vector", OPM[n])
        wt("vector", XCW[n])
        if n >= 4:
            wt("vector", H1S[n - 4])
        Z1V[n] = eop("vector", lambda h, n=n, sl=n % 4, xl=n % 3:
                     h.scalar_tensor_tensor(
            z1[:, sl, :], ops2[n % 2][:], 1.0, xC[:, xl, :],
            ALU.mult, ALU.add, accum_out=svc(SV_S1, n)))
        wt("scalar", Z1V[n])
        if n == 0:
            wt("scalar", CONST)
        SQ1[n] = eop("scalar", lambda h, n=n, sl=n % 4: h.activation(
            sqd[:], z1[:, sl, :], AF.Square, accum_out=svc(SV_S2, n)))

    def c_lnr(n):
        if not (0 <= n < NT):
            return
        eop("scalar", lambda h, n=n: h.activation(
            svc(SV_MU, n), svc(SV_S1, n), AF.Copy, scale=cstc(C_INV1024)))
        eop("scalar", lambda h, n=n: h.activation(
            svc(SV_MS, n), svc(SV_S2, n), AF.Identity,
            scale=cstc(C_INV1024), bias=cstc(C_LNEPS)))
        eop("scalar", lambda h, n=n: h.activation(
            svc(SV_VAR, n), svc(SV_MU, n), AF.Square))
        eop("scalar", lambda h, n=n: h.activation(
            svc(SV_L, n), svc(SV_VAR, n), AF.Ln, scale=cstc(C_NEG1),
            bias=svc(SV_MS, n)))
        LNR1[n] = eop("scalar", lambda h, n=n: h.activation(
            svc(SV_RSTD, n), svc(SV_L, n), AF.Exp, scale=cstc(C_NHALF)))

    def c_quant(n):
        if not (0 <= n < NT):
            return
        if n >= 4:
            wt("vector", Z2V[n - 4])
        H1S[n] = eop("vector", lambda h, n=n, sl=n % 4, zl=n % 4:
                     h.scalar_tensor_tensor(
            h1[:, sl, :], z1[:, zl, :], svc(SV_MU, n),
            rstd_bc(n), ALU.subtract, ALU.mult))
        eop("vector", lambda h, n=n, sl=n % 4: h.tensor_reduce(
            svc(SV_AM, n), h1[:, sl, :], AXX, ALU.max,
            apply_absolute_value=True))
        eop("vector", lambda h, n=n: h.tensor_scalar(
            lsc(n), svc(SV_AM, n), 1e-12, 1.0 / 127.0,
            ALU.max, ALU.mult))
        eop("vector", lambda h, n=n: h.reciprocal(
            svc(SV_RLS, n), lsc(n)))
        eop("vector", lambda h, n=n, sl=n % 4: h.tensor_scalar(
            qf[:], h1[:, sl, :], svc(SV_RLS, n), None, ALU.mult))
        QIV[n] = eop("vector", lambda h: h.tensor_scalar(
            qi[:], qf[:], 127.0, -127.0, ALU.min, ALU.max))
        if n >= 2:
            wt("vector", TQF[n - 2])
        QBB[n] = eop("vector", lambda h, sl=n % 2: h.tensor_copy(
            qltm[:, sl, :], qi[:]))
        wt("sync", QBB[n])
        if n >= 4:
            wt("sync", FFM[n - 4])
        TQF[n] = op("sync", lambda h, sl=n % 4, ql=n % 2: h.dma_start_transpose(
            qlT[:, sl, :, :], qltm[:, ql, :]), "tqf%d" % (n % 2), 16)

    def rstd_bc(n):
        return svc(SV_RSTD, n).broadcast_to([128, D])

    def rstd2_bc(n):
        return svc(SV_RSTD2, n).broadcast_to([128, D])

    def c_bc():
        wt("tensor", cst_state["RS2"])
        wt("tensor", CONST)
        last = None
        for hf in range(2):
            last = eop("tensor", lambda h, hf=hf: h.matmul(
                ffs2[1][:, hf * 512:(hf + 1) * 512], ones_f[:],
                rs_row[:, hf * 512:(hf + 1) * 512],
                start=True, stop=True, skip_group_check=True))
        wt("scalar", last)
        cst_state["RSBC"] = eop("scalar", lambda h: h.activation(
            rsbc[:], ffs2[1][:], AF.Copy))

    def c_ffn(n):
        if not (0 <= n < NT):
            return
        wt("tensor", TQF[n])
        if n == 0:
            wt("tensor", TQRL[7])
        if n == 1:
            wt("tensor", cst_state["RSBC"])
        if n >= 2:
            wt("tensor", EP0[n - 2])
        last = None
        for dch in range(8):
            for hf in range(2):
                last = eop("tensor", lambda h, sl=n % 4, fl=n % 2, hf=hf,
                           d2=dch: h.matmul(
                    ffs2[fl][:, hf * 512:(hf + 1) * 512],
                    qlT[:, sl, d2, :], rhq[:, d2, hf * 512:(hf + 1) * 512],
                    start=(d2 == 0), stop=(d2 == 7),
                    skip_group_check=True))
        FFM[n] = last

    def c_ln2a(n):
        if not (0 <= n < NT):
            return
        wt("scalar", FFM[n])
        if n >= 2:
            wt("scalar", T2P[n - 2])
        EP0[n] = eop("scalar", lambda h, n=n, fl=n % 2: h.activation(
            t1[:, n % 2, :], ffs2[fl][:], AF.Copy, scale=lsc(n)))
    def c_t2(n):
        if not (0 <= n < NT):
            return
        wt("vector", EP0[n])
        if n == 0:
            wt("vector", cst_state["RSBC"])
        T2P[n] = eop("vector", lambda h, n=n, sl=n % 2:
                     h.scalar_tensor_tensor(
            t2[:, sl, :], t1[:, sl, :], 1.0, rsbc[:], ALU.mult, ALU.mult,
            accum_out=svc(SV_S3, n)))

    def c_ln2z(n):
        if not (0 <= n < NT):
            return
        if n >= 3:
            wt("vector", FIN[n - 3])
        Z2V[n] = eop("vector", lambda h, n=n, sl=n % 3, hl=n % 4, tl=n % 2:
                     h.tensor_tensor(
            z2[:, sl, :], t2[:, tl, :], h1[:, hl, :], ALU.add))
        wt("scalar", Z2V[n])
        SQ2[n] = eop("scalar", lambda h, n=n, sl=n % 3: h.activation(
            sqd[:], z2[:, sl, :], AF.Square, accum_out=svc(SV_S4, n)))

    def c_ln2b(n):
        if not (0 <= n < NT):
            return
        eop("scalar", lambda h, n=n: h.activation(
            svc(SV_MU2, n), svc(SV_S3, n), AF.Copy, scale=cstc(C_INV1024)))
        eop("scalar", lambda h, n=n: h.activation(
            svc(SV_MS2, n), svc(SV_S4, n), AF.Identity,
            scale=cstc(C_INV1024), bias=cstc(C_LNEPS)))
        eop("scalar", lambda h, n=n: h.activation(
            svc(SV_VAR2, n), svc(SV_MU2, n), AF.Square))
        eop("scalar", lambda h, n=n: h.activation(
            svc(SV_L2, n), svc(SV_VAR2, n), AF.Ln, scale=cstc(C_NEG1),
            bias=svc(SV_MS2, n)))
        LNR2[n] = eop("scalar", lambda h, n=n: h.activation(
            svc(SV_RSTD2, n), svc(SV_L2, n), AF.Exp, scale=cstc(C_NHALF)))
        eop("scalar", lambda h, n=n: h.activation(
            svc(SV_MUN, n), svc(SV_S3, n), AF.Copy, scale=cstc(C_NINV1024)))
        LNB2V[n] = eop("scalar", lambda h, n=n: h.activation(
            svc(SV_LNB2, n), svc(SV_MUN, n), AF.Copy,
            scale=svc(SV_RSTD2, n)))

    def c_fin(n):
        if not (0 <= n < NT):
            return
        wt("vector", LNR2[n])
        if n >= 2:
            wt("vector", STO[n - 2])
        FIN[n] = eop("vector", lambda h, n=n, sl=n % 2, zl=n % 3:
                     h.scalar_tensor_tensor(
            outb[:, sl, :], z2[:, zl, :], svc(SV_MU2, n),
            rstd2_bc(n), ALU.subtract, ALU.mult))
        wt("sync", FIN[n])
        STO[n] = op("sync", lambda h, n=n, sl=n % 2: h.dma_start(
            out=out_d[n * 128:(n + 1) * 128, :], in_=outb[:, sl, :]),
            "sto%d" % (n % 2), 16)

    wt("sync", PHIQ[NT - 1])
    wt("sync", RQPL[NT - 1])
    wt("sync", EXQ[NT - 1])
    wt("vector", RQPL[NT - 1])
    wt("vector", EXQ[NT - 1])
    c_rhs_load(0)
    c_rhs_load(1)
    for oc in range(8):
        c_rhs_quant(oc)
        c_rhs_load(oc + 2)
    cst_state["RS2"] = c_rs_row()

    barrier(endB)

    c_xload(0)
    for j in range(NT + 8):
        c_xload(j + 1)
        c_ln1(j - 1)
        c_opm(j)
        if j == 1:
            c_bc()
        c_lnr(j - 2)
        c_quant(j - 3)
        c_ffn(j - 4)
        c_ln2a(j - 5)
        c_t2(j - 5)
        c_ln2z(j - 6)
        c_ln2b(j - 7)
        c_fin(j - 8)

    wt("sync", STO[NT - 2])
    wt("sync", STO[NT - 1])

    with nc.Block() as block:
        def runner(eng):
            def run(h):
                for item in streams[eng]:
                    if item[0] == "wait":
                        h.wait_ge(sem[item[1]], item[2])
                    else:
                        inst = item[1](h)
                        if item[2] is not None:
                            inst.then_inc(sem[item[2]], item[3])
            return run
        block.sync(runner("sync"))
        block.gpsimd(runner("gpsimd"))
        block.vector(runner("vector"))
        block.scalar(runner("scalar"))
        block.tensor(runner("tensor"))

    esC.close()
    esBC.close()
    es.close()
    return nc


_NC_CACHE = None


def kernel(L0_attn_x, L0_attn_Wq, L0_attn_Wk, L0_attn_Wv, L0_attn_Wo,
           L0_ffn_rhs):
    global _NC_CACHE
    x = np.ascontiguousarray(np.asarray(L0_attn_x, dtype=np.float32))
    wq = np.ascontiguousarray(np.asarray(L0_attn_Wq, dtype=np.float32).T)
    wk = np.ascontiguousarray(np.asarray(L0_attn_Wk, dtype=np.float32).T)
    wv = np.ascontiguousarray(np.asarray(L0_attn_Wv, dtype=np.float32).T)
    wo = np.ascontiguousarray(np.asarray(L0_attn_Wo, dtype=np.float32).T)
    rhsT = np.ascontiguousarray(np.asarray(L0_ffn_rhs, dtype=np.float32).T)

    inv = (10000.0 ** (-np.arange(HF, dtype=np.float64) / HF)).astype(np.float32)
    tri = np.ascontiguousarray(
        np.tril(np.ones((128, 128), dtype=np.float32)).T)
    cstv = np.zeros((128, 8), dtype=np.float32)
    cstv[:, 0] = 1.0 / 1024.0
    cstv[:, 1] = 1e-5
    cstv[:, 2] = -0.5
    cstv[:, 3] = -1.0
    cstv[:, 4] = -1.0 / 1024.0
    ones1 = np.ones((1, 128), dtype=np.float32)

    if _NC_CACHE is None:
        _NC_CACHE = build_nc()
    nc = _NC_CACHE

    in_maps = []
    for c in range(NCORES):
        b, qi2 = divmod(c, 4)
        t0 = qi2 * TC
        xs = np.ascontiguousarray(x[b, t0:t0 + TC, :])
        xTs = np.ascontiguousarray(xs.T)
        pos = (t0 + np.arange(TC)).astype(np.float32)
        ang = pos[:, None] * inv[None, :]
        cosv = np.cos(ang).astype(np.float32)
        sinv = np.sin(ang).astype(np.float32)
        cs = np.ascontiguousarray(np.concatenate(
            [cosv, sinv, cosv], axis=1))
        pw = np.zeros((128, NCORES), dtype=np.float32)
        for r in range(4 * b, 4 * b + qi2):
            pw[:, r] = 1.0
        in_maps.append({
            "x": xs, "xT": xTs, "wqT": wq, "wkT": wk, "wvT": wv, "woT": wo,
            "rhsT": rhsT, "cs": cs, "tri": tri, "pw": pw, "cstd": cstv,
            "ones1": ones1,
        })

    res = run_bass_kernel_spmd(nc, in_maps, core_ids=list(range(NCORES)))
    out = np.empty((2, 8192, 1024), dtype=np.float32)
    for c in range(NCORES):
        b, qi2 = divmod(c, 4)
        out[b, qi2 * TC:(qi2 + 1) * TC, :] = res.results[c]["out"]
    return out


# version 37
# speedup vs baseline: 1.0396x; 1.0035x over previous
import numpy as np
import concourse.bass as bass
import concourse.mybir as mybir
from concourse.bass_utils import run_bass_kernel_spmd
from contextlib import ExitStack

F32 = mybir.dt.float32
BF16 = mybir.dt.bfloat16
I32 = mybir.dt.int32
ALU = mybir.AluOpType
AF = mybir.ActivationFunctionType
AXX = mybir.AxisListType.X

NCORES = 8
TC = 2048
NT = 16
D = 1024
H = 16
HF = 32
NP = 8
SW = 65 * NP

ENGS = ("sync", "gpsimd", "vector", "scalar", "tensor")


def build_nc():
    nc = bass.Bass("TRN2", target_bir_lowering=False, debug=False)

    x_d = nc.dram_tensor("x", [TC, D], F32, kind="ExternalInput").ap()
    xT_d = nc.dram_tensor("xT", [D, TC], F32, kind="ExternalInput").ap()
    wq_d = nc.dram_tensor("wqT", [D, D], F32, kind="ExternalInput").ap()
    wk_d = nc.dram_tensor("wkT", [D, D], F32, kind="ExternalInput").ap()
    wv_d = nc.dram_tensor("wvT", [D, D], F32, kind="ExternalInput").ap()
    wo_d = nc.dram_tensor("woT", [D, D], F32, kind="ExternalInput").ap()
    rhsT_d = nc.dram_tensor("rhsT", [D, D], F32, kind="ExternalInput").ap()
    cs_d = nc.dram_tensor("cs", [TC, 3 * HF], F32, kind="ExternalInput").ap()
    tri_d = nc.dram_tensor("tri", [128, 128], F32, kind="ExternalInput").ap()
    pw_d = nc.dram_tensor("pw", [128, NCORES], F32, kind="ExternalInput").ap()
    cst_d = nc.dram_tensor("cstd", [128, 8], F32, kind="ExternalInput").ap()
    ones_d = nc.dram_tensor("ones1", [1, 128], F32, kind="ExternalInput").ap()
    out_d = nc.dram_tensor("out", [TC, D], F32, kind="ExternalOutput").ap()
    cc_in = nc.dram_tensor("cc_in", [128, SW], F32)
    cc_out = nc.dram_tensor("cc_out", [NCORES * 128, SW], F32)
    rs_sc = nc.dram_tensor("rs_sc", [128, 8], F32)
    v65_d = nc.dram_tensor("v65sp", [128, NT * 65 * H], BF16)

    streams = {e: [] for e in ENGS}
    cnt = {}
    es = ExitStack()
    sem = {}
    ENG_SEM = {"tensor": "pe", "vector": "vec", "scalar": "scl",
               "gpsimd": "pl", "sync": "syq"}

    def S(name):
        if name not in sem:
            sem[name] = es.enter_context(nc.semaphore("s_" + name))
            cnt[name] = 0
        return sem[name]

    def op(eng, fn, sm=None, inc=0):
        if sm is not None:
            S(sm)
            cnt[sm] += inc
            streams[eng].append(("op", fn, sm, inc))
            return (sm, cnt[sm])
        streams[eng].append(("op", fn, None, 0))
        return None

    def eop(eng, fn):
        own = ENG_SEM[eng]
        if eng != "tensor" and cnt.get(own, 0) > 0:
            streams[eng].append(("wait", own, cnt[own]))
        return op(eng, fn, own, 1)

    def wt(eng, spec):
        if spec is None:
            import sys
            print("WT-NONE-SKIP on", eng, file=sys.stderr)
            return
        sm, val = spec
        if val > 0:
            streams[eng].append(("wait", sm, val))

    def marks_now():
        return dict(cnt)

    def barrier(marks):
        for e in ENGS:
            own = ENG_SEM[e]
            for sm, val in marks.items():
                if sm != own and sm not in ("cc", "ccin") and val > 0:
                    streams[e].append(("wait", sm, val))

    Sg = es.enter_context(nc.sbuf_tensor("Sg", [128, SW], F32))
    tri_b = es.enter_context(nc.sbuf_tensor("tri_b", [128, 128], BF16))
    pw_t = es.enter_context(nc.sbuf_tensor("pw_t", [128, NCORES], F32))
    cst = es.enter_context(nc.sbuf_tensor("cst", [128, 8], F32))
    ones_f = es.enter_context(nc.sbuf_tensor("ones_f", [1, 128], F32))
    zb1 = es.enter_context(nc.sbuf_tensor("zb1", [1, 128], BF16))
    esAB = ExitStack()
    phk_fm = esAB.enter_context(nc.sbuf_tensor("phk_fm", [128, 8, TC], BF16))
    v65 = esAB.enter_context(nc.sbuf_tensor("v65", [128, 4, 65 * H], BF16))
    D_bf = esAB.enter_context(nc.sbuf_tensor("D_bf", [128, NT, SW], BF16))
    cs_t = esAB.enter_context(nc.sbuf_tensor("cs_t", [128, NT, 3 * HF], F32))
    wqb = esAB.enter_context(nc.sbuf_tensor("wqb", [128, 8, D], BF16))
    xtB = esAB.enter_context(nc.sbuf_tensor("xtB", [128, 2, 8, 128], BF16))

    C_INV1024, C_LNEPS, C_NHALF, C_NEG1, C_NINV1024 = 0, 1, 2, 3, 4

    def cstc(i):
        return cst[:, i:i + 1]

    TRI = op("gpsimd", lambda h: h.dma_start(out=tri_b[:], in_=tri_d),
             "tri", 16)
    op("sync", lambda h: h.dma_start(
        out=cs_t[:], in_=cs_d.rearrange("(n p) i -> p n i", p=128)),
        "const", 16)
    op("sync", lambda h: h.dma_start(out=pw_t[:], in_=pw_d), "const", 16)
    op("sync", lambda h: h.dma_start(out=cst[:], in_=cst_d), "const", 16)
    CONST = op("sync", lambda h: h.dma_start(out=ones_f[:], in_=ones_d),
               "const", 16)

    eop("vector", lambda h: h.memset(zb1[:], 0.0))
    eop("vector", lambda h: h.memset(zb5[:], 0.0))
    eop("vector", lambda h: h.memset(Sg[:], 0.0))
    V_INIT = eop("vector", lambda h: h.memset(
        v65[:].rearrange("p n (g w) -> p n g w", w=65)[:, :, :, 64:65], 1.0))

    esA = ExitStack()
    wkv = esA.enter_context(nc.sbuf_tensor("wkv", [128, 16, D], BF16))
    xtA = esA.enter_context(nc.sbuf_tensor("xtA", [128, 2, 8, 128], BF16))
    ktm = esA.enter_context(nc.sbuf_tensor("ktm", [128, 2, D], BF16))
    mA = esA.enter_context(nc.sbuf_tensor("mA", [128, 4, 512], F32))
    rk = esA.enter_context(nc.sbuf_tensor("rk", [128, 2, D], F32))
    mnk = esA.enter_context(nc.sbuf_tensor("mnk", [128, 1, D], F32))
    ek = esA.enter_context(nc.sbuf_tensor("ek", [128, 2, D], BF16))
    Stot = esA.enter_context(nc.sbuf_tensor("Stot", [128, SW], F32))
    zb5 = esA.enter_context(nc.sbuf_tensor("zb5", [1, 512], BF16))
    pjk = esA.enter_context(nc.psum_tensor("pjk", [128, D], F32))
    pjv = esA.enter_context(nc.psum_tensor("pjv", [128, D], F32))
    dps = [esA.enter_context(nc.psum_tensor(f"dps{i}", [128, 512], F32))
           for i in range(2)]

    WKA = op("gpsimd", lambda h: h.dma_start(
        out=wkv[:, 0:4, :],
        in_=wk_d[0:512, :].rearrange("(c p) o -> p c o", p=128)), "wka", 16)
    WK = op("gpsimd", lambda h: h.dma_start(
        out=wkv[:, 4:8, :],
        in_=wk_d[512:1024, :].rearrange("(c p) o -> p c o", p=128)),
        "wkb", 16)
    WVA = op("gpsimd", lambda h: h.dma_start(
        out=wkv[:, 8:12, :],
        in_=wv_d[0:512, :].rearrange("(c p) o -> p c o", p=128)), "wva", 16)
    WKV = op("gpsimd", lambda h: h.dma_start(
        out=wkv[:, 12:16, :],
        in_=wv_d[512:1024, :].rearrange("(c p) o -> p c o", p=128)),
        "wvb", 16)

    wt("tensor", V_INIT)
    for i in range(2):
        eop("tensor", lambda h, i=i: h.matmul(
            dps[i][:, 0:260], zb1[:], zb5[:, 0:260],
            start=True, stop=False, skip_group_check=True))

    PJ, VE, RMK, PHIK, DMM, DEV, TQK, XTA, EXK = \
        [None] * NT, [None] * NT, [None] * NT, [None] * NT, [None] * NT, \
        [None] * NT, [None] * NT, [None] * NT, [None] * NT
    MINKL = [None] * NT
    RKPL = [None] * NT
    PJKL = [None] * NT

    def a_xload(n):
        if not (0 <= n < NT):
            return
        if n >= 2:
            wt("gpsimd", PJ[n - 2])
        XTA[n] = op("gpsimd", lambda h, n=n, sl=n % 2: h.dma_start(
            out=xtA[:, sl, :, :],
            in_=xT_d[:, n * 128:(n + 1) * 128].rearrange(
                "(c p) t -> p c t", p=128)), "xta%d" % (n % 2), 16)

    def a_kvproj(n):
        if not (0 <= n < NT):
            return
        wt("tensor", XTA[n])
        if n == 0:
            wt("tensor", WKA)
            wt("tensor", WK)
        if n >= 1:
            wt("tensor", RMK[n - 1])
        last = None
        for cch in range(8):
            for hf in range(2):
                last = eop("tensor", lambda h, sl=n % 2, hf=hf, c=cch:
                           h.matmul(
                    pjk[:, hf * 512:(hf + 1) * 512],
                    xtA[:, sl, c, :], wkv[:, c, hf * 512:(hf + 1) * 512],
                    start=(c == 0), stop=(c == 7),
                    skip_group_check=True))
        PJKL[n] = last
        if n == 0:
            wt("tensor", WVA)
            wt("tensor", WKV)
        if n >= 1:
            wt("tensor", VE[n - 1])
        for cch in range(8):
            for hf in range(2):
                last = eop("tensor", lambda h, sl=n % 2, hf=hf, c=cch:
                           h.matmul(
                    pjv[:, hf * 512:(hf + 1) * 512],
                    xtA[:, sl, c, :], wkv[:, 8 + c, hf * 512:(hf + 1) * 512],
                    start=(c == 0), stop=(c == 7),
                    skip_group_check=True))
        PJ[n] = last

    def a_D(n):
        if not (0 <= n < NT):
            return
        wt("tensor", PHIK[n])
        wt("tensor", VE[n])
        if n >= 1:
            wt("tensor", DEV[n - 1])
        last = None
        for p in range(NP):
            bank, cb = p // 4, (p % 4) * 65
            for par in range(2):
                hh = 2 * p + par
                last = eop("tensor", lambda h, sl=n % 2, par=par, hh=hh,
                           bank=bank, cb=cb, n=n: h.matmul(
                    dps[bank][par * 64:(par + 1) * 64, cb:cb + 65],
                    ktm[:, sl, hh * 64:(hh + 1) * 64],
                    v65[:, n % 4, hh * 65:(hh + 1) * 65],
                    start=False,
                    stop=(n == NT - 1 and hh == 2 * NP - 1),
                    tile_position=(0, par * 64), skip_group_check=True))
        DMM[n] = last

    def a_m(n):
        if not (0 <= n < NT):
            return
        wt("vector", PJKL[n])
        if n == 0:
            wt("vector", CONST)
        if n >= 1:
            wt("vector", RKPL[n - 1])
        xx = pjk[:].rearrange("p (g t w) -> p t g w", t=2, w=32)
        last = None
        for half in range(2):
            tab = cs_t[:, n, half * 32:half * 32 + 64].rearrange(
                "p (t o i) -> p t o i", t=2, o=1).broadcast_to(
                [128, 2, 16, 32])
            last = eop("vector", lambda h, half=half, tab=tab:
                       h.tensor_tensor(
                mA[:, 2 * half:2 * half + 2, :].rearrange(
                    "p t (g i) -> p t g i", i=32),
                xx, tab, ALU.mult))
        RMK[n] = last

    def a_r(n):
        if not (0 <= n < NT):
            return
        wt("gpsimd", RMK[n])
        if n >= 2:
            wt("gpsimd", PHIK[n - 2])
        eop("gpsimd", lambda h, sl=n % 2: h.tensor_tensor(
            rk[:, sl, :].rearrange("p (g w) -> p g w", w=64)[:, :, 0:32],
            mA[:, 0, :].rearrange("p (g i) -> p g i", i=32),
            mA[:, 1, :].rearrange("p (g i) -> p g i", i=32), ALU.subtract))
        RKPL[n] = eop("gpsimd", lambda h, sl=n % 2: h.tensor_tensor(
            rk[:, sl, :].rearrange("p (g w) -> p g w", w=64)[:, :, 32:64],
            mA[:, 2, :].rearrange("p (g i) -> p g i", i=32),
            mA[:, 3, :].rearrange("p (g i) -> p g i", i=32), ALU.add))

    def a_exp(n):
        if not (0 <= n < NT):
            return
        wt("scalar", RKPL[n])
        if n == 0:
            wt("scalar", CONST)
        eop("scalar", lambda h, sl=n % 2: h.activation(
            mnk[:, 0, :], rk[:, sl, :], AF.Relu, scale=cstc(C_NEG1)))
        if n >= 2:
            wt("scalar", PHIK[n - 2])
        EXK[n] = eop("scalar", lambda h, sl=n % 2: h.activation(
            ek[:, sl, :], mnk[:, 0, :], AF.Exp, scale=cstc(C_NEG1)))

    def a_stt(n):
        if not (0 <= n < NT):
            return
        wt("vector", EXK[n])
        if n >= 2:
            wt("vector", TQK[n - 2])
        if n >= 2:
            wt("vector", DMM[n - 2])
        PHIK[n] = eop("vector", lambda h, sl=n % 2: h.scalar_tensor_tensor(
            ktm[:, sl, :], rk[:, sl, :], 0.0, ek[:, sl, :],
            ALU.max, ALU.add))

    SPV = [None] * NT

    def a_vevac(n):
        if not (0 <= n < NT):
            return
        wt("scalar", PJ[n])
        if n == 0:
            wt("scalar", V_INIT)
        if n >= 4:
            wt("scalar", SPV[n - 4])
            wt("scalar", DMM[n - 4])
        VE[n] = eop("scalar", lambda h, n=n, sl=n % 4: h.activation(
            v65[:, sl, :].rearrange("p (g w) -> p g w", w=65)[:, :, 0:64],
            pjv[:].rearrange("p (g w) -> p g w", w=64), AF.Copy))
        wt("scalar", VE[n])
        SPV[n] = op("scalar", lambda h, n=n, sl=n % 4: h.dma_start(
            out=v65_d.ap()[:, n * 1040:(n + 1) * 1040], in_=v65[:, sl, :]),
            "spv%d" % (n % 2), 16)

    def a_tq(n):
        if not (0 <= n < NT):
            return
        wt("sync", PHIK[n])
        TQK[n] = op("sync", lambda h, n=n, sl=n % 2: h.dma_start_transpose(
            phk_fm[:, :, n * 128:(n + 1) * 128], ktm[:, sl, :]),
            "tqk%d" % (n % 2), 16)

    def a_devac(n):
        if not (0 <= n < NT):
            return
        wt("vector", DMM[n])
        eop("vector", lambda h, n=n: h.tensor_copy(
            D_bf[:, n, 0:260], dps[0][:, 0:260]))
        DEV[n] = eop("vector", lambda h, n=n: h.tensor_copy(
            D_bf[:, n, 260:520], dps[1][:, 0:260]))

    XTB = [None] * NT
    WQ = None

    def b_xload(c):
        if not (0 <= c < NT):
            return
        if XTB[c] is not None:
            return
        if c >= 2:
            wt("gpsimd", PJQL[c - 2])
        XTB[c] = op("gpsimd", lambda h, c=c, sl=c % 2: h.dma_start(
            out=xtB[:, sl, :, :],
            in_=xT_d[:, c * 128:(c + 1) * 128].rearrange(
                "(c p) t -> p c t", p=128)), "xtb%d" % (c % 2), 16)

    a_xload(0)
    for j in range(NT + 2):
        a_xload(j + 1)
        a_kvproj(j)
        a_m(j)
        a_stt(j - 2)
        a_tq(j - 2)
        a_D(j - 2)
        a_devac(j - 2)
        a_r(j)
        a_vevac(j)
        a_exp(j - 1)
        if 8 <= j <= 15:
            cch = j - 8
            WQ = op("gpsimd", lambda h, c=cch: h.dma_start(
                out=wqb[:, c, :],
                in_=wq_d[c * 128:(c + 1) * 128, :]), "wq", 16)
        if j == 14:
            b_xload(0)
        if j == 15:
            b_xload(1)

    eop("vector", lambda h: h.tensor_copy(Stot[:, 0:260], dps[0][:, 0:260]))
    V_STOT = eop("vector", lambda h: h.tensor_copy(
        Stot[:, 260:520], dps[1][:, 0:260]))
    wt("gpsimd", V_STOT)
    CCW = op("gpsimd", lambda h: h.dma_start(out=cc_in.ap(), in_=Stot[:]),
             "ccin", 16)
    wt("gpsimd", CCW)
    CC = op("gpsimd", lambda h: h.collective_compute(
        "AllGather", ALU.bypass, replica_groups=[list(range(NCORES))],
        ins=[cc_in.ap().opt()], outs=[cc_out.ap().opt()]), "cc", 1)

    endA = marks_now()
    esA.close()
    esBC = ExitStack()
    YT = esBC.enter_context(nc.sbuf_tensor("YT", [128, 8, TC], BF16, side="right"))
    wo_b = esBC.enter_context(nc.sbuf_tensor("wo_b", [128, 8, D], BF16, side="right"))

    esB = ExitStack()
    Am = esB.enter_context(nc.sbuf_tensor("Am", [128, 4, 2048], BF16))
    qfm = esB.enter_context(nc.sbuf_tensor("qfm", [128, 4, 8, 128], BF16))
    Ytm = esB.enter_context(nc.sbuf_tensor("Ytm", [128, 2, D], BF16))
    S65 = esB.enter_context(nc.sbuf_tensor("S65", [128, 2, SW], BF16))
    denf = esB.enter_context(nc.sbuf_tensor("denf", [128, H], F32))
    rden = esB.enter_context(nc.sbuf_tensor("rden", [128, H], F32))
    denm = esB.enter_context(nc.sbuf_tensor("denm", [128, H], F32))
    gsl4 = esB.enter_context(nc.sbuf_tensor("gsl4", [128, 5, SW], F32))
    mB = esB.enter_context(nc.sbuf_tensor("mB", [128, 4, 512], F32))
    pjs = esB.enter_context(nc.sbuf_tensor("pjs", [128, 2, D], F32))
    rq = esB.enter_context(nc.sbuf_tensor("rq", [128, 2, D], F32))
    mnq = esB.enter_context(nc.sbuf_tensor("mnq", [128, 1, D], F32))
    eq = esB.enter_context(nc.sbuf_tensor("eq", [128, 1, D], BF16))
    qtm = esB.enter_context(nc.sbuf_tensor("qtm", [128, 2, D], BF16))
    pjq = esB.enter_context(nc.psum_tensor("pjq", [128, D], F32))
    apsm = esB.enter_context(nc.psum_tensor("apsm", [128, D], F32))
    yps = [esB.enter_context(nc.psum_tensor(f"yps{i}", [128, 512], F32))
           for i in range(3)]

    barrier(endA)

    WO = op("gpsimd", lambda h: h.dma_start(
        out=wo_b[:], in_=wo_d.rearrange("(c p) o -> p c o", p=128)), "wo", 16)

    SLW, SGP = [None] * NCORES, [None] * NCORES

    def b_slabs():
        ranks = [0, 1, 2, 4, 5, 6]
        wt("scalar", CC)
        for i, r in enumerate(ranks[:5]):
            SLW[r] = op("scalar", lambda h, i=i, r=r: h.dma_start(
                out=gsl4[:, i, :],
                in_=cc_out.ap()[r * 128:(r + 1) * 128, :]),
                "slab%d" % (i % 4), 16)
        for i, r in enumerate(ranks[:5]):
            wt("vector", SLW[r])
            SGP[r] = eop("vector", lambda h, i=i, r=r: h.scalar_tensor_tensor(
                Sg[:], gsl4[:, i, :], pw_t[:, r:r + 1], Sg[:],
                ALU.mult, ALU.add))
        wt("scalar", SGP[ranks[0]])
        SLW[6] = op("scalar", lambda h: h.dma_start(
            out=gsl4[:, 0, :], in_=cc_out.ap()[6 * 128:7 * 128, :]),
            "slab0", 16)
        wt("vector", SLW[6])
        SGP[6] = eop("vector", lambda h: h.scalar_tensor_tensor(
            Sg[:], gsl4[:, 0, :], pw_t[:, 6:7], Sg[:],
            ALU.mult, ALU.add))
        SGP[NCORES - 1] = SGP[6]

    S65R = [None] * NT

    def am_off(hh):
        p, par = hh // 2, hh % 2
        return (p // 4) * 1024 + par * 512 + (p % 4) * 128

    PJQ, RMQ, PHIQ, TQQ, YMM, YEV, TQY, EXQ = \
        [None] * NT, [None] * NT, [None] * NT, [None] * NT, [None] * NT, \
        [None] * NT, [None] * NT, [None] * NT
    MKh = [[None, None] for _ in range(NT)]
    AMMh = [[None, None] for _ in range(NT)]
    MINQL = [None] * NT
    RQPL = [None] * NT
    PJQL = [None] * NT
    PJS = [None] * NT

    def b_qproj(c):
        if not (0 <= c < NT):
            return
        wt("tensor", XTB[c])
        if c == 0:
            wt("tensor", WQ)
        if c >= 1:
            wt("tensor", PJS[c - 1])
        last = None
        for cch in range(8):
            for hf in range(2):
                last = eop("tensor", lambda h, sl=c % 2, hf=hf, c2=cch:
                           h.matmul(
                    pjq[:, hf * 512:(hf + 1) * 512],
                    xtB[:, sl, c2, :], wqb[:, c2, hf * 512:(hf + 1) * 512],
                    start=(c2 == 0), stop=(c2 == 7),
                    skip_group_check=True))
        PJQL[c] = last
        wt("scalar", PJQL[c])
        if c >= 2:
            wt("scalar", RMQ[c - 2])
        PJS[c] = eop("scalar", lambda h, sl=c % 2: h.activation(
            pjs[:, sl, :], pjq[:], AF.Copy))

    def b_m(c):
        if not (0 <= c < NT):
            return
        wt("vector", PJS[c])
        if c >= 1:
            wt("vector", RQPL[c - 1])
        xx = pjs[:, c % 2, :].rearrange("p (g t w) -> p t g w", t=2, w=32)
        last = None
        for half in range(2):
            tab = cs_t[:, c, half * 32:half * 32 + 64].rearrange(
                "p (t o i) -> p t o i", t=2, o=1).broadcast_to(
                [128, 2, 16, 32])
            last = eop("vector", lambda h, half=half, tab=tab:
                       h.tensor_tensor(
                mB[:, 2 * half:2 * half + 2, :].rearrange(
                    "p t (g i) -> p t g i", i=32),
                xx, tab, ALU.mult))
        RMQ[c] = last

    def b_r(c):
        if not (0 <= c < NT):
            return
        wt("gpsimd", RMQ[c])
        if c >= 2:
            wt("gpsimd", PHIQ[c - 2])
        eop("gpsimd", lambda h, sl=c % 2: h.tensor_tensor(
            rq[:, sl, :].rearrange("p (g w) -> p g w", w=64)[:, :, 0:32],
            mB[:, 0, :].rearrange("p (g i) -> p g i", i=32),
            mB[:, 1, :].rearrange("p (g i) -> p g i", i=32), ALU.subtract))
        RQPL[c] = eop("gpsimd", lambda h, sl=c % 2: h.tensor_tensor(
            rq[:, sl, :].rearrange("p (g w) -> p g w", w=64)[:, :, 32:64],
            mB[:, 2, :].rearrange("p (g i) -> p g i", i=32),
            mB[:, 3, :].rearrange("p (g i) -> p g i", i=32), ALU.add))

    def b_exp(c):
        if not (0 <= c < NT):
            return
        wt("scalar", RQPL[c])
        eop("scalar", lambda h, sl=c % 2: h.activation(
            mnq[:, 0, :], rq[:, sl, :], AF.Relu, scale=cstc(C_NEG1)))
        if c >= 1:
            wt("scalar", PHIQ[c - 1])
        EXQ[c] = eop("scalar", lambda h: h.activation(
            eq[:, 0, :], mnq[:, 0, :], AF.Exp, scale=cstc(C_NEG1)))

    def b_stt(c):
        if not (0 <= c < NT):
            return
        wt("vector", EXQ[c])
        if c >= 2:
            wt("vector", TQQ[c - 2])
        PHIQ[c] = eop("vector", lambda h, sl=c % 2: h.scalar_tensor_tensor(
            qtm[:, sl, :], rq[:, sl, :], 0.0, eq[:, 0, :],
            ALU.max, ALU.add))

    def b_tqq(c):
        if not (0 <= c < NT):
            return
        wt("sync", PHIQ[c])
        if c >= 4:
            wt("sync", YMM[c - 4])
        TQQ[c] = op("sync", lambda h, sl=c % 4, ql=c % 2: h.dma_start_transpose(
            qfm[:, sl, :, :], qtm[:, ql, :]), "tqq%d" % (c % 2), 16)

    def b_amm(c, half):
        if not (0 <= c < NT):
            return
        if half == 0:
            wt("tensor", TQK[c])
            wt("tensor", TQQ[c])
            if c >= 1:
                wt("tensor", MKh[c - 1][1])
        else:
            wt("tensor", MKh[c][0])
        last = None
        for pp in range(4):
            p = half * 4 + pp
            for par in range(2):
                last = eop("tensor", lambda h, sl=c % 4, p=p, pp=pp, par=par,
                           c=c: h.matmul(
                    apsm[:, par * 512 + pp * 128:par * 512 + pp * 128 + 128],
                    phk_fm[par * 64:(par + 1) * 64, p,
                           c * 128:(c + 1) * 128],
                    qfm[par * 64:(par + 1) * 64, sl, p, :],
                    start=(pp == 0), stop=(pp == 3),
                    skip_group_check=True))
        AMMh[c][half] = last

    def b_mask(c, half):
        if not (0 <= c < NT):
            return
        wt("vector", AMMh[c][half])
        if c == 0 and half == 0:
            wt("vector", TRI)
        if c >= 4:
            wt("vector", YMM[c - 4])
        tribc = tri_b[:].rearrange("p (o i) -> p o i", o=1).broadcast_to(
            [128, 8, 128])
        MKh[c][half] = eop("vector", lambda h, sl=c % 4, half=half,
                           tribc=tribc: h.tensor_tensor(
            Am[:, sl, half * 1024:half * 1024 + 1024].rearrange(
                "p (g i) -> p g i", i=128),
            apsm[:].rearrange("p (g i) -> p g i", i=128),
            tribc, ALU.mult))

    def b_s65(c):
        if not (1 <= c < NT):
            return
        if c == 1:
            wt("gpsimd", SGP[NCORES - 1])
        if c >= 2:
            wt("gpsimd", YMM[c - 2])
        S65R[c] = eop("gpsimd", lambda h, c=c, d2=c % 2: h.tensor_tensor(
            S65[:, d2, :], Sg[:], D_bf[:, c - 1, :], ALU.add))

    VB = [None] * NT

    def b_vload(c):
        if not (0 <= c < NT):
            return
        if c >= 4:
            wt("sync", YMM[c - 4])
        VB[c] = op("sync", lambda h, c=c, sl=c % 4: h.dma_start(
            out=v65[:, sl, :], in_=v65_d.ap()[:, c * 1040:(c + 1) * 1040]),
            "vb%d" % (c % 2), 16)

    def b_y(c):
        if not (0 <= c < NT):
            return
        wt("tensor", MKh[c][1])
        wt("tensor", S65R[c])
        wt("tensor", VB[c])
        if c >= 1:
            wt("tensor", YEV[c - 1])
        last = None
        for p in range(NP):
            g, off = p // 3, (p % 3) * 130
            for par in range(2):
                hh = 2 * p + par
                st = (p % 3 == 0 and par == 0)
                eop("tensor", lambda h, sl=c % 4, hh=hh, g=g, off=off,
                    par=par, c=c, st=st: h.matmul(
                    yps[g][:, off + par * 65: off + par * 65 + 65],
                    Am[:, sl, am_off(hh):am_off(hh) + 128],
                    v65[:, c % 4, hh * 65:(hh + 1) * 65],
                    start=st, stop=False, skip_group_check=True))
                sp2 = ((p % 3 == 2 or p == NP - 1) and par == 1)
                last = eop("tensor", lambda h, sl=c % 4, d2=c % 2, p=p,
                           par=par, g=g,
                           off=off, sp2=sp2: h.matmul(
                    yps[g][:, off + par * 65: off + par * 65 + 65],
                    qfm[par * 64:(par + 1) * 64, sl, p, :],
                    S65[par * 64:(par + 1) * 64, d2, p * 65:(p + 1) * 65],
                    start=False, stop=sp2, skip_group_check=True))
        YMM[c] = last

    def b_yev(c):
        if not (0 <= c < NT):
            return
        wt("vector", YMM[c])
        if c >= 2:
            wt("vector", TQY[c - 2])
        for g in range(3):
            ng = 3 if g < 2 else 2
            src2 = yps[g][:, 0:130 * ng].rearrange(
                "p (a b) -> p a b", b=65)[:, :, 64:65]
            eop("vector", lambda h, g=g, ng=ng, src2=src2: h.tensor_copy(
                denf[:, 6 * g: 6 * g + 2 * ng],
                src2.rearrange("p a b -> p (a b)")))
        eop("vector", lambda h: h.tensor_single_scalar(
            denm[:], denf[:], 1e-6, ALU.max))
        eop("vector", lambda h: h.reciprocal(rden[:], denm[:]))
        last = None
        for g in range(3):
            ng = 3 if g < 2 else 2
            ysrc = yps[g][:, 0:130 * ng].rearrange(
                "p (a b) -> p a b", b=65)[:, :, 0:64]
            ydst = Ytm[:, c % 2, 128 * 3 * g: 128 * 3 * g + 128 * ng].rearrange(
                "p (a b) -> p a b", b=64)
            rsrc = rden[:, 6 * g:6 * g + 2 * ng].rearrange(
                "p (a o) -> p a o", o=1).broadcast_to([128, 2 * ng, 64])
            last = eop("vector", lambda h, ysrc=ysrc, ydst=ydst,
                       rsrc=rsrc: h.tensor_tensor(
                ydst, ysrc, rsrc, ALU.mult))
        YEV[c] = last

    def b_tqy(c):
        if not (0 <= c < NT):
            return
        wt("sync", YEV[c])
        TQY[c] = op("sync", lambda h, c=c, sl=c % 2: h.dma_start_transpose(
            YT[:, :, c * 128:(c + 1) * 128], Ytm[:, sl, :]),
            "tqy%d" % (c % 2), 16)

    b_xload(0)
    for j in range(NT + 5):
        b_xload(j + 1)
        b_stt(j - 3)
        b_tqq(j - 3)
        b_amm(j - 4, 0)
        b_qproj(j)
        b_mask(j - 4, 0)
        b_amm(j - 4, 1)
        b_m(j - 1)
        b_mask(j - 4, 1)
        b_vload(j - 3)
        if j == 5:
            b_slabs()
            wt("vector", SGP[NCORES - 1])
            S65R[0] = eop("vector", lambda h: h.tensor_copy(
                S65[:, 0, :], Sg[:]))
        b_y(j - 5)
        b_yev(j - 5)
        b_r(j - 1)
        b_s65(j - 4)
        b_exp(j - 2)
        b_tqy(j - 5)

    endB = marks_now()
    esB.close()
    esAB.close()

    esC = ExitStack()
    rhq = esC.enter_context(nc.sbuf_tensor("rhq", [128, 8, D], BF16))
    qrT = esC.enter_context(nc.sbuf_tensor("qrT", [128, 2, D], BF16))
    rhf = esC.enter_context(nc.sbuf_tensor("rhf", [128, 2, D], F32))
    rsq = esC.enter_context(nc.sbuf_tensor("rsq", [128, 8], F32))
    rrq = esC.enter_context(nc.sbuf_tensor("rrq", [128, 8], F32))
    rs_row = esC.enter_context(nc.sbuf_tensor("rs_row", [1, D], F32))
    rsbc = esC.enter_context(nc.sbuf_tensor("rsbc", [128, D], F32))
    qfr = esC.enter_context(nc.sbuf_tensor("qfr", [128, D], F32))
    qir = esC.enter_context(nc.sbuf_tensor("qir", [128, D], I32))
    xC = esC.enter_context(nc.sbuf_tensor("xC", [128, 3, D], F32))
    z1 = esC.enter_context(nc.sbuf_tensor("z1", [128, 4, D], F32))
    h1 = esC.enter_context(nc.sbuf_tensor("h1", [128, 4, D], F32))
    z2 = esC.enter_context(nc.sbuf_tensor("z2", [128, 3, D], F32))
    sqd = esC.enter_context(nc.sbuf_tensor("sqd", [128, D], F32))
    t1 = esC.enter_context(nc.sbuf_tensor("t1", [128, 2, D], F32))
    t2 = esC.enter_context(nc.sbuf_tensor("t2", [128, D], F32))
    qf = esC.enter_context(nc.sbuf_tensor("qf", [128, D], F32))
    qi = esC.enter_context(nc.sbuf_tensor("qi", [128, D], I32))
    qltm = esC.enter_context(nc.sbuf_tensor("qltm", [128, 2, D], BF16))
    qlT = esC.enter_context(nc.sbuf_tensor("qlT", [128, 4, 8, 128], BF16))
    lsv = esC.enter_context(nc.sbuf_tensor("lsv", [128, 4], F32))
    outb = esC.enter_context(nc.sbuf_tensor("outb", [128, 2, D], F32))
    sv = esC.enter_context(nc.sbuf_tensor("sv", [128, 60], F32))
    ops2 = [esC.enter_context(nc.psum_tensor(f"ops2_{i}", [128, D], F32))
            for i in range(2)]
    ffs2 = [esC.enter_context(nc.psum_tensor(f"ffs{i}", [128, D], F32))
            for i in range(2)]

    (SV_MU, SV_MS, SV_VAR, SV_L, SV_RSTD, SV_AM, SV_LS, SV_RLS,
     SV_S1, SV_S2, SV_MU2, SV_MS2, SV_VAR2, SV_L2, SV_RSTD2,
     SV_S3, SV_S4, SV_LNB, SV_LNB2, SV_MUN) = range(20)

    def svc(i, n):
        j = i + 20 * (n % 3)
        return sv[:, j:j + 1]

    def lsc(n):
        return lsv[:, (n % 4):(n % 4) + 1]

    RHL, QRT = [None] * 8, [None] * 8
    TQRL = [None] * 8

    def c_rhs_load(oc):
        if not (0 <= oc < 8):
            return
        if oc >= 2:
            wt("sync", QRT[oc - 2])
        RHL[oc] = op("sync", lambda h, oc=oc: h.dma_start(
            out=rhf[:, oc % 2, :], in_=rhsT_d[oc * 128:(oc + 1) * 128, :]),
            "rhf%d" % (oc % 2), 16)

    def c_rhs_quant(oc):
        if not (0 <= oc < 8):
            return
        wt("vector", RHL[oc])
        eop("vector", lambda h, oc=oc: h.tensor_reduce(
            rsq[:, oc:oc + 1], rhf[:, oc % 2, :], AXX,
            ALU.max, apply_absolute_value=True))
        eop("vector", lambda h, oc=oc: h.tensor_scalar(
            rsq[:, oc:oc + 1], rsq[:, oc:oc + 1], 1e-12, 1.0 / 127.0,
            ALU.max, ALU.mult))
        eop("vector", lambda h, oc=oc: h.reciprocal(
            rrq[:, oc:oc + 1], rsq[:, oc:oc + 1]))
        eop("vector", lambda h, oc=oc: h.tensor_tensor(
            qfr[:], rhf[:, oc % 2, :],
            rrq[:, oc:oc + 1].broadcast_to([128, D]), ALU.mult))
        eop("vector", lambda h: h.tensor_scalar(
            qir[:], qfr[:], 127.0, -127.0, ALU.min, ALU.max))
        if oc >= 2:
            wt("vector", TQRL[oc - 2])
        QRT[oc] = eop("vector", lambda h, oc=oc: h.tensor_copy(
            qrT[:, oc % 2, :], qir[:]))
        wt("sync", QRT[oc])
        TQRL[oc] = op("sync", lambda h, oc=oc: h.dma_start_transpose(
            rhq[:, :, oc * 128:(oc + 1) * 128], qrT[:, oc % 2, :]), "tqr", 16)

    def c_rs_row():
        RS1 = op("sync", lambda h: h.dma_start(out=rs_sc.ap(), in_=rsq[:]),
                 "rs1", 16)
        wt("sync", RS1)
        return op("sync", lambda h: h.dma_start(
            out=rs_row[:].rearrange("o (c p) -> o c p", p=128),
            in_=rs_sc.ap().rearrange("p (o c) -> o c p", o=1),
            allow_slow_non_contiguous=True), "rs2", 16)

    OPM, Z1V, SQ1, LNV1, LNR1, H1S = ([None] * NT for _ in range(6))
    LNB1V, LNB2V, SC3, T2P = ([None] * NT for _ in range(4))
    AMV, QIV, QBB, TQF, FFM = ([None] * NT for _ in range(5))
    EP0, Z2V, SQ2, LNV2, LNR2, FIN, STO, XCW = ([None] * NT for _ in range(8))
    cst_state = {}

    def c_xload(n):
        if not (0 <= n < NT):
            return
        if n >= 3:
            wt("sync", Z1V[n - 3])
        XCW[n] = op("sync", lambda h, n=n, sl=n % 3: h.dma_start(
            out=xC[:, sl, :], in_=x_d[n * 128:(n + 1) * 128, :]),
            "xc%d" % (n % 3), 16)

    def c_opm(n):
        if not (0 <= n < NT):
            return
        if n == 0:
            wt("tensor", WO)
        if n >= 2:
            wt("tensor", Z1V[n - 2])
        last = None
        for e in range(8):
            for hf in range(2):
                last = eop("tensor", lambda h, n=n, hf=hf, e=e: h.matmul(
                    ops2[n % 2][:, hf * 512:(hf + 1) * 512],
                    YT[:, e, n * 128:(n + 1) * 128],
                    wo_b[:, e, hf * 512:(hf + 1) * 512],
                    start=(e == 0), stop=(e == 7),
                    skip_group_check=True))
        OPM[n] = last

    def c_ln1(n):
        if not (0 <= n < NT):
            return
        wt("vector", OPM[n])
        wt("vector", XCW[n])
        if n >= 4:
            wt("vector", H1S[n - 4])
        Z1V[n] = eop("vector", lambda h, n=n, sl=n % 4, xl=n % 3:
                     h.scalar_tensor_tensor(
            z1[:, sl, :], ops2[n % 2][:], 1.0, xC[:, xl, :],
            ALU.mult, ALU.add, accum_out=svc(SV_S1, n)))
        wt("scalar", Z1V[n])
        if n == 0:
            wt("scalar", CONST)
        SQ1[n] = eop("scalar", lambda h, n=n, sl=n % 4: h.activation(
            sqd[:], z1[:, sl, :], AF.Square, accum_out=svc(SV_S2, n)))

    def c_lnr(n):
        if not (0 <= n < NT):
            return
        eop("scalar", lambda h, n=n: h.activation(
            svc(SV_MU, n), svc(SV_S1, n), AF.Copy, scale=cstc(C_INV1024)))
        eop("scalar", lambda h, n=n: h.activation(
            svc(SV_MS, n), svc(SV_S2, n), AF.Identity,
            scale=cstc(C_INV1024), bias=cstc(C_LNEPS)))
        eop("scalar", lambda h, n=n: h.activation(
            svc(SV_VAR, n), svc(SV_MU, n), AF.Square))
        eop("scalar", lambda h, n=n: h.activation(
            svc(SV_L, n), svc(SV_VAR, n), AF.Ln, scale=cstc(C_NEG1),
            bias=svc(SV_MS, n)))
        LNR1[n] = eop("scalar", lambda h, n=n: h.activation(
            svc(SV_RSTD, n), svc(SV_L, n), AF.Exp, scale=cstc(C_NHALF)))

    def c_quant(n):
        if not (0 <= n < NT):
            return
        if n >= 4:
            wt("vector", Z2V[n - 4])
        H1S[n] = eop("vector", lambda h, n=n, sl=n % 4, zl=n % 4:
                     h.scalar_tensor_tensor(
            h1[:, sl, :], z1[:, zl, :], svc(SV_MU, n),
            rstd_bc(n), ALU.subtract, ALU.mult))
        eop("vector", lambda h, n=n, sl=n % 4: h.tensor_reduce(
            svc(SV_AM, n), h1[:, sl, :], AXX, ALU.max,
            apply_absolute_value=True))
        eop("vector", lambda h, n=n: h.tensor_scalar(
            lsc(n), svc(SV_AM, n), 1e-12, 1.0 / 127.0,
            ALU.max, ALU.mult))
        eop("vector", lambda h, n=n: h.reciprocal(
            svc(SV_RLS, n), lsc(n)))
        eop("vector", lambda h, n=n, sl=n % 4: h.tensor_scalar(
            qf[:], h1[:, sl, :], svc(SV_RLS, n), None, ALU.mult))
        QIV[n] = eop("vector", lambda h: h.tensor_scalar(
            qi[:], qf[:], 127.0, -127.0, ALU.min, ALU.max))
        if n >= 2:
            wt("vector", TQF[n - 2])
        QBB[n] = eop("vector", lambda h, sl=n % 2: h.tensor_copy(
            qltm[:, sl, :], qi[:]))
        wt("sync", QBB[n])
        if n >= 4:
            wt("sync", FFM[n - 4])
        TQF[n] = op("sync", lambda h, sl=n % 4, ql=n % 2: h.dma_start_transpose(
            qlT[:, sl, :, :], qltm[:, ql, :]), "tqf%d" % (n % 2), 16)

    def rstd_bc(n):
        return svc(SV_RSTD, n).broadcast_to([128, D])

    def rstd2_bc(n):
        return svc(SV_RSTD2, n).broadcast_to([128, D])

    def c_bc():
        wt("tensor", cst_state["RS2"])
        wt("tensor", CONST)
        last = None
        for hf in range(2):
            last = eop("tensor", lambda h, hf=hf: h.matmul(
                ffs2[1][:, hf * 512:(hf + 1) * 512], ones_f[:],
                rs_row[:, hf * 512:(hf + 1) * 512],
                start=True, stop=True, skip_group_check=True))
        wt("scalar", last)
        cst_state["RSBC"] = eop("scalar", lambda h: h.activation(
            rsbc[:], ffs2[1][:], AF.Copy))

    def c_ffn(n):
        if not (0 <= n < NT):
            return
        wt("tensor", TQF[n])
        if n == 0:
            wt("tensor", TQRL[7])
        if n == 1:
            wt("tensor", cst_state["RSBC"])
        if n >= 2:
            wt("tensor", EP0[n - 2])
        last = None
        for dch in range(8):
            for hf in range(2):
                last = eop("tensor", lambda h, sl=n % 4, fl=n % 2, hf=hf,
                           d2=dch: h.matmul(
                    ffs2[fl][:, hf * 512:(hf + 1) * 512],
                    qlT[:, sl, d2, :], rhq[:, d2, hf * 512:(hf + 1) * 512],
                    start=(d2 == 0), stop=(d2 == 7),
                    skip_group_check=True))
        FFM[n] = last

    def c_ln2a(n):
        if not (0 <= n < NT):
            return
        wt("scalar", FFM[n])
        if n >= 2:
            wt("scalar", T2P[n - 2])
        EP0[n] = eop("scalar", lambda h, n=n, fl=n % 2: h.activation(
            t1[:, n % 2, :], ffs2[fl][:], AF.Copy, scale=lsc(n)))
        wt("vector", EP0[n])
        if n == 0:
            wt("vector", cst_state["RSBC"])
        T2P[n] = eop("vector", lambda h, n=n, sl=n % 2:
                     h.scalar_tensor_tensor(
            t2[:], t1[:, sl, :], 1.0, rsbc[:], ALU.mult, ALU.mult,
            accum_out=svc(SV_S3, n)))
        if n >= 3:
            wt("vector", FIN[n - 3])
        Z2V[n] = eop("vector", lambda h, n=n, sl=n % 3, hl=n % 4:
                     h.tensor_tensor(
            z2[:, sl, :], t2[:], h1[:, hl, :], ALU.add))
        wt("scalar", Z2V[n])
        SQ2[n] = eop("scalar", lambda h, n=n, sl=n % 3: h.activation(
            sqd[:], z2[:, sl, :], AF.Square, accum_out=svc(SV_S4, n)))

    def c_ln2b(n):
        if not (0 <= n < NT):
            return
        eop("scalar", lambda h, n=n: h.activation(
            svc(SV_MU2, n), svc(SV_S3, n), AF.Copy, scale=cstc(C_INV1024)))
        eop("scalar", lambda h, n=n: h.activation(
            svc(SV_MS2, n), svc(SV_S4, n), AF.Identity,
            scale=cstc(C_INV1024), bias=cstc(C_LNEPS)))
        eop("scalar", lambda h, n=n: h.activation(
            svc(SV_VAR2, n), svc(SV_MU2, n), AF.Square))
        eop("scalar", lambda h, n=n: h.activation(
            svc(SV_L2, n), svc(SV_VAR2, n), AF.Ln, scale=cstc(C_NEG1),
            bias=svc(SV_MS2, n)))
        LNR2[n] = eop("scalar", lambda h, n=n: h.activation(
            svc(SV_RSTD2, n), svc(SV_L2, n), AF.Exp, scale=cstc(C_NHALF)))
        eop("scalar", lambda h, n=n: h.activation(
            svc(SV_MUN, n), svc(SV_S3, n), AF.Copy, scale=cstc(C_NINV1024)))
        LNB2V[n] = eop("scalar", lambda h, n=n: h.activation(
            svc(SV_LNB2, n), svc(SV_MUN, n), AF.Copy,
            scale=svc(SV_RSTD2, n)))

    def c_fin(n):
        if not (0 <= n < NT):
            return
        wt("vector", LNR2[n])
        if n >= 2:
            wt("vector", STO[n - 2])
        FIN[n] = eop("vector", lambda h, n=n, sl=n % 2, zl=n % 3:
                     h.scalar_tensor_tensor(
            outb[:, sl, :], z2[:, zl, :], svc(SV_MU2, n),
            rstd2_bc(n), ALU.subtract, ALU.mult))
        wt("sync", FIN[n])
        STO[n] = op("sync", lambda h, n=n, sl=n % 2: h.dma_start(
            out=out_d[n * 128:(n + 1) * 128, :], in_=outb[:, sl, :]),
            "sto%d" % (n % 2), 16)

    wt("sync", PHIQ[NT - 1])
    wt("sync", RQPL[NT - 1])
    wt("sync", EXQ[NT - 1])
    wt("vector", RQPL[NT - 1])
    wt("vector", EXQ[NT - 1])
    c_rhs_load(0)
    c_rhs_load(1)
    for oc in range(8):
        c_rhs_quant(oc)
        c_rhs_load(oc + 2)
    cst_state["RS2"] = c_rs_row()

    barrier(endB)

    c_xload(0)
    for j in range(NT + 7):
        c_xload(j + 1)
        c_ln1(j - 1)
        c_opm(j)
        if j == 1:
            c_bc()
        c_lnr(j - 2)
        c_quant(j - 3)
        c_ffn(j - 4)
        c_ln2a(j - 5)
        c_ln2b(j - 6)
        c_fin(j - 7)

    wt("sync", STO[NT - 2])
    wt("sync", STO[NT - 1])

    with nc.Block() as block:
        def runner(eng):
            def run(h):
                for item in streams[eng]:
                    if item[0] == "wait":
                        h.wait_ge(sem[item[1]], item[2])
                    else:
                        inst = item[1](h)
                        if item[2] is not None:
                            inst.then_inc(sem[item[2]], item[3])
            return run
        block.sync(runner("sync"))
        block.gpsimd(runner("gpsimd"))
        block.vector(runner("vector"))
        block.scalar(runner("scalar"))
        block.tensor(runner("tensor"))

    esC.close()
    esBC.close()
    es.close()
    return nc


_NC_CACHE = None


def kernel(L0_attn_x, L0_attn_Wq, L0_attn_Wk, L0_attn_Wv, L0_attn_Wo,
           L0_ffn_rhs):
    global _NC_CACHE
    x = np.ascontiguousarray(np.asarray(L0_attn_x, dtype=np.float32))
    wq = np.ascontiguousarray(np.asarray(L0_attn_Wq, dtype=np.float32).T)
    wk = np.ascontiguousarray(np.asarray(L0_attn_Wk, dtype=np.float32).T)
    wv = np.ascontiguousarray(np.asarray(L0_attn_Wv, dtype=np.float32).T)
    wo = np.ascontiguousarray(np.asarray(L0_attn_Wo, dtype=np.float32).T)
    rhsT = np.ascontiguousarray(np.asarray(L0_ffn_rhs, dtype=np.float32).T)

    inv = (10000.0 ** (-np.arange(HF, dtype=np.float64) / HF)).astype(np.float32)
    tri = np.ascontiguousarray(
        np.tril(np.ones((128, 128), dtype=np.float32)).T)
    cstv = np.zeros((128, 8), dtype=np.float32)
    cstv[:, 0] = 1.0 / 1024.0
    cstv[:, 1] = 1e-5
    cstv[:, 2] = -0.5
    cstv[:, 3] = -1.0
    cstv[:, 4] = -1.0 / 1024.0
    ones1 = np.ones((1, 128), dtype=np.float32)

    if _NC_CACHE is None:
        _NC_CACHE = build_nc()
    nc = _NC_CACHE

    in_maps = []
    for c in range(NCORES):
        b, qi2 = divmod(c, 4)
        t0 = qi2 * TC
        xs = np.ascontiguousarray(x[b, t0:t0 + TC, :])
        xTs = np.ascontiguousarray(xs.T)
        pos = (t0 + np.arange(TC)).astype(np.float32)
        ang = pos[:, None] * inv[None, :]
        cosv = np.cos(ang).astype(np.float32)
        sinv = np.sin(ang).astype(np.float32)
        cs = np.ascontiguousarray(np.concatenate(
            [cosv, sinv, cosv], axis=1))
        pw = np.zeros((128, NCORES), dtype=np.float32)
        for r in range(4 * b, 4 * b + qi2):
            pw[:, r] = 1.0
        in_maps.append({
            "x": xs, "xT": xTs, "wqT": wq, "wkT": wk, "wvT": wv, "woT": wo,
            "rhsT": rhsT, "cs": cs, "tri": tri, "pw": pw, "cstd": cstv,
            "ones1": ones1,
        })

    res = run_bass_kernel_spmd(nc, in_maps, core_ids=list(range(NCORES)))
    out = np.empty((2, 8192, 1024), dtype=np.float32)
    for c in range(NCORES):
        b, qi2 = divmod(c, 4)
        out[b, qi2 * TC:(qi2 + 1) * TC, :] = res.results[c]["out"]
    return out
